# version 02
# baseline (speedup 1.0000x reference)
import contextlib
import numpy as np
import ml_dtypes
import concourse.bass as bass
import concourse.mybir as mybir
from concourse.bass_utils import run_bass_kernel_spmd

F32 = mybir.dt.float32
BF16 = mybir.dt.bfloat16
U8 = mybir.dt.uint8
BF = ml_dtypes.bfloat16
AL = mybir.AluOpType
AF = mybir.ActivationFunctionType
AX = mybir.AxisListType.X

NCORE = 8
TC = 1024
L = 256
NCH = TC // L
D, DI, DFF, S = 2048, 4096, 8192, 16
ND, NDI, NFF = D // 128, DI // 128, DFF // 128
MAGIC = float(1.5 * 2 ** 23)
EPS = 1e-5

OFF_A = 0
OFF_B = 33024
OFF_C = 98560
OFF_D = 164096
ARENA = 200960

ENGS = ("sync", "gpsimd", "vector", "scalar", "tensor")
DEBUG_TAPS = False


class Ctr:
    def __init__(self, sem):
        self.sem = sem
        self.v = 0


class Sched:
    def __init__(self):
        self.q = {e: [] for e in ENGS}

    PARANOID = True

    def emit(self, eng, fn, waits=(), inc=None, inc_n=1, selfwait=False):
        if self.PARANOID and inc is not None and eng in ("vector", "scalar", "gpsimd"):
            selfwait = True
        if inc is not None and inc_n == 16:
            selfwait = True
        wl = [(c.sem, v) for (c, v) in waits]
        if inc is not None:
            inc.v += inc_n
        tgt = (inc.sem, inc_n, inc.v) if inc is not None else None

        def thunk(e, fn=fn, wl=wl, tgt=tgt, selfwait=selfwait):
            for sem_, v in wl:
                e.wait_ge(sem_, v)
            inst = fn(e)
            if tgt is not None:
                inst.then_inc(tgt[0], tgt[1])
                if selfwait:
                    e.wait_ge(tgt[0], tgt[2])

        self.q[eng].append(thunk)


def build_program():
    nc = bass.Bass("TRN2", target_bir_lowering=False)

    def din(name, shape, dtype=F32):
        return nc.dram_tensor(name, shape, dtype, kind="ExternalInput").ap()

    xm = [din("xm0", [TC, D]), din("xm1", [TC, D])]
    xh = [din("xh0", [3, D]), din("xh1", [3, D])]
    w_inT = din("w_inT", [D, DFF])
    w_xT = din("w_xT", [128, NDI, 80])
    out_wT = din("out_wT", [DI, D])
    w_gT = din("w_gT", [D, DFF])
    w_uT = din("w_uT", [D, DFF])
    w_dT = din("w_dT", [DFF, D])
    conv_w = din("conv_w", [128, NDI, 4])
    conv_b = din("conv_b", [128, NDI])
    d_vec = din("d_vec", [128, NDI])
    dt_bias = din("dt_bias", [128, NDI])
    a_row = din("a_row", [1, S])
    wsl = din("wsl", [128, 3, 16384])
    idn = din("idn", [128, 128], BF16)
    idnf = din("idnf", [128, 128], F32)
    gmask = din("gmask", [128, 2, L])
    chm = din("chm", [128, NCORE])
    yout = nc.dram_tensor("y", [2, TC, D], F32, kind="ExternalOutput").ap()

    gin_d = nc.dram_tensor("gin_d", [128, 513], F32).ap()
    gout_d = nc.dram_tensor("gout_d", [128 * NCORE, 513], F32).ap()
    ws_in_d = nc.dram_tensor("ws_in_d", [128, 4], F32).ap()
    ws_out_d = nc.dram_tensor("ws_out_d", [128, 4], F32).ap()
    row_d = nc.dram_tensor("row_d", [4, TC], F32).ap()
    col_d = nc.dram_tensor("col_d", [16, 8], F32).ap()
    x1_d = nc.dram_tensor("x1_d", [8, 128, D], F32).ap()
    scr_g = nc.dram_tensor("scr_g", [D, DFF], BF16).ap()
    scr_u = nc.dram_tensor("scr_u", [D, DFF], BF16).ap()
    scr_d = nc.dram_tensor("scr_d", [DFF, D], BF16).ap()

    taps = {}
    if DEBUG_TAPS:
        taps = {
            "t_xf": nc.dram_tensor("t_xf", [128, ND, 3 + TC], BF16, kind="ExternalOutput").ap(),
            "t_xcv": nc.dram_tensor("t_xcv", [128, NDI, TC], BF16, kind="ExternalOutput").ap(),
            "t_pB": nc.dram_tensor("t_pB", [16, TC], F32, kind="ExternalOutput").ap(),
            "t_pC": nc.dram_tensor("t_pC", [16, TC], F32, kind="ExternalOutput").ap(),
            "t_dt": nc.dram_tensor("t_dt", [1, TC], F32, kind="ExternalOutput").ap(),
            "t_ug": nc.dram_tensor("t_ug", [1, TC], F32, kind="ExternalOutput").ap(),
            "t_cf": nc.dram_tensor("t_cf", [16, NCH, L], BF16, kind="ExternalOutput").ap(),
            "t_bdtr": nc.dram_tensor("t_bdtr", [16, NCH, L], BF16, kind="ExternalOutput").ap(),
            "t_cbar": nc.dram_tensor("t_cbar", [16, NCH, L], BF16, kind="ExternalOutput").ap(),
            "t_hpack": nc.dram_tensor("t_hpack", [128, NCH, 512], F32, kind="ExternalOutput").ap(),
            "t_hloc": nc.dram_tensor("t_hloc", [128, 512], F32, kind="ExternalOutput").ap(),
            "t_hrun": nc.dram_tensor("t_hrun", [128, 512], F32, kind="ExternalOutput").ap(),
            "t_ymix": nc.dram_tensor("t_ymix", [128, NDI, TC], BF16, kind="ExternalOutput").ap(),
            "t_g": nc.dram_tensor("t_g", [128, NDI, TC], BF16, kind="ExternalOutput").ap(),
            "t_attn": nc.dram_tensor("t_attn", [128, ND, TC], F32, kind="ExternalOutput").ap(),
            "t_xq": nc.dram_tensor("t_xq", [128, ND, TC], BF16, kind="ExternalOutput").ap(),
            "t_g2": nc.dram_tensor("t_g2", [128, NFF, TC], BF16, kind="ExternalOutput").ap(),
            "t_g2r": nc.dram_tensor("t_g2r", [128, NFF, TC], BF16, kind="ExternalOutput").ap(),
            "t_rws": nc.dram_tensor("t_rws", [128, 3], F32, kind="ExternalOutput").ap(),
            "t_r128": nc.dram_tensor("t_r128", [128, TC], F32, kind="ExternalOutput").ap(),
            "t_wq": nc.dram_tensor("t_wq", [128, 2, 16, 128], BF16, kind="ExternalOutput").ap(),
        }

    sch = Sched()

    with contextlib.ExitStack() as top:
        sem = {k: Ctr(top.enter_context(nc.semaphore(k)))
               for k in ["dX", "dW", "dH", "tC", "vC", "sC", "cc", "bar"]}
        dX, dW, dH, tC, vC, sC, cc, bar = (sem[k] for k in
                                           ["dX", "dW", "dH", "tC", "vC", "sC", "cc", "bar"])

        pb = [top.enter_context(nc.psum_tensor(f"pb{i}", [128, 512], F32)) for i in range(8)]

        def pbf(i, n=512):
            return pb[i][:, 0:n]

        def pbb(i, n=1024):
            return pb[i][:, :].bitcast(BF16)[:, 0:n]

        def barrier():
            bar.v += 5
            tgt = bar.v
            dxw, dww, dhw, ccw = dX.v, dW.v, dH.v, cc.v
            for eng in ENGS:
                def thunk(e, dxw=dxw, dww=dww, dhw=dhw, ccw=ccw, tgt=tgt):
                    e.wait_ge(dX.sem, dxw)
                    e.wait_ge(dW.sem, dww)
                    e.wait_ge(dH.sem, dhw)
                    e.wait_ge(cc.sem, ccw)
                    e.nop().then_inc(bar.sem, 1)
                    e.wait_ge(bar.sem, tgt)
                sch.q[eng].append(thunk)

        arena = top.enter_context(nc.sbuf_tensor("arena", [128, ARENA], U8))

        def V(off, shape, dtype):
            nb = int(np.prod(shape[1:])) * mybir.dt.size(dtype)
            ap = arena[0:shape[0], off:off + nb].bitcast(dtype)
            if len(shape) == 2:
                return ap
            if len(shape) == 3:
                return ap.rearrange("p (a b) -> p a b", a=shape[1], b=shape[2])
            if len(shape) == 4:
                return ap.rearrange("p (a b c) -> p a b c", a=shape[1], b=shape[2], c=shape[3])
            raise ValueError(shape)

        def sb(name, shape, dtype):
            return top.enter_context(nc.sbuf_tensor(name, shape, dtype))

        idn_s = sb("idn_s", [128, 128], BF16)
        idnf_s = sb("idnf_s", [128, 128], F32)
        gm_s = sb("gm_s", [128, 2, L], F32)
        chm_s = sb("chm_s", [128, NCORE], F32)
        cw_s = sb("cw_s", [128, NDI, 4], F32)
        cb_s = sb("cb_s", [128, NDI], F32)
        dv_s = sb("dv_s", [128, NDI], F32)
        wx_s = sb("wx_s", [128, NDI, 80], BF16)
        a16 = sb("a16", [16, 4], F32)
        a128 = sb("a128", [128, 1], F32)
        mb_s = sb("mb_s", [1, 8], F32)
        ones_s = sb("ones_s", [128, 8], F32)
        rws_s = sb("rws_s", [128, 4], F32)
        wsb_s = sb("wsb_s", [128, 4], F32)
        wsum_s = sb("wsum_s", [128, 4], F32)
        sumc = sb("sumc", [128, 24], F32)
        rst = sb("rst", [8, 128], F32)
        zcol = sb("zcol", [128, 1], F32)
        e16b = sb("e16b", [16, L], BF16)
        btT = sb("btT", [128, NCH, 2, 16], BF16)
        gsb = sb("gsb", [128, 2, L], BF16)
        ec16 = sb("ec16", [16, NCH], F32)
        ec128 = sb("ec128", [128, NCH], F32)
        sc16 = sb("sc16", [16, 8], F32)
        ujb = sb("ujb", [128, 1], F32)
        ej_s = sb("ej_s", [128, 1], F32)
        amx = sb("amx", [128, 16], F32)
        scol8 = sb("scol8", [128, 8], F32)
        arow_s = sb("arow_s", [1, S], F32)
        cst = sb("cst", [128, 4], F32)

        xf = V(OFF_A, [128, ND, 3 + TC], BF16)
        xcv = V(OFF_B, [128, NDI, TC], BF16)
        wbA = V(OFF_C, [128, 2, 16, 256], BF16)
        xss = V(OFF_C + 16384, [128, 2, 1056], F32)
        cvt = V(OFF_C + 24832, [128, 2, TC], F32)
        xcT = V(OFF_C, [128, 2, DI], BF16)
        hpack = V(OFF_C + 16384, [128, NCH, 512], F32)
        ug16 = V(OFF_C + 24576, [16, TC], F32)
        dt16 = V(OFF_C + 28672, [16, TC], F32)
        bdt = V(OFF_C + 32768, [16, NCH, L], F32)
        pB_s = V(OFF_D + 18432, [16, TC], F32)
        pC_s = V(OFF_D + 22528, [16, TC], F32)
        cf_s = V(OFF_C + 40960, [16, NCH, L], BF16)
        bdtr = V(OFF_C + 43008, [16, NCH, L], BF16)
        cbar = V(OFF_C + 45056, [16, NCH, L], BF16)
        h0b = V(OFF_C + 47104, [16, DI], BF16)
        hloc = V(OFF_C + 55296, [128, 512], F32)
        hrun = V(OFF_C + 57344, [128, 512], F32)
        htmp = V(OFF_C + 59392, [128, 512], F32)
        gsl = V(OFF_C + 61440, [128, 512], F32)
        xtokb = V(OFF_D, [128, 8, D], BF16)
        xhs = V(OFF_D + 32768, [3, D], BF16)
        dt_s = V(OFF_D, [1, TC], F32)
        ug_s = V(OFF_D + 4096, [1, TC], F32)
        onerow = V(OFF_D + 8192, [1, TC], F32)
        e16a = V(OFF_D + 12288, [16, L], F32)
        e16f = V(OFF_D + 13312, [16, L], F32)
        hst = V(OFF_D + 14336, [16, 512], F32)
        zsil = V(OFF_D + 16384, [128, 2, 512], BF16)
        wst = V(OFF_B, [128, 2, 4096], F32)
        wxf = V(OFF_B + 32768, [128, NDI, 80], F32)
        dtb = V(OFF_B + 43008, [128, NDI], F32)
        xq_f = V(OFF_A, [128, ND, TC], BF16)
        attn_f = V(OFF_C, [128, ND, TC], F32)
        wbO = V(OFF_D, [128, 2, NDI, 128], BF16)
        stage2 = V(OFF_D + 16384, [128, D], F32)
        x1t = V(OFF_D + 24576, [128, D], F32)
        xqt = V(OFF_D + 32768, [128, D], BF16)
        g2 = V(OFF_B, [128, NFF, TC], BF16)
        wbG = V(OFF_D, [128, 2, 16, 128], BF16)
        wbU = V(OFF_D + 8192, [128, 2, 16, 128], BF16)
        wfst = V(OFF_D + 16384, [128, 4, 128], F32)
        qtmp = V(OFF_D + 18432, [128, 512], F32)
        sil2 = V(OFF_D + 24576, [128, 2, 512], BF16)
        mxp = V(OFF_D + 26624, [128, TC], F32)
        sg128 = V(OFF_D + 30720, [128, TC], F32)
        r128 = V(OFF_D + 30720, [128, TC], F32)
        su128 = V(OFF_D + 20480, [128, TC], F32)
        dnf = V(OFF_A, [128, ND, TC], BF16)
        wbD = V(OFF_D, [128, 2, 32, 128], BF16)
        wfst2 = V(OFF_D + 16384, [128, 8, 128], F32)
        qtmp2 = V(OFF_D + 20480, [128, TC], F32)
        s2b = V(OFF_D + 24576, [128, TC], F32)
        stage3 = V(OFF_B, [128, D], F32)
        x1rr = V(OFF_B + 8192, [128, D], F32)

        for dst, src in [(idn_s[:, :], idn[:, :]), (idnf_s[:, :], idnf[:, :]),
                         (gm_s[:, :, :], gmask[:, :, :]), (chm_s[:, :], chm[:, :]),
                         (arow_s[:, :], a_row[:, :]),
                         (cw_s[:, :, :], conv_w[:, :, :]),
                         (cb_s[:, :], conv_b[:, :]),
                         (dv_s[:, :], d_vec[:, :]),
                         (dtb[:, :], dt_bias[:, :]),
                         (wxf[:, :, :], w_xT[:, :, :])]:
            sch.emit("sync", lambda e, dst=dst, src=src: e.dma_start(out=dst, in_=src),
                     inc=dX, inc_n=16)
        pre_dX = dX.v

        sch.emit("vector", lambda e: e.memset(ones_s[:, :], 1.0), inc=vC, selfwait=True)
        sch.emit("vector", lambda e: e.memset(cst[:, 0:1], EPS), inc=vC, selfwait=True)
        sch.emit("vector", lambda e: e.memset(cst[:, 1:2], 1.0), inc=vC, selfwait=True)
        sch.emit("vector", lambda e: e.memset(cst[:, 2:3], MAGIC), inc=vC, selfwait=True)
        sch.emit("vector", lambda e: e.memset(zcol[:, :], 0.0), inc=vC, selfwait=True)
        sch.emit("vector", lambda e: e.tensor_copy(wx_s[:, :, :], wxf[:, :, :]),
                 waits=[(dX, pre_dX)], inc=vC)
        sch.emit("vector", lambda e: e.tensor_reduce(wsum_s[:, 3:4], dtb[:, :], AX, AL.add),
                 waits=[(dX, pre_dX)], inc=vC)
        v0 = vC.v
        sch.emit("tensor", lambda e: e.matmul(pb[6][0:1, 0:1], ones_s[:, 0:1],
                                              wsum_s[:, 3:4], start=True, stop=True),
                 waits=[(vC, v0)], inc=tC)
        t0 = tC.v
        sch.emit("vector", lambda e: e.tensor_scalar(mb_s[0:1, 0:1], pb[6][0:1, 0:1],
                                                     1.0 / DI, None, AL.mult),
                 waits=[(tC, t0)], inc=vC)
        sch.emit("scalar", lambda e: e.activation(arow_s[0:1, :], arow_s[0:1, :], AF.Exp),
                 waits=[(dX, pre_dX)], inc=sC)
        s0 = sC.v
        sch.emit("tensor", lambda e: e.transpose(pb[7][0:16, 0:1], arow_s[0:1, 0:16],
                                                 idnf_s[0:1, 0:1]),
                 waits=[(sC, s0)], inc=tC)
        t1 = tC.v
        sch.emit("vector", lambda e: e.tensor_scalar(a16[0:16, 0:1], pb[7][0:16, 0:1],
                                                     -1.0, None, AL.mult),
                 waits=[(tC, t1)], inc=vC)
        sch.emit("vector", lambda e: e.tensor_copy(a16[0:16, 1:2], pb[7][0:16, 0:1]), inc=vC)
        v1 = vC.v
        sch.emit("sync", lambda e: e.dma_start(out=col_d[:, 0:1], in_=a16[0:16, 0:1], allow_slow_non_contiguous=True),
                 waits=[(vC, v1)], inc=dX, inc_n=16)
        d0 = dX.v
        for g in range(8):
            sch.emit("sync", lambda e, g=g: e.dma_start(out=a128[16 * g:16 * (g + 1), 0:1],
                                                        in_=col_d[:, 0:1], allow_slow_non_contiguous=True),
                     waits=[(dX, d0)], inc=dX, inc_n=16)
        sch.emit("vector", lambda e: e.memset(wsum_s[:, 0:3], 0.0), inc=vC, selfwait=True)
        for w3 in range(3):
            for ch in range(4):
                p2 = ch % 2
                vprev = vC.v
                sch.emit("sync", lambda e, w3=w3, ch=ch, p2=p2: e.dma_start(
                    out=wst[:, p2, :], in_=wsl[:, w3, 4096 * ch:4096 * (ch + 1)]),
                    waits=[(vC, max(0, vprev - 1))], inc=dX, inc_n=16)
                dm = dX.v
                sch.emit("vector", lambda e, p2=p2: e.tensor_reduce(
                    wsum_s[:, 3:4], wst[:, p2, :], AX, AL.add, apply_absolute_value=True),
                    waits=[(dX, dm)], inc=vC)
                sch.emit("vector", lambda e, w3=w3: e.tensor_tensor(
                    wsum_s[:, w3:w3 + 1], wsum_s[:, w3:w3 + 1], wsum_s[:, 3:4], AL.add),
                    inc=vC)
        v2 = vC.v
        sch.emit("sync", lambda e: e.dma_start(out=ws_in_d[:, 0:4], in_=wsum_s[:, 0:4]),
                 waits=[(vC, v2)], inc=dX, inc_n=16)
        d1 = dX.v
        sch.emit("gpsimd", lambda e: e.collective_compute(
            "AllReduce", AL.add, replica_groups=[list(range(NCORE))],
            ins=[ws_in_d[:, :]], outs=[ws_out_d[:, :]]),
            waits=[(dX, d1)], inc=cc)
        cc0 = cc.v
        sch.emit("sync", lambda e: e.dma_start(out=wsum_s[:, 0:4], in_=ws_out_d[:, :]),
                 waits=[(cc, cc0)], inc=dX, inc_n=16)
        d2 = dX.v
        sch.emit("tensor", lambda e: e.matmul(pb[6][0:1, 0:3], ones_s[:, 0:1],
                                              wsum_s[:, 0:3], start=True, stop=True),
                 waits=[(dX, d2)], inc=tC)
        t2 = tC.v
        sch.emit("vector", lambda e: e.tensor_scalar(wsum_s[0:1, 0:3], pb[6][0:1, 0:3],
                                                     1.0 / 16777216.0, EPS, AL.mult, AL.add),
                 waits=[(tC, t2)], inc=vC)
        v3 = vC.v
        sch.emit("sync", lambda e: e.dma_start(out=ws_out_d[0:1, 0:3], in_=wsum_s[0:1, 0:3]),
                 waits=[(vC, v3)], inc=dX, inc_n=16)
        d3 = dX.v
        sch.emit("sync", lambda e: e.dma_start(out=rws_s[:, 0:3],
                                               in_=ws_out_d[0:1, 0:3].broadcast_to([128, 3])),
                 waits=[(dX, d3)], inc=dX, inc_n=16)
        d4 = dX.v
        sch.emit("vector", lambda e: e.tensor_copy(wsb_s[:, 0:3], rws_s[:, 0:3]),
                 waits=[(dX, d4)], inc=vC)
        sch.emit("vector", lambda e: e.reciprocal(rws_s[:, 0:3], rws_s[:, 0:3]), inc=vC)
        barrier()

        def tap(name, src, waits):
            if name in taps:
                sch.emit("sync", lambda e, name=name, src=src: e.dma_start(
                    out=taps[name][...], in_=src), waits=waits, inc=dX, inc_n=16)
                barrier()

        if "t_rws" in taps:
            sch.emit("sync", lambda e: e.dma_start(out=taps["t_rws"][...],
                                                   in_=rws_s[:, 0:3]),
                     waits=[(vC, vC.v)], inc=dX, inc_n=16)

        grp_t = {}
        for r in range(2):
            sch.emit("gpsimd", lambda e, r=r: e.dma_start(
                out=xtokb[:, :, :], in_=xm[r].rearrange("(n p) d -> p n d", p=128)),
                inc=dW, inc_n=16)
            sch.emit("gpsimd", lambda e, r=r: e.dma_start(out=xhs[:, :], in_=xh[r][:, :]),
                     inc=dW, inc_n=16)
            d_x0 = dW.v
            nblk = 0
            for dh in range(ND):
                for tt in range(9):
                    bank = 6 + (nblk % 2)
                    vprev = vC.v
                    if tt < 8:
                        sch.emit("tensor", lambda e, dh=dh, tt=tt, bank=bank: e.transpose(
                            pbb(bank, 128), xtokb[:, tt, dh * 128:(dh + 1) * 128],
                            idn_s[:, :]),
                            waits=[(dW, d_x0), (vC, max(0, vprev - 1))], inc=tC)
                        tm = tC.v
                        sch.emit("vector", lambda e, dh=dh, tt=tt, bank=bank: e.tensor_copy(
                            xf[:, dh, 3 + tt * 128: 3 + (tt + 1) * 128], pbb(bank, 128)),
                            waits=[(tC, tm)], inc=vC)
                    else:
                        sch.emit("tensor", lambda e, dh=dh, bank=bank: e.transpose(
                            pbb(bank, 3), xhs[0:3, dh * 128:(dh + 1) * 128],
                            idn_s[0:3, 0:3]),
                            waits=[(dW, d_x0), (vC, max(0, vprev - 1))], inc=tC)
                        tm = tC.v
                        sch.emit("vector", lambda e, dh=dh, bank=bank: e.tensor_copy(
                            xf[:, dh, 0:3], pbb(bank, 3)), waits=[(tC, tm)], inc=vC)
                    nblk += 1
            v_xf = vC.v
            if r == 0:
                tap("t_xf", xf[:, :, :], [(vC, v_xf)])

            evm = {}
            smark = {}
            for g in range(16):
                sch.emit("gpsimd", lambda e, g=g: e.dma_start(
                    out=wbA[:, g % 2, :, :],
                    in_=w_inT[:, 256 * g:256 * (g + 1)].rearrange("(kh p) n -> p kh n", p=128)),
                    waits=[(tC, grp_t.get(("a", r, g - 2), 0))], inc=dW, inc_n=16)
                d_g = dW.v
                for nl in range(2):
                    n_hat = g * 2 + nl
                    pA, pB, pC = (0, 1, 4) if n_hat % 2 == 0 else (2, 3, 5)
                    for kh in range(16):
                        w0 = [(dW, d_g),
                              (vC, max(evm.get(n_hat - 2, 0), v_xf))] if kh == 0 else []
                        sch.emit("tensor", lambda e, g=g, nl=nl, kh=kh, pA=pA: e.matmul(
                            pbf(pA), wbA[:, g % 2, kh, nl * 128:(nl + 1) * 128],
                            xf[:, kh, 0:512], start=(kh == 0), stop=(kh == 15),
                            skip_group_check=True), waits=w0)
                        sch.emit("tensor", lambda e, g=g, nl=nl, kh=kh, pB=pB: e.matmul(
                            pbf(pB), wbA[:, g % 2, kh, nl * 128:(nl + 1) * 128],
                            xf[:, kh, 512:1024], start=(kh == 0), stop=(kh == 15),
                            skip_group_check=True))
                        sch.emit("tensor", lambda e, g=g, nl=nl, kh=kh, pC=pC: e.matmul(
                            pbf(pC, 3), wbA[:, g % 2, kh, nl * 128:(nl + 1) * 128],
                            xf[:, kh, 1024:1027], start=(kh == 0), stop=(kh == 15),
                            skip_group_check=True),
                            inc=(tC if kh == 15 else None))
                    grp_t[("a", r, g)] = tC.v
                    t_mm = tC.v
                    sp = n_hat % 2
                    sch.emit("vector", lambda e, sp=sp, pA=pA: e.tensor_copy(
                        xss[:, sp, 0:512], pbf(pA)), waits=[(tC, t_mm)], inc=vC)
                    sch.emit("vector", lambda e, sp=sp, pB=pB: e.tensor_copy(
                        xss[:, sp, 512:1024], pbf(pB)), inc=vC)
                    sch.emit("vector", lambda e, sp=sp, pC=pC: e.tensor_copy(
                        xss[:, sp, 1024:1027], pbf(pC, 3)), inc=vC)
                    evm[n_hat] = vC.v
                    sch.emit("vector", lambda e, sp=sp, n_hat=n_hat: e.tensor_scalar(
                        cvt[:, sp, :], xss[:, sp, 0:TC], cw_s[:, n_hat, 0:1],
                        cb_s[:, n_hat:n_hat + 1], AL.mult, AL.add),
                        waits=[(sC, smark.get(n_hat - 2, 0))], inc=vC)
                    for k in range(1, 4):
                        sch.emit("vector", lambda e, sp=sp, n_hat=n_hat, k=k:
                                 e.scalar_tensor_tensor(
                                     cvt[:, sp, :], xss[:, sp, k:TC + k],
                                     cw_s[:, n_hat, k:k + 1], cvt[:, sp, :],
                                     AL.mult, AL.add), inc=vC)
                    v_cv = vC.v
                    sch.emit("scalar", lambda e, sp=sp, n_hat=n_hat: e.activation(
                        xcv[:, n_hat, :], cvt[:, sp, :], AF.Silu),
                        waits=[(vC, v_cv)], inc=sC)
                    smark[n_hat] = sC.v
            s_xcv = sC.v
            if r == 0:
                tap("t_xcv", xcv[:, :, :], [(sC, s_xcv)])
            barrier()

            sch.emit("vector", lambda e: e.memset(onerow[:, :], 1.0), inc=vC, selfwait=True)
            for nh in range(NDI):
                sch.emit("tensor", lambda e, nh=nh: e.matmul(
                    pb[6][0:80, :], wx_s[:, nh, :], xcv[:, nh, 0:512],
                    start=(nh == 0), stop=(nh == NDI - 1), skip_group_check=True))
                sch.emit("tensor", lambda e, nh=nh: e.matmul(
                    pb[7][0:80, :], wx_s[:, nh, :], xcv[:, nh, 512:1024],
                    start=(nh == 0), stop=(nh == NDI - 1), skip_group_check=True),
                    inc=(tC if nh == NDI - 1 else None))
            t_p = tC.v
            sch.emit("vector", lambda e: e.tensor_copy(pB_s[:, 0:512], pb[6][32:48, :]),
                     waits=[(tC, t_p)], inc=vC)
            sch.emit("vector", lambda e: e.tensor_copy(pB_s[:, 512:1024], pb[7][32:48, :]),
                     inc=vC)
            sch.emit("vector", lambda e: e.tensor_copy(pC_s[:, 0:512], pb[6][64:80, :]),
                     inc=vC)
            sch.emit("vector", lambda e: e.tensor_copy(pC_s[:, 512:1024], pb[7][64:80, :]),
                     inc=vC)
            v_p = vC.v
            sch.emit("scalar", lambda e: e.activation(dt_s[0:1, 0:512], pb[6][0:1, :],
                                                      AF.Exp, bias=mb_s[0:1, 0:1]),
                     waits=[(vC, v_p)], inc=sC)
            sch.emit("scalar", lambda e: e.activation(dt_s[0:1, 512:1024], pb[7][0:1, :],
                                                      AF.Exp, bias=mb_s[0:1, 0:1]),
                     inc=sC)
            sch.emit("scalar", lambda e: e.activation(dt_s[0:1, :], dt_s[0:1, :], AF.Ln,
                                                      bias=cst[0:1, 1:2]), inc=sC)
            s_dt = sC.v
            sch.emit("vector", lambda e: e.tensor_tensor_scan(
                ug_s[0:1, :], onerow[0:1, :], dt_s[0:1, :], 0.0, AL.mult, AL.add),
                waits=[(sC, s_dt)], inc=vC)
            v_ug = vC.v
            sch.emit("sync", lambda e: e.dma_start(out=row_d[0:1, :], in_=ug_s[0:1, :]),
                     waits=[(vC, v_ug)], inc=dX, inc_n=16)
            sch.emit("sync", lambda e: e.dma_start(out=row_d[1:2, :], in_=dt_s[0:1, :]),
                     inc=dX, inc_n=16)
            d_row = dX.v
            sch.emit("sync", lambda e: e.dma_start(
                out=ug16[:, :], in_=row_d[0:1, :].broadcast_to([16, TC])),
                waits=[(dX, d_row)], inc=dX, inc_n=16)
            sch.emit("sync", lambda e: e.dma_start(
                out=dt16[:, :], in_=row_d[1:2, :].broadcast_to([16, TC])),
                inc=dX, inc_n=16)
            d_b16 = dX.v

            for c in range(NCH):
                lo, hi = c * L, (c + 1) * L
                if c == 0:
                    sch.emit("vector", lambda e: e.memset(sc16[0:16, 0:1], 0.0),
                             waits=[(dX, d_b16)], inc=vC, selfwait=True)
                else:
                    sch.emit("vector", lambda e, lo=lo: e.tensor_copy(
                        sc16[0:16, 0:1], ug16[:, lo - 1:lo]),
                        waits=[(dX, d_b16)], inc=vC)
                sch.emit("vector", lambda e, hi=hi: e.tensor_copy(
                    sc16[0:16, 1:2], ug16[:, hi - 1:hi]), inc=vC)
                sch.emit("vector", lambda e: e.tensor_tensor(
                    sc16[0:16, 2:3], sc16[0:16, 0:1], sc16[0:16, 1:2], AL.add), inc=vC)
                sch.emit("vector", lambda e: e.tensor_scalar(
                    sc16[0:16, 2:3], sc16[0:16, 2:3], 0.5, None, AL.mult), inc=vC)
                sch.emit("vector", lambda e, c=c, lo=lo, hi=hi: e.tensor_tensor(
                    bdt[:, c, :], pB_s[:, lo:hi], dt16[:, lo:hi], AL.mult), inc=vC)
                sch.emit("vector", lambda e, lo=lo, hi=hi: e.tensor_scalar(
                    e16a[:, :], ug16[:, lo:hi], sc16[0:16, 2:3], 5.3,
                    AL.subtract, AL.min), inc=vC)
                sch.emit("vector", lambda e: e.tensor_scalar(
                    e16a[:, :], e16a[:, :], -5.3, None, AL.max), inc=vC)
                v_td = vC.v
                sch.emit("scalar", lambda e: e.activation(e16f[:, :], e16a[:, :], AF.Exp,
                                                          scale=a16[0:16, 0:1]),
                         waits=[(vC, v_td)], inc=sC)
                s_F = sC.v
                sch.emit("vector", lambda e, c=c, lo=lo, hi=hi: e.tensor_tensor(
                    cf_s[:, c, :], pC_s[:, lo:hi], e16f[:, :], AL.mult),
                    waits=[(sC, s_F)], inc=vC)
                v_cf = vC.v
                sch.emit("scalar", lambda e: e.activation(e16f[:, :], e16a[:, :], AF.Exp,
                                                          scale=a16[0:16, 1:2]),
                         waits=[(vC, v_cf)], inc=sC)
                s_R = sC.v
                sch.emit("vector", lambda e, c=c: e.tensor_tensor(
                    bdtr[:, c, :], bdt[:, c, :], e16f[:, :], AL.mult),
                    waits=[(sC, s_R)], inc=vC)
                sch.emit("vector", lambda e, lo=lo, hi=hi: e.tensor_scalar(
                    e16a[:, :], ug16[:, lo:hi], sc16[0:16, 0:1], None, AL.subtract),
                    inc=vC)
                v_eb = vC.v
                sch.emit("scalar", lambda e: e.activation(e16f[:, :], e16a[:, :], AF.Exp,
                                                          scale=a16[0:16, 0:1]),
                         waits=[(vC, v_eb)], inc=sC)
                s_Eb = sC.v
                sch.emit("vector", lambda e, c=c, lo=lo, hi=hi: e.tensor_tensor(
                    cbar[:, c, :], pC_s[:, lo:hi], e16f[:, :], AL.mult),
                    waits=[(sC, s_Eb)], inc=vC)
                sch.emit("vector", lambda e, lo=lo, hi=hi: e.tensor_scalar(
                    e16a[:, :], ug16[:, lo:hi], sc16[0:16, 1:2], None, AL.subtract),
                    inc=vC)
                v_e2 = vC.v
                sch.emit("scalar", lambda e: e.activation(e16f[:, :], e16a[:, :], AF.Exp,
                                                          scale=a16[0:16, 1:2]),
                         waits=[(vC, v_e2)], inc=sC)
                s_E2 = sC.v
                sch.emit("vector", lambda e, c=c: e.tensor_tensor(
                    e16b[:, :], bdt[:, c, :], e16f[:, :], AL.mult),
                    waits=[(sC, s_E2)], inc=vC)
                v_bt = vC.v
                for th in range(2):
                    sch.emit("tensor", lambda e, th=th: e.transpose(
                        pbb(6 + th, 16), e16b[0:16, th * 128:(th + 1) * 128],
                        idn_s[0:16, 0:16]),
                        waits=[(vC, v_bt)], inc=tC)
                    t_bt = tC.v
                    sch.emit("vector", lambda e, c=c, th=th: e.tensor_copy(
                        btT[:, c, th, :], pbb(6 + th, 16)), waits=[(tC, t_bt)], inc=vC)
                sch.emit("vector", lambda e: e.tensor_tensor(
                    sc16[0:16, 3:4], sc16[0:16, 1:2], sc16[0:16, 0:1], AL.subtract),
                    inc=vC)
                v_ec = vC.v
                sch.emit("scalar", lambda e, c=c: e.activation(
                    ec16[0:16, c:c + 1], sc16[0:16, 3:4], AF.Exp, scale=a16[0:16, 0:1]),
                    waits=[(vC, v_ec)], inc=sC)
            s_prep = sC.v
            sch.emit("sync", lambda e: e.dma_start(out=col_d[:, 0:NCH], in_=ec16[0:16, :], allow_slow_non_contiguous=True),
                     waits=[(sC, s_prep)], inc=dX, inc_n=16)
            d_ec = dX.v
            for g in range(8):
                sch.emit("sync", lambda e, g=g: e.dma_start(
                    out=ec128[16 * g:16 * (g + 1), :], in_=col_d[:, 0:NCH],
                    allow_slow_non_contiguous=True),
                    waits=[(dX, d_ec)], inc=dX, inc_n=16)
            d_ec128 = dX.v

            if r == 0:
                tap("t_pB", pB_s[:, :], [(vC, vC.v)])
                tap("t_pC", pC_s[:, :], [(vC, vC.v)])
                tap("t_dt", dt_s[:, :], [(sC, sC.v)])
                tap("t_ug", ug_s[:, :], [(vC, vC.v)])
                tap("t_cf", cf_s[:, :, :], [(vC, vC.v)])
                tap("t_bdtr", bdtr[:, :, :], [(vC, vC.v)])
                tap("t_cbar", cbar[:, :, :], [(vC, vC.v)])

            def emit_xcT(c):
                lo = c * L
                for th in range(2):
                    for dh in range(NDI):
                        bank = 6 + ((th * NDI + dh) % 2)
                        vprev = vC.v
                        sch.emit("tensor", lambda e, th=th, dh=dh, bank=bank, lo=lo:
                                 e.transpose(
                                     pbb(bank, 128),
                                     xcv[:, dh, lo + th * 128: lo + (th + 1) * 128],
                                     idn_s[:, :]),
                                 waits=[(vC, max(0, vprev - 1))], inc=tC)
                        tm = tC.v
                        sch.emit("vector", lambda e, th=th, dh=dh, bank=bank: e.tensor_copy(
                            xcT[:, th, dh * 128:(dh + 1) * 128], pbb(bank, 128)),
                            waits=[(tC, tm)], inc=vC)
                return vC.v

            sch.emit("vector", lambda e: e.memset(hloc[:, :], 0.0), inc=vC, selfwait=True)
            hv = {}
            for c in range(NCH):
                v_xcT = emit_xcT(c)
                for dgr in range(8):
                    hb = 4 + dgr % 2
                    sch.emit("tensor", lambda e, c=c, dgr=dgr, hb=hb: e.matmul(
                        pb[hb][0:16, :], btT[:, c, 0, :],
                        xcT[:, 0, dgr * 512:(dgr + 1) * 512], start=True, stop=False,
                        skip_group_check=True),
                        waits=[(vC, max(v_xcT, hv.get((c, dgr - 2), 0)))])
                    sch.emit("tensor", lambda e, c=c, dgr=dgr, hb=hb: e.matmul(
                        pb[hb][0:16, :], btT[:, c, 1, :],
                        xcT[:, 1, dgr * 512:(dgr + 1) * 512], start=False, stop=True,
                        skip_group_check=True), inc=tC)
                    t_h = tC.v
                    dhp = dH.v
                    sch.emit("vector", lambda e, hb=hb: e.tensor_copy(
                        hst[0:16, :], pb[hb][0:16, :]),
                        waits=[(tC, t_h), (dH, dhp)], inc=vC)
                    hv[(c, dgr)] = vC.v
                    v_h = vC.v
                    sch.emit("sync", lambda e, c=c, dgr=dgr: e.dma_start(
                        out=hpack[16 * dgr:16 * (dgr + 1), c, :], in_=hst[0:16, :]),
                        waits=[(vC, v_h)], inc=dH, inc_n=16)
                d_hc = dH.v
                sch.emit("vector", lambda e, c=c: e.scalar_tensor_tensor(
                    hloc[:, :], hloc[:, :], ec128[:, c:c + 1], hpack[:, c, :],
                    AL.mult, AL.add), waits=[(dH, d_hc), (dX, d_ec128)], inc=vC)
            v_hloc = vC.v

            if r == 0:
                tap("t_hpack", hpack[:, :, :], [(vC, vC.v)])
                tap("t_hloc", hloc[:, :], [(vC, vC.v)])

            sch.emit("sync", lambda e: e.dma_start(out=gin_d[:, 0:512], in_=hloc[:, :]),
                     waits=[(vC, v_hloc)], inc=dX, inc_n=16)
            sch.emit("sync", lambda e: e.dma_start(out=gin_d[0:1, 512:513],
                                                   in_=ug_s[0:1, TC - 1:TC]),
                     inc=dX, inc_n=16)
            sch.emit("sync", lambda e: e.dma_start(out=gin_d[1:128, 512:513],
                                                   in_=zcol[0:127, :],
                                                   allow_slow_non_contiguous=True),
                     inc=dX, inc_n=16)
            d_gin = dX.v
            sch.emit("gpsimd", lambda e: e.collective_compute(
                "AllGather", AL.bypass, replica_groups=[list(range(NCORE))],
                ins=[gin_d[:, :]], outs=[gout_d[:, :]]),
                waits=[(dX, d_gin)], inc=cc)
            cc_g = cc.v
            sch.emit("vector", lambda e: e.memset(hrun[:, :], 0.0), inc=vC, selfwait=True)
            for j in range(NCORE - 1):
                vpr = vC.v
                sch.emit("sync", lambda e, j=j: e.dma_start(
                    out=gsl[:, :], in_=gout_d[128 * j:128 * (j + 1), 0:512]),
                    waits=[(cc, cc_g), (vC, vpr)], inc=dX, inc_n=16)
                sch.emit("sync", lambda e, j=j: e.dma_start(
                    out=ujb[:, :],
                    in_=gout_d[128 * j:128 * j + 1, 512:513].broadcast_to([128, 1])),
                    inc=dX, inc_n=16)
                d_j = dX.v
                sch.emit("vector", lambda e, j=j: e.tensor_tensor(
                    ujb[:, :], ujb[:, :], chm_s[:, j:j + 1], AL.mult),
                    waits=[(dX, d_j)], inc=vC)
                v_um = vC.v
                sch.emit("scalar", lambda e: e.activation(ej_s[:, :], ujb[:, :], AF.Exp,
                                                          scale=a128[:, 0:1]),
                         waits=[(vC, v_um)], inc=sC)
                s_ej = sC.v
                sch.emit("vector", lambda e: e.tensor_scalar(
                    htmp[:, :], hrun[:, :], ej_s[:, 0:1], None, AL.mult),
                    waits=[(sC, s_ej)], inc=vC)
                sch.emit("vector", lambda e, j=j: e.scalar_tensor_tensor(
                    hrun[:, :], gsl[:, :], chm_s[:, j:j + 1], htmp[:, :],
                    AL.mult, AL.add), inc=vC)

            if r == 0:
                tap("t_hrun", hrun[:, :], [(vC, vC.v)])

            yev = {}
            gv = {}
            for c in range(NCH):
                lo = c * L
                v_hr = vC.v
                t_pre = tC.v
                for g in range(8):
                    sch.emit("gpsimd", lambda e, g=g: e.dma_start(
                        out=h0b[0:16, 512 * g:512 * (g + 1)],
                        in_=hrun[16 * g:16 * (g + 1), :]),
                        waits=([(vC, v_hr), (tC, t_pre)] if g == 0 else []),
                        inc=dW, inc_n=16)
                d_h0w = dW.v
                v_xcT = emit_xcT(c)
                for th in range(2):
                    gb = 4 + th
                    sch.emit("tensor", lambda e, c=c, th=th, gb=gb: e.matmul(
                        pbf(gb, L), bdtr[0:16, c, th * 128:(th + 1) * 128],
                        cf_s[0:16, c, :], start=True, stop=True),
                        waits=[(vC, max(v_xcT, gv.get((c - 1, th), 0)))], inc=tC)
                    t_g = tC.v
                    sch.emit("vector", lambda e, th=th, gb=gb: e.tensor_tensor(
                        gsb[:, th, :], pbf(gb, L), gm_s[:, th, :], AL.mult),
                        waits=[(tC, t_g)], inc=vC)
                    gv[(c, th)] = vC.v
                v_g = vC.v
                for dh in range(NDI):
                    pban = dh % 2
                    sch.emit("tensor", lambda e, c=c, dh=dh, pban=pban: e.matmul(
                        pbf(pban, L), h0b[0:16, dh * 128:(dh + 1) * 128],
                        cbar[0:16, c, :], start=True, stop=False, skip_group_check=True),
                        waits=[(vC, v_g), (dW, d_h0w),
                               (vC, yev.get(dh - 2, 0))])
                    for th in range(2):
                        sch.emit("tensor", lambda e, c=c, dh=dh, th=th, pban=pban: e.matmul(
                            pbf(pban, L), xcT[:, th, dh * 128:(dh + 1) * 128],
                            gsb[:, th, :], start=False, stop=(th == 1),
                            skip_group_check=True),
                            inc=(tC if th == 1 else None))
                    t_y = tC.v
                    sch.emit("vector", lambda e, c=c, dh=dh, pban=pban, lo=lo:
                             e.scalar_tensor_tensor(
                                 xcv[:, dh, lo:lo + L], xcv[:, dh, lo:lo + L],
                                 dv_s[:, dh:dh + 1], pbf(pban, L), AL.mult, AL.add),
                             waits=[(tC, t_y)], inc=vC)
                    yev[dh] = vC.v
                sch.emit("vector", lambda e, c=c: e.scalar_tensor_tensor(
                    hrun[:, :], hrun[:, :], ec128[:, c:c + 1], hpack[:, c, :],
                    AL.mult, AL.add), waits=[(dW, d_h0w)], inc=vC)
            if r == 0:
                tap("t_ymix", xcv[:, :, :], [(vC, vC.v)])
            barrier()

            evz = {}
            for g in range(16):
                sch.emit("gpsimd", lambda e, g=g: e.dma_start(
                    out=wbA[:, g % 2, :, :],
                    in_=w_inT[:, DI + 256 * g:DI + 256 * (g + 1)].rearrange(
                        "(kh p) n -> p kh n", p=128)),
                    waits=[(tC, grp_t.get(("z", r, g - 2), 0))], inc=dW, inc_n=16)
                d_g = dW.v
                for nl in range(2):
                    n_hat = g * 2 + nl
                    pA, pB = (0, 1) if n_hat % 2 == 0 else (2, 3)
                    for kh in range(16):
                        w0 = [(dW, d_g), (sC, evz.get(n_hat - 2, 0))] if kh == 0 else []
                        sch.emit("tensor", lambda e, g=g, nl=nl, kh=kh, pA=pA: e.matmul(
                            pbf(pA), wbA[:, g % 2, kh, nl * 128:(nl + 1) * 128],
                            xf[:, kh, 3:515], start=(kh == 0), stop=(kh == 15),
                            skip_group_check=True), waits=w0)
                        sch.emit("tensor", lambda e, g=g, nl=nl, kh=kh, pB=pB: e.matmul(
                            pbf(pB), wbA[:, g % 2, kh, nl * 128:(nl + 1) * 128],
                            xf[:, kh, 515:1027], start=(kh == 0), stop=(kh == 15),
                            skip_group_check=True),
                            inc=(tC if kh == 15 else None))
                    grp_t[("z", r, g)] = tC.v
                    t_mm = tC.v
                    v_pr = vC.v
                    sch.emit("scalar", lambda e, pA=pA: e.activation(
                        zsil[:, 0, :], pbf(pA), AF.Silu),
                        waits=[(tC, t_mm), (vC, v_pr)], inc=sC)
                    sch.emit("scalar", lambda e, pB=pB: e.activation(
                        zsil[:, 1, :], pbf(pB), AF.Silu), inc=sC)
                    evz[n_hat] = sC.v
                    s_z = sC.v
                    sch.emit("vector", lambda e, n_hat=n_hat: e.tensor_tensor(
                        xcv[:, n_hat, 0:512], xcv[:, n_hat, 0:512], zsil[:, 0, :],
                        AL.mult), waits=[(sC, s_z)], inc=vC)
                    sch.emit("vector", lambda e, n_hat=n_hat: e.tensor_tensor(
                        xcv[:, n_hat, 512:1024], xcv[:, n_hat, 512:1024], zsil[:, 1, :],
                        AL.mult), inc=vC)
            if r == 0:
                tap("t_g", xcv[:, :, :], [(vC, vC.v)])
            barrier()

            evo = {}
            for oh in range(ND):
                sch.emit("gpsimd", lambda e, oh=oh: e.dma_start(
                    out=wbO[:, oh % 2, :, :],
                    in_=out_wT[:, 128 * oh:128 * (oh + 1)].rearrange(
                        "(kh p) o -> p kh o", p=128)),
                    waits=[(tC, grp_t.get(("o", r, oh - 2), 0))], inc=dW, inc_n=16)
                d_g = dW.v
                pA, pB = (0, 1) if oh % 2 == 0 else (2, 3)
                for kh in range(NDI):
                    w0 = [(dW, d_g), (vC, evo.get(oh - 2, 0))] if kh == 0 else []
                    sch.emit("tensor", lambda e, oh=oh, kh=kh, pA=pA: e.matmul(
                        pbf(pA), wbO[:, oh % 2, kh, :], xcv[:, kh, 0:512],
                        start=(kh == 0), stop=(kh == NDI - 1), skip_group_check=True),
                        waits=w0)
                    sch.emit("tensor", lambda e, oh=oh, kh=kh, pB=pB: e.matmul(
                        pbf(pB), wbO[:, oh % 2, kh, :], xcv[:, kh, 512:1024],
                        start=(kh == 0), stop=(kh == NDI - 1), skip_group_check=True),
                        inc=(tC if kh == NDI - 1 else None))
                grp_t[("o", r, oh)] = tC.v
                t_mm = tC.v
                sch.emit("vector", lambda e, oh=oh, pA=pA: e.tensor_copy(
                    attn_f[:, oh, 0:512], pbf(pA)), waits=[(tC, t_mm)], inc=vC)
                sch.emit("vector", lambda e, oh=oh, pB=pB: e.tensor_copy(
                    attn_f[:, oh, 512:1024], pbf(pB)), inc=vC)
                evo[oh] = vC.v
            if r == 0:
                tap("t_attn", attn_f[:, :, :], [(vC, vC.v)])
            barrier()

            for tt in range(8):
                vpr = vC.v
                sch.emit("sync", lambda e, tt=tt, r=r: e.dma_start(
                    out=stage2[:, :],
                    in_=xm[r].rearrange("(n p) d -> p n d", p=128)[:, tt, :]),
                    waits=[(vC, vpr)], inc=dX, inc_n=16)
                d_x = dX.v
                for oh in range(ND):
                    bank = 6 + (oh % 2)
                    vprev = vC.v
                    sch.emit("tensor", lambda e, tt=tt, oh=oh, bank=bank: e.transpose(
                        pb[bank][:, 0:128], attn_f[:, oh, tt * 128:(tt + 1) * 128],
                        idnf_s[:, :]),
                        waits=[(vC, max(0, vprev - 1)), (dX, d_x)], inc=tC)
                    tm = tC.v
                    sch.emit("vector", lambda e, oh=oh, bank=bank: e.scalar_tensor_tensor(
                        stage2[:, oh * 128:(oh + 1) * 128], pb[bank][:, 0:128], 1.0,
                        stage2[:, oh * 128:(oh + 1) * 128], AL.mult, AL.add,
                        accum_out=sumc[:, oh:oh + 1]),
                        waits=[(tC, tm)], inc=vC)
                v_s2 = vC.v
                sch.emit("vector", lambda e: e.tensor_reduce(
                    sumc[:, 16:17], sumc[:, 0:16], AX, AL.add), waits=[(vC, v_s2)], inc=vC)
                sch.emit("scalar", lambda e: e.activation(
                    x1t[:, :], stage2[:, :], AF.Square, accum_out=sumc[:, 17:18]),
                    waits=[(vC, v_s2)], inc=sC)
                s_sq = sC.v
                sch.emit("vector", lambda e: e.tensor_scalar(
                    sumc[:, 18:19], sumc[:, 16:17], 1.0 / D, None, AL.mult), inc=vC)
                sch.emit("vector", lambda e: e.tensor_scalar(
                    sumc[:, 19:20], sumc[:, 17:18], 1.0 / D, None, AL.mult),
                    waits=[(sC, s_sq)], inc=vC)
                sch.emit("vector", lambda e: e.tensor_tensor(
                    sumc[:, 20:21], sumc[:, 18:19], sumc[:, 18:19], AL.mult), inc=vC)
                sch.emit("vector", lambda e: e.tensor_tensor(
                    sumc[:, 19:20], sumc[:, 19:20], sumc[:, 20:21], AL.subtract), inc=vC)
                v_var = vC.v
                sch.emit("scalar", lambda e: e.activation(
                    sumc[:, 20:21], sumc[:, 19:20], AF.Sqrt, bias=cst[:, 0:1]),
                    waits=[(vC, v_var)], inc=sC)
                s_std = sC.v
                sch.emit("vector", lambda e: e.reciprocal(sumc[:, 21:22], sumc[:, 20:21]),
                         waits=[(sC, s_std)], inc=vC)
                sch.emit("vector", lambda e: e.tensor_tensor(
                    sumc[:, 22:23], sumc[:, 18:19], sumc[:, 21:22], AL.mult), inc=vC)
                sch.emit("vector", lambda e: e.tensor_scalar(
                    sumc[:, 22:23], sumc[:, 22:23], -1.0, None, AL.mult), inc=vC)
                v_inv = vC.v
                sch.emit("scalar", lambda e: e.activation(
                    x1t[:, :], stage2[:, :], AF.Identity, scale=sumc[:, 21:22],
                    bias=sumc[:, 22:23]), waits=[(vC, v_inv)], inc=sC)
                s_x1 = sC.v
                sch.emit("sync", lambda e, tt=tt: e.dma_start(
                    out=x1_d[tt, :, :], in_=x1t[:, :]), waits=[(sC, s_x1)],
                    inc=dX, inc_n=16)
                sch.emit("vector", lambda e: e.tensor_reduce(
                    sumc[:, 16:17], x1t[:, :], AX, AL.max, apply_absolute_value=True),
                    waits=[(sC, s_x1)], inc=vC)
                sch.emit("vector", lambda e: e.tensor_scalar(
                    sumc[:, 17:18], sumc[:, 16:17], 1.0 / 127.0, EPS, AL.mult, AL.add),
                    inc=vC)
                sch.emit("vector", lambda e, tt=tt: e.tensor_copy(
                    scol8[:, tt:tt + 1], sumc[:, 17:18]), inc=vC)
                sch.emit("vector", lambda e: e.reciprocal(sumc[:, 18:19], sumc[:, 17:18]),
                         inc=vC)
                sch.emit("vector", lambda e: e.tensor_scalar(
                    stage2[:, :], x1t[:, :], sumc[:, 18:19], MAGIC, AL.mult, AL.add),
                    inc=vC)
                sch.emit("vector", lambda e: e.tensor_scalar(
                    xqt[:, :], stage2[:, :], MAGIC, None, AL.subtract), inc=vC)
                v_xq = vC.v
                for oh in range(ND):
                    bank = 6 + (oh % 2)
                    vprev = vC.v
                    sch.emit("tensor", lambda e, oh=oh, bank=bank: e.transpose(
                        pbb(bank, 128), xqt[:, oh * 128:(oh + 1) * 128], idn_s[:, :]),
                        waits=[(vC, v_xq), (vC, max(0, vprev - 1))], inc=tC)
                    tm = tC.v
                    sch.emit("vector", lambda e, tt=tt, oh=oh, bank=bank: e.tensor_copy(
                        xq_f[:, oh, tt * 128:(tt + 1) * 128], pbb(bank, 128)),
                        waits=[(tC, tm)], inc=vC)
            if r == 0:
                tap("t_xq", xq_f[:, :, :], [(vC, vC.v)])
            barrier()
            v_sc = vC.v
            sch.emit("tensor", lambda e: e.transpose(pb[6][0:8, 0:128], scol8[:, :],
                                                     idnf_s[:, :]),
                     waits=[(vC, v_sc)], inc=tC)
            t_sc = tC.v
            sch.emit("vector", lambda e: e.tensor_copy(rst[:, :], pb[6][0:8, 0:128]),
                     waits=[(tC, t_sc)], inc=vC)
            v_sc2 = vC.v
            sch.emit("sync", lambda e: e.dma_start(
                out=row_d[3:4, 0:TC].rearrange("o (b t) -> (o b) t", b=8),
                in_=rst[:, :]), waits=[(vC, v_sc2)], inc=dX, inc_n=16)
            d_sc = dX.v
            sch.emit("sync", lambda e: e.dma_start(
                out=sg128[:, :], in_=row_d[3:4, 0:TC].broadcast_to([128, TC])),
                waits=[(dX, d_sc)], inc=dX, inc_n=16)
            d_sg = dX.v
            sch.emit("vector", lambda e: e.tensor_scalar(
                su128[:, :], sg128[:, :], wsb_s[:, 1:2], None, AL.mult),
                waits=[(dX, d_sg)], inc=vC)
            sch.emit("vector", lambda e: e.tensor_scalar(
                sg128[:, :], sg128[:, :], wsb_s[:, 0:1], None, AL.mult), inc=vC)
            v_sgsu = vC.v

            def quant_w(dst_ap, src_ap, n_el, wcol, dwait):
                sch.emit("scalar", lambda e, dst_ap=dst_ap, src_ap=src_ap, wcol=wcol,
                         n_el=n_el: e.activation(
                             qtmp[:, 0:n_el], src_ap, AF.Identity,
                             scale=rws_s[:, wcol:wcol + 1], bias=cst[:, 2:3]),
                         waits=[(dX, dwait)], inc=sC)
                s_q = sC.v
                sch.emit("vector", lambda e, n_el=n_el: e.tensor_scalar(
                    qtmp[:, 0:n_el], qtmp[:, 0:n_el], MAGIC, 1.0, AL.subtract, AL.min),
                    waits=[(sC, s_q)], inc=vC)
                sch.emit("vector", lambda e, dst_ap=dst_ap, n_el=n_el: e.tensor_scalar(
                    dst_ap, qtmp[:, 0:n_el], -1.0, None, AL.max), inc=vC)

            sch.emit("vector", lambda e: e.memset(mxp[:, :], 0.0), inc=vC, selfwait=True)
            evf = {}
            for nh in range(NFF):
                par = nh % 2
                if r == 0:
                    for wname, wsrc, wdst, scr, wcol in [
                            ("g", w_gT, wbG, scr_g, 0), ("u", w_uT, wbU, scr_u, 1)]:
                        for qrt in range(4):
                            dpr = dX.v
                            sch.emit("sync", lambda e, nh=nh, wsrc=wsrc, qrt=qrt:
                                     e.dma_start(
                                         out=wfst[:, :, :],
                                         in_=wsrc[512 * qrt:512 * (qrt + 1),
                                                  128 * nh:128 * (nh + 1)].rearrange(
                                             "(kh p) n -> p kh n", p=128)),
                                     waits=[(tC, grp_t.get(("f", r, nh - 2), 0)),
                                            (vC, vC.v), (dX, dpr)],
                                     inc=dX, inc_n=16)
                            d_w = dX.v
                            quant_w(wdst[:, par, 4 * qrt:4 * (qrt + 1), :].rearrange(
                                "p a b -> p (a b)"),
                                wfst[:, :, :].rearrange("p a b -> p (a b)"),
                                512, wcol, d_w)
                        v_q = vC.v
                        sch.emit("sync", lambda e, nh=nh, scr=scr, par=par, wdst=wdst:
                                 e.dma_start(
                                     out=scr[:, 128 * nh:128 * (nh + 1)].rearrange(
                                         "(kh p) n -> p kh n", p=128),
                                     in_=wdst[:, par, :, :]),
                                 waits=[(vC, v_q)], inc=dX, inc_n=16)
                    w_wait = [(vC, vC.v)]
                else:
                    sch.emit("sync", lambda e, nh=nh, par=par: e.dma_start(
                        out=wbG[:, par, :, :],
                        in_=scr_g[:, 128 * nh:128 * (nh + 1)].rearrange(
                            "(kh p) n -> p kh n", p=128)),
                        waits=[(tC, grp_t.get(("f", r, nh - 2), 0))], inc=dX, inc_n=16)
                    sch.emit("sync", lambda e, nh=nh, par=par: e.dma_start(
                        out=wbU[:, par, :, :],
                        in_=scr_u[:, 128 * nh:128 * (nh + 1)].rearrange(
                            "(kh p) n -> p kh n", p=128)),
                        inc=dX, inc_n=16)
                    w_wait = [(dX, dX.v)]
                pg = (0, 1) if par == 0 else (4, 5)
                pu = (2, 3) if par == 0 else (6, 7)
                for kh in range(ND):
                    w0 = (w_wait + [(vC, evf.get(nh - 2, (0, 0))[0]),
                                    (sC, evf.get(nh - 2, (0, 0))[1])]) if kh == 0 else []
                    sch.emit("tensor", lambda e, kh=kh, par=par, pg=pg: e.matmul(
                        pbf(pg[0]), wbG[:, par, kh, :], xq_f[:, kh, 0:512],
                        start=(kh == 0), stop=(kh == ND - 1), skip_group_check=True),
                        waits=w0)
                    sch.emit("tensor", lambda e, kh=kh, par=par, pg=pg: e.matmul(
                        pbf(pg[1]), wbG[:, par, kh, :], xq_f[:, kh, 512:1024],
                        start=(kh == 0), stop=(kh == ND - 1), skip_group_check=True))
                    sch.emit("tensor", lambda e, kh=kh, par=par, pu=pu: e.matmul(
                        pbf(pu[0]), wbU[:, par, kh, :], xq_f[:, kh, 0:512],
                        start=(kh == 0), stop=(kh == ND - 1), skip_group_check=True))
                    sch.emit("tensor", lambda e, kh=kh, par=par, pu=pu: e.matmul(
                        pbf(pu[1]), wbU[:, par, kh, :], xq_f[:, kh, 512:1024],
                        start=(kh == 0), stop=(kh == ND - 1), skip_group_check=True),
                        inc=(tC if kh == ND - 1 else None))
                grp_t[("f", r, nh)] = tC.v
                t_mm = tC.v
                sch.emit("vector", lambda e, pg=pg: e.tensor_tensor(
                    pbf(pg[0]), pbf(pg[0]), sg128[:, 0:512], AL.mult),
                    waits=[(tC, t_mm), (vC, v_sgsu)], inc=vC)
                sch.emit("vector", lambda e, pg=pg: e.tensor_tensor(
                    pbf(pg[1]), pbf(pg[1]), sg128[:, 512:1024], AL.mult), inc=vC)
                sch.emit("vector", lambda e, pu=pu: e.tensor_tensor(
                    pbf(pu[0]), pbf(pu[0]), su128[:, 0:512], AL.mult), inc=vC)
                sch.emit("vector", lambda e, pu=pu: e.tensor_tensor(
                    pbf(pu[1]), pbf(pu[1]), su128[:, 512:1024], AL.mult), inc=vC)
                v_dq = vC.v
                sch.emit("scalar", lambda e, pg=pg: e.activation(
                    sil2[:, 0, :], pbf(pg[0]), AF.Silu),
                    waits=[(vC, v_dq)], inc=sC)
                sch.emit("scalar", lambda e, pg=pg: e.activation(
                    sil2[:, 1, :], pbf(pg[1]), AF.Silu), inc=sC)
                s_sil = sC.v
                sch.emit("vector", lambda e, nh=nh, pu=pu: e.tensor_tensor(
                    g2[:, nh, 0:512], sil2[:, 0, :], pbf(pu[0]), AL.mult),
                    waits=[(sC, s_sil)], inc=vC)
                sch.emit("vector", lambda e, nh=nh, pu=pu: e.tensor_tensor(
                    g2[:, nh, 512:1024], sil2[:, 1, :], pbf(pu[1]), AL.mult), inc=vC)
                evf[nh] = (vC.v, sC.v)
                sch.emit("vector", lambda e, nh=nh: e.tensor_tensor(
                    mxp[:, :], mxp[:, :], g2[:, nh, :], AL.max), inc=vC)
                sch.emit("vector", lambda e, nh=nh: e.scalar_tensor_tensor(
                    mxp[:, :], g2[:, nh, :], -1.0, mxp[:, :], AL.mult, AL.max), inc=vC)
            v_gup = vC.v
            v_mx = vC.v
            for tb in range(8):
                bank = 6 + (tb % 2)
                vprev = vC.v
                sch.emit("tensor", lambda e, tb=tb, bank=bank: e.transpose(
                    pb[bank][:, 0:128], mxp[:, tb * 128:(tb + 1) * 128], idnf_s[:, :]),
                    waits=[(vC, v_mx), (vC, max(0, vprev - 1))], inc=tC)
                tm = tC.v
                sch.emit("vector", lambda e, tb=tb, bank=bank: e.tensor_reduce(
                    amx[:, tb:tb + 1], pb[bank][:, 0:128], AX, AL.max),
                    waits=[(tC, tm)], inc=vC)
            sch.emit("vector", lambda e: e.tensor_scalar(
                amx[:, 8:16], amx[:, 0:8], 1.0 / 127.0, EPS, AL.mult, AL.add), inc=vC)
            v_s2 = vC.v
            sch.emit("tensor", lambda e: e.transpose(pb[6][0:8, 0:128], amx[:, 8:16],
                                                     idnf_s[:, :]),
                     waits=[(vC, v_s2)], inc=tC)
            t_s2 = tC.v
            sch.emit("vector", lambda e: e.tensor_copy(rst[:, :], pb[6][0:8, 0:128]),
                     waits=[(tC, t_s2)], inc=vC)
            v_s2b = vC.v
            sch.emit("sync", lambda e: e.dma_start(
                out=row_d[2:3, 0:TC].rearrange("o (b t) -> (o b) t", b=8),
                in_=rst[:, :]), waits=[(vC, v_s2b)], inc=dX, inc_n=16)
            d_s2 = dX.v
            sch.emit("sync", lambda e: e.dma_start(
                out=s2b[:, :], in_=row_d[2:3, 0:TC].broadcast_to([128, TC])),
                waits=[(dX, d_s2)], inc=dX, inc_n=16)
            sch.emit("vector", lambda e: e.reciprocal(amx[:, 8:16], amx[:, 8:16]),
                     waits=[(vC, v_s2b)], inc=vC)
            v_r = vC.v
            sch.emit("tensor", lambda e: e.transpose(pb[6][0:8, 0:128], amx[:, 8:16],
                                                     idnf_s[:, :]),
                     waits=[(vC, v_r)], inc=tC)
            t_r = tC.v
            sch.emit("vector", lambda e: e.tensor_copy(rst[:, :], pb[6][0:8, 0:128]),
                     waits=[(tC, t_r), (dX, d_s2)], inc=vC)
            v_r2 = vC.v
            sch.emit("sync", lambda e: e.dma_start(
                out=row_d[2:3, 0:TC].rearrange("o (b t) -> (o b) t", b=8),
                in_=rst[:, :]),
                waits=[(vC, v_r2)], inc=dX, inc_n=16)
            d_r = dX.v
            sch.emit("sync", lambda e: e.dma_start(
                out=r128[:, :], in_=row_d[2:3, 0:TC].broadcast_to([128, TC])),
                waits=[(dX, d_r)], inc=dX, inc_n=16)
            d_r128 = dX.v
            if r == 0:
                tap("t_g2r", g2[:, :, :], [(vC, vC.v)])
                tap("t_r128", r128[:, :], [(dX, d_r128)])
                tap("t_wq", wbG[:, :, :, :], [(vC, vC.v)])
            for nh in range(NFF):
                for hf in range(2):
                    w0 = [(dX, d_r128)] if (nh == 0 and hf == 0) else []
                    sch.emit("vector", lambda e, nh=nh, hf=hf: e.tensor_tensor(
                        qtmp[:, :], g2[:, nh, 512 * hf:512 * (hf + 1)],
                        r128[:, 512 * hf:512 * (hf + 1)], AL.mult), waits=w0, inc=vC)
                    sch.emit("vector", lambda e: e.tensor_scalar(
                        qtmp[:, :], qtmp[:, :], MAGIC, None, AL.add), inc=vC)
                    sch.emit("vector", lambda e, nh=nh, hf=hf: e.tensor_scalar(
                        g2[:, nh, 512 * hf:512 * (hf + 1)], qtmp[:, :],
                        MAGIC, None, AL.subtract), inc=vC)
            if r == 0:
                tap("t_g2", g2[:, :, :], [(vC, vC.v)])
            barrier()

            def quant_d(dst_ap, src_ap, n_el, dwait):
                sch.emit("scalar", lambda e, dst_ap=dst_ap, src_ap=src_ap, n_el=n_el:
                         e.activation(qtmp2[:, 0:n_el], src_ap, AF.Identity,
                                      scale=rws_s[:, 2:3], bias=cst[:, 2:3]),
                         waits=[(dX, dwait)], inc=sC)
                s_q = sC.v
                sch.emit("vector", lambda e, n_el=n_el: e.tensor_scalar(
                    qtmp2[:, 0:n_el], qtmp2[:, 0:n_el], MAGIC, 1.0, AL.subtract, AL.min),
                    waits=[(sC, s_q)], inc=vC)
                sch.emit("vector", lambda e, dst_ap=dst_ap, n_el=n_el: e.tensor_scalar(
                    dst_ap, qtmp2[:, 0:n_el], -1.0, None, AL.max), inc=vC)

            evd = {}
            for oh in range(ND):
                par = oh % 2
                pA, pB = (0, 1) if par == 0 else (2, 3)
                if r == 0:
                    for half in range(2):
                        for q4 in range(4):
                            k0 = half * 4096 + q4 * 1024
                            dpr = dX.v
                            sch.emit("sync", lambda e, oh=oh, k0=k0: e.dma_start(
                                out=wfst2[:, :, :],
                                in_=w_dT[k0:k0 + 1024,
                                         128 * oh:128 * (oh + 1)].rearrange(
                                    "(kh p) n -> p kh n", p=128)),
                                waits=[(tC, tC.v), (vC, vC.v), (dX, dpr)],
                                inc=dX, inc_n=16)
                            d_w = dX.v
                            quant_d(wbD[:, par, (half * 32 + q4 * 8) % 32:
                                        (half * 32 + q4 * 8) % 32 + 8, :].rearrange(
                                "p a b -> p (a b)"),
                                wfst2[:, :, :].rearrange("p a b -> p (a b)"),
                                1024, d_w)
                        v_q = vC.v
                        sch.emit("sync", lambda e, oh=oh, par=par, half=half: e.dma_start(
                            out=scr_d[4096 * half:4096 * (half + 1),
                                      128 * oh:128 * (oh + 1)].rearrange(
                                "(kh p) n -> p kh n", p=128),
                            in_=wbD[:, par, :, :]), waits=[(vC, v_q)], inc=dX, inc_n=16)
                        for kh in range(32):
                            khg = half * 32 + kh
                            w0 = [(vC, v_q), (vC, evd.get(oh - 2, 0))] if kh == 0 else []
                            sch.emit("tensor", lambda e, kh=kh, khg=khg, par=par, pA=pA:
                                     e.matmul(
                                         pbf(pA), wbD[:, par, kh, :], g2[:, khg, 0:512],
                                         start=(khg == 0), stop=(khg == NFF - 1),
                                         skip_group_check=True), waits=w0)
                            sch.emit("tensor", lambda e, kh=kh, khg=khg, par=par, pB=pB:
                                     e.matmul(
                                         pbf(pB), wbD[:, par, kh, :], g2[:, khg, 512:1024],
                                         start=(khg == 0), stop=(khg == NFF - 1),
                                         skip_group_check=True),
                                     inc=(tC if kh == 31 else None))
                else:
                    for half in range(2):
                        sch.emit("sync", lambda e, oh=oh, par=par, half=half: e.dma_start(
                            out=wbD[:, par, :, :],
                            in_=scr_d[4096 * half:4096 * (half + 1),
                                      128 * oh:128 * (oh + 1)].rearrange(
                                "(kh p) n -> p kh n", p=128)),
                            waits=[(tC, tC.v), (vC, evd.get(oh - 2, 0))],
                            inc=dX, inc_n=16)
                        d_w = dX.v
                        for kh in range(32):
                            khg = half * 32 + kh
                            w0 = [(dX, d_w)] if kh == 0 else []
                            sch.emit("tensor", lambda e, kh=kh, khg=khg, par=par, pA=pA:
                                     e.matmul(
                                         pbf(pA), wbD[:, par, kh, :], g2[:, khg, 0:512],
                                         start=(khg == 0), stop=(khg == NFF - 1),
                                         skip_group_check=True), waits=w0)
                            sch.emit("tensor", lambda e, kh=kh, khg=khg, par=par, pB=pB:
                                     e.matmul(
                                         pbf(pB), wbD[:, par, kh, :], g2[:, khg, 512:1024],
                                         start=(khg == 0), stop=(khg == NFF - 1),
                                         skip_group_check=True),
                                     inc=(tC if kh == 31 else None))
                grp_t[("d", r, oh)] = tC.v
                t_mm = tC.v
                sch.emit("vector", lambda e, oh=oh, pA=pA: e.scalar_tensor_tensor(
                    dnf[:, oh, 0:512], pbf(pA), wsb_s[:, 2:3], s2b[:, 0:512],
                    AL.mult, AL.mult), waits=[(tC, t_mm)], inc=vC)
                sch.emit("vector", lambda e, oh=oh, pB=pB: e.scalar_tensor_tensor(
                    dnf[:, oh, 512:1024], pbf(pB), wsb_s[:, 2:3], s2b[:, 512:1024],
                    AL.mult, AL.mult), inc=vC)
                evd[oh] = vC.v
            barrier()

            for tt in range(8):
                vpr = vC.v
                sch.emit("sync", lambda e, tt=tt: e.dma_start(
                    out=x1rr[:, :], in_=x1_d[tt, :, :]), waits=[(vC, vpr)],
                    inc=dX, inc_n=16)
                d_x1 = dX.v
                for oh in range(ND):
                    bank = 6 + (oh % 2)
                    vprev = vC.v
                    sch.emit("tensor", lambda e, tt=tt, oh=oh, bank=bank: e.transpose(
                        pbb(bank, 128), dnf[:, oh, tt * 128:(tt + 1) * 128], idn_s[:, :]),
                        waits=[(vC, max(0, vprev - 1))], inc=tC)
                    tm = tC.v
                    sch.emit("vector", lambda e, oh=oh, bank=bank: e.scalar_tensor_tensor(
                        stage3[:, oh * 128:(oh + 1) * 128], pbb(bank, 128), 1.0,
                        x1rr[:, oh * 128:(oh + 1) * 128], AL.mult, AL.add,
                        accum_out=sumc[:, oh:oh + 1]),
                        waits=[(tC, tm), (dX, d_x1)], inc=vC)
                v_s3 = vC.v
                sch.emit("vector", lambda e: e.tensor_reduce(
                    sumc[:, 16:17], sumc[:, 0:16], AX, AL.add), waits=[(vC, v_s3)], inc=vC)
                sch.emit("scalar", lambda e: e.activation(
                    x1rr[:, :], stage3[:, :], AF.Square, accum_out=sumc[:, 17:18]),
                    waits=[(vC, v_s3)], inc=sC)
                s_sq = sC.v
                sch.emit("vector", lambda e: e.tensor_scalar(
                    sumc[:, 18:19], sumc[:, 16:17], 1.0 / D, None, AL.mult), inc=vC)
                sch.emit("vector", lambda e: e.tensor_scalar(
                    sumc[:, 19:20], sumc[:, 17:18], 1.0 / D, None, AL.mult),
                    waits=[(sC, s_sq)], inc=vC)
                sch.emit("vector", lambda e: e.tensor_tensor(
                    sumc[:, 20:21], sumc[:, 18:19], sumc[:, 18:19], AL.mult), inc=vC)
                sch.emit("vector", lambda e: e.tensor_tensor(
                    sumc[:, 19:20], sumc[:, 19:20], sumc[:, 20:21], AL.subtract), inc=vC)
                v_var = vC.v
                sch.emit("scalar", lambda e: e.activation(
                    sumc[:, 20:21], sumc[:, 19:20], AF.Sqrt, bias=cst[:, 0:1]),
                    waits=[(vC, v_var)], inc=sC)
                s_std = sC.v
                sch.emit("vector", lambda e: e.reciprocal(sumc[:, 21:22], sumc[:, 20:21]),
                         waits=[(sC, s_std)], inc=vC)
                sch.emit("vector", lambda e: e.tensor_tensor(
                    sumc[:, 22:23], sumc[:, 18:19], sumc[:, 21:22], AL.mult), inc=vC)
                sch.emit("vector", lambda e: e.tensor_scalar(
                    sumc[:, 22:23], sumc[:, 22:23], -1.0, None, AL.mult), inc=vC)
                v_inv = vC.v
                sch.emit("scalar", lambda e: e.activation(
                    stage3[:, :], stage3[:, :], AF.Identity, scale=sumc[:, 21:22],
                    bias=sumc[:, 22:23]), waits=[(vC, v_inv)], inc=sC)
                s_out = sC.v
                sch.emit("sync", lambda e, tt=tt, r=r: e.dma_start(
                    out=yout[r, :, :].rearrange("(n p) d -> p n d", p=128)[:, tt, :],
                    in_=stage3[:, :]), waits=[(sC, s_out)], inc=dX, inc_n=16)
            barrier()

        dxw, dww, dhw = dX.v, dW.v, dH.v
        for eng in ENGS:
            def thunk(e, dxw=dxw, dww=dww, dhw=dhw):
                e.wait_ge(dX.sem, dxw)
                e.wait_ge(dW.sem, dww)
                e.wait_ge(dH.sem, dhw)
            sch.q[eng].append(thunk)

        with nc.Block() as block:
            @block.sync
            def _(e):
                for fn in sch.q["sync"]:
                    fn(e)

            @block.gpsimd
            def _(e):
                for fn in sch.q["gpsimd"]:
                    fn(e)

            @block.vector
            def _(e):
                for fn in sch.q["vector"]:
                    fn(e)

            @block.scalar
            def _(e):
                for fn in sch.q["scalar"]:
                    fn(e)

            @block.tensor
            def _(e):
                for fn in sch.q["tensor"]:
                    fn(e)

    return nc


_NC_CACHE = None


def kernel(L0_attn_x, L0_attn_W_in, L0_attn_conv_w, L0_attn_conv_b, L0_attn_W_x,
           L0_attn_A_log, L0_attn_dt_bias, L0_attn_out_W, L0_attn_D,
           L0_ffn_w_gate, L0_ffn_w_up, L0_ffn_w_down):
    global _NC_CACHE
    x = np.ascontiguousarray(np.asarray(L0_attn_x, dtype=np.float32))
    w_inT = np.ascontiguousarray(np.asarray(L0_attn_W_in, np.float32).T)
    conv_w = np.ascontiguousarray(
        np.asarray(L0_attn_conv_w, np.float32).reshape(NDI, 128, 4).transpose(1, 0, 2))
    conv_b = np.ascontiguousarray(
        np.asarray(L0_attn_conv_b, np.float32).reshape(NDI, 128).T)
    _wx = np.asarray(L0_attn_W_x, np.float32)
    _wxt = np.zeros((128, NDI, 80), np.float32)
    _wxd = _wx.T.reshape(NDI, 128, 33).transpose(1, 0, 2)
    _wxt[:, :, 0] = _wxd[:, :, 32]
    _wxt[:, :, 32:48] = _wxd[:, :, 0:16]
    _wxt[:, :, 64:80] = _wxd[:, :, 16:32]
    w_xT = np.ascontiguousarray(_wxt)
    a_log = np.ascontiguousarray(np.asarray(L0_attn_A_log, np.float32)[0:1, :])
    dt_b = np.ascontiguousarray(np.asarray(L0_attn_dt_bias, np.float32).reshape(NDI, 128).T)
    out_wT = np.ascontiguousarray(np.asarray(L0_attn_out_W, np.float32).T)
    d_vec = np.ascontiguousarray(np.asarray(L0_attn_D, np.float32).reshape(NDI, 128).T)
    w_gT = np.ascontiguousarray(np.asarray(L0_ffn_w_gate, np.float32).T)
    w_uT = np.ascontiguousarray(np.asarray(L0_ffn_w_up, np.float32).T)
    w_dT = np.ascontiguousarray(np.asarray(L0_ffn_w_down, np.float32).T)

    idn = np.eye(128, dtype=BF)
    idnf = np.eye(128, dtype=np.float32)
    tau = (np.arange(2)[None, :, None] * 128 + np.arange(128)[:, None, None])
    tcol = np.arange(L)[None, None, :]
    gmask = np.ascontiguousarray(
        np.broadcast_to((tcol >= tau), (128, 2, L)).astype(np.float32))

    if _NC_CACHE is None:
        _NC_CACHE = build_program()
    nc = _NC_CACHE

    in_maps = []
    for c in range(NCORE):
        chm = np.zeros((128, NCORE), np.float32)
        chm[:, :c] = 1.0
        im = {
            "w_inT": w_inT, "w_xT": w_xT, "out_wT": out_wT,
            "w_gT": w_gT, "w_uT": w_uT, "w_dT": w_dT,
            "conv_w": conv_w, "conv_b": conv_b, "d_vec": d_vec,
            "dt_bias": dt_b, "a_row": a_log,
            "idn": idn, "idnf": idnf, "gmask": gmask, "chm": chm,
            "wsl": np.ascontiguousarray(np.stack([
                w_gT.reshape(8, -1)[c],
                w_uT.reshape(8, -1)[c],
                w_dT.reshape(8, -1)[c],
            ]).reshape(3, 128, 16384).transpose(1, 0, 2)),
        }
        for r in range(2):
            t0 = c * TC
            im[f"xm{r}"] = np.ascontiguousarray(x[r, t0:t0 + TC, :])
            if c == 0:
                im[f"xh{r}"] = np.zeros((3, D), np.float32)
            else:
                im[f"xh{r}"] = np.ascontiguousarray(x[r, t0 - 3:t0, :])
        in_maps.append(im)

    res = run_bass_kernel_spmd(nc, in_maps, core_ids=list(range(NCORE)))
    out = np.empty((2, 8192, D), np.float32)
    for c in range(NCORE):
        yc = np.asarray(res.results[c]["y"]).reshape(2, TC, D)
        for r in range(2):
            out[r, c * TC:(c + 1) * TC, :] = yc[r]
    return out


# version 03
# speedup vs baseline: 1.0452x; 1.0452x over previous
import contextlib
import numpy as np
import ml_dtypes
import concourse.bass as bass
import concourse.mybir as mybir
from concourse.bass_utils import run_bass_kernel_spmd

F32 = mybir.dt.float32
BF16 = mybir.dt.bfloat16
U8 = mybir.dt.uint8
BF = ml_dtypes.bfloat16
AL = mybir.AluOpType
AF = mybir.ActivationFunctionType
AX = mybir.AxisListType.X

NCORE = 8
TC = 1024
L = 256
NCH = TC // L
D, DI, DFF, S = 2048, 4096, 8192, 16
ND, NDI, NFF = D // 128, DI // 128, DFF // 128
MAGIC = float(1.5 * 2 ** 23)
EPS = 1e-5

OFF_A = 0
OFF_B = 33024
OFF_C = 98560
OFF_D = 164096
ARENA = 200960

ENGS = ("sync", "gpsimd", "vector", "scalar", "tensor")
DEBUG_TAPS = False


class Ctr:
    def __init__(self, sem):
        self.sem = sem
        self.v = 0


class Sched:
    def __init__(self):
        self.q = {e: [] for e in ENGS}

    def emit(self, eng, fn, waits=(), inc=None, inc_n=1, selfwait=False):
        if inc is not None and eng in ("vector", "scalar", "gpsimd"):
            selfwait = True
        if inc is not None and inc_n == 16:
            selfwait = True
        wl = [(c.sem, v) for (c, v) in waits]
        if selfwait and inc is not None and inc_n == 1:
            wl.append((inc.sem, inc.v))
        if inc is not None:
            inc.v += inc_n
        tgt = (inc.sem, inc_n, inc.v) if inc is not None else None

        def thunk(e, fn=fn, wl=wl, tgt=tgt, selfwait=selfwait):
            for sem_, v in wl:
                e.wait_ge(sem_, v)
            inst = fn(e)
            if tgt is not None:
                inst.then_inc(tgt[0], tgt[1])
                if selfwait:
                    e.wait_ge(tgt[0], tgt[2])

        self.q[eng].append(thunk)


def build_program():
    nc = bass.Bass("TRN2", target_bir_lowering=False)

    def din(name, shape, dtype=F32):
        return nc.dram_tensor(name, shape, dtype, kind="ExternalInput").ap()

    xm = [din("xm0", [TC, D]), din("xm1", [TC, D])]
    xh = [din("xh0", [3, D]), din("xh1", [3, D])]
    w_inT = din("w_inT", [D, DFF])
    w_xT = din("w_xT", [128, NDI, 80])
    out_wT = din("out_wT", [DI, D])
    w_gT = din("w_gT", [D, DFF])
    w_uT = din("w_uT", [D, DFF])
    w_dT = din("w_dT", [DFF, D])
    conv_w = din("conv_w", [128, NDI, 4])
    conv_b = din("conv_b", [128, NDI])
    d_vec = din("d_vec", [128, NDI])
    dt_bias = din("dt_bias", [128, NDI])
    a_row = din("a_row", [1, S])
    wsl = din("wsl", [128, 3, 16384])
    idn = din("idn", [128, 128], BF16)
    idnf = din("idnf", [128, 128], F32)
    gmask = din("gmask", [128, 2, L])
    chm = din("chm", [128, NCORE])
    yout = nc.dram_tensor("y", [2, TC, D], F32, kind="ExternalOutput").ap()

    gin_d = nc.dram_tensor("gin_d", [128, 513], F32).ap()
    gout_d = nc.dram_tensor("gout_d", [128 * NCORE, 513], F32).ap()
    ws_in_d = nc.dram_tensor("ws_in_d", [128, 4], F32).ap()
    ws_out_d = nc.dram_tensor("ws_out_d", [128, 4], F32).ap()
    row_d = nc.dram_tensor("row_d", [4, TC], F32).ap()
    col_d = nc.dram_tensor("col_d", [16, 8], F32).ap()
    x1_d = nc.dram_tensor("x1_d", [8, 128, D], F32).ap()
    scr_g = nc.dram_tensor("scr_g", [D, DFF], BF16).ap()
    scr_u = nc.dram_tensor("scr_u", [D, DFF], BF16).ap()
    scr_d = nc.dram_tensor("scr_d", [DFF, D], BF16).ap()

    taps = {}
    if DEBUG_TAPS:
        taps = {
            "t_xf": nc.dram_tensor("t_xf", [128, ND, 3 + TC], BF16, kind="ExternalOutput").ap(),
            "t_xcv": nc.dram_tensor("t_xcv", [128, NDI, TC], BF16, kind="ExternalOutput").ap(),
            "t_pB": nc.dram_tensor("t_pB", [16, TC], F32, kind="ExternalOutput").ap(),
            "t_pC": nc.dram_tensor("t_pC", [16, TC], F32, kind="ExternalOutput").ap(),
            "t_dt": nc.dram_tensor("t_dt", [1, TC], F32, kind="ExternalOutput").ap(),
            "t_ug": nc.dram_tensor("t_ug", [1, TC], F32, kind="ExternalOutput").ap(),
            "t_cf": nc.dram_tensor("t_cf", [16, NCH, L], BF16, kind="ExternalOutput").ap(),
            "t_bdtr": nc.dram_tensor("t_bdtr", [16, NCH, L], BF16, kind="ExternalOutput").ap(),
            "t_cbar": nc.dram_tensor("t_cbar", [16, NCH, L], BF16, kind="ExternalOutput").ap(),
            "t_hpack": nc.dram_tensor("t_hpack", [128, NCH, 512], F32, kind="ExternalOutput").ap(),
            "t_hloc": nc.dram_tensor("t_hloc", [128, 512], F32, kind="ExternalOutput").ap(),
            "t_hrun": nc.dram_tensor("t_hrun", [128, 512], F32, kind="ExternalOutput").ap(),
            "t_ymix": nc.dram_tensor("t_ymix", [128, NDI, TC], BF16, kind="ExternalOutput").ap(),
            "t_g": nc.dram_tensor("t_g", [128, NDI, TC], BF16, kind="ExternalOutput").ap(),
            "t_attn": nc.dram_tensor("t_attn", [128, ND, TC], F32, kind="ExternalOutput").ap(),
            "t_xq": nc.dram_tensor("t_xq", [128, ND, TC], BF16, kind="ExternalOutput").ap(),
            "t_g2": nc.dram_tensor("t_g2", [128, NFF, TC], BF16, kind="ExternalOutput").ap(),
            "t_g2r": nc.dram_tensor("t_g2r", [128, NFF, TC], BF16, kind="ExternalOutput").ap(),
            "t_rws": nc.dram_tensor("t_rws", [128, 3], F32, kind="ExternalOutput").ap(),
            "t_r128": nc.dram_tensor("t_r128", [128, TC], F32, kind="ExternalOutput").ap(),
            "t_wq": nc.dram_tensor("t_wq", [128, 2, 16, 128], BF16, kind="ExternalOutput").ap(),
        }

    sch = Sched()

    with contextlib.ExitStack() as top:
        sem = {k: Ctr(top.enter_context(nc.semaphore(k)))
               for k in ["dX", "dW", "dH", "tC", "vC", "sC", "cc", "bar"]}
        dX, dW, dH, tC, vC, sC, cc, bar = (sem[k] for k in
                                           ["dX", "dW", "dH", "tC", "vC", "sC", "cc", "bar"])

        pb = [top.enter_context(nc.psum_tensor(f"pb{i}", [128, 512], F32)) for i in range(8)]

        def pbf(i, n=512):
            return pb[i][:, 0:n]

        def pbb(i, n=1024):
            return pb[i][:, :].bitcast(BF16)[:, 0:n]

        def barrier():
            bar.v += 5
            tgt = bar.v
            dxw, dww, dhw, ccw = dX.v, dW.v, dH.v, cc.v
            for eng in ENGS:
                def thunk(e, dxw=dxw, dww=dww, dhw=dhw, ccw=ccw, tgt=tgt):
                    e.wait_ge(dX.sem, dxw)
                    e.wait_ge(dW.sem, dww)
                    e.wait_ge(dH.sem, dhw)
                    e.wait_ge(cc.sem, ccw)
                    e.nop().then_inc(bar.sem, 1)
                    e.wait_ge(bar.sem, tgt)
                sch.q[eng].append(thunk)

        arena = top.enter_context(nc.sbuf_tensor("arena", [128, ARENA], U8))

        def V(off, shape, dtype):
            nb = int(np.prod(shape[1:])) * mybir.dt.size(dtype)
            ap = arena[0:shape[0], off:off + nb].bitcast(dtype)
            if len(shape) == 2:
                return ap
            if len(shape) == 3:
                return ap.rearrange("p (a b) -> p a b", a=shape[1], b=shape[2])
            if len(shape) == 4:
                return ap.rearrange("p (a b c) -> p a b c", a=shape[1], b=shape[2], c=shape[3])
            raise ValueError(shape)

        def sb(name, shape, dtype):
            return top.enter_context(nc.sbuf_tensor(name, shape, dtype))

        idn_s = sb("idn_s", [128, 128], BF16)
        idnf_s = sb("idnf_s", [128, 128], F32)
        gm_s = sb("gm_s", [128, 2, L], F32)
        chm_s = sb("chm_s", [128, NCORE], F32)
        cw_s = sb("cw_s", [128, NDI, 4], F32)
        cb_s = sb("cb_s", [128, NDI], F32)
        dv_s = sb("dv_s", [128, NDI], F32)
        wx_s = sb("wx_s", [128, NDI, 80], BF16)
        a16 = sb("a16", [16, 4], F32)
        a128 = sb("a128", [128, 1], F32)
        mb_s = sb("mb_s", [1, 8], F32)
        ones_s = sb("ones_s", [128, 8], F32)
        rws_s = sb("rws_s", [128, 4], F32)
        wsb_s = sb("wsb_s", [128, 4], F32)
        wsum_s = sb("wsum_s", [128, 4], F32)
        sumc = sb("sumc", [128, 24], F32)
        rst = sb("rst", [8, 128], F32)
        zcol = sb("zcol", [128, 1], F32)
        e16b = sb("e16b", [16, L], BF16)
        btT = sb("btT", [128, NCH, 2, 16], BF16)
        gsb = sb("gsb", [128, 2, L], BF16)
        ec16 = sb("ec16", [16, NCH], F32)
        ec128 = sb("ec128", [128, NCH], F32)
        sc16 = sb("sc16", [16, 8], F32)
        ujb = sb("ujb", [128, 1], F32)
        ej_s = sb("ej_s", [128, 1], F32)
        amx = sb("amx", [128, 16], F32)
        scol8 = sb("scol8", [128, 8], F32)
        arow_s = sb("arow_s", [1, S], F32)
        cst = sb("cst", [128, 4], F32)

        xf = V(OFF_A, [128, ND, 3 + TC], BF16)
        xcv = V(OFF_B, [128, NDI, TC], BF16)
        wbA = V(OFF_C, [128, 2, 16, 256], BF16)
        xss = V(OFF_C + 16384, [128, 2, 1056], F32)
        cvt = V(OFF_C + 24832, [128, 2, TC], F32)
        xcT = V(OFF_C, [128, 2, DI], BF16)
        hpack = V(OFF_C + 16384, [128, NCH, 512], F32)
        ug16 = V(OFF_C + 24576, [16, TC], F32)
        dt16 = V(OFF_C + 28672, [16, TC], F32)
        bdt = V(OFF_C + 32768, [16, NCH, L], F32)
        pB_s = V(OFF_D + 18432, [16, TC], F32)
        pC_s = V(OFF_D + 22528, [16, TC], F32)
        cf_s = V(OFF_C + 40960, [16, NCH, L], BF16)
        bdtr = V(OFF_C + 43008, [16, NCH, L], BF16)
        cbar = V(OFF_C + 45056, [16, NCH, L], BF16)
        h0b = V(OFF_C + 47104, [16, DI], BF16)
        hloc = V(OFF_C + 55296, [128, 512], F32)
        hrun = V(OFF_C + 57344, [128, 512], F32)
        htmp = V(OFF_C + 59392, [128, 512], F32)
        gsl = V(OFF_C + 61440, [128, 512], F32)
        xtokb = V(OFF_D, [128, 8, D], BF16)
        xhs = V(OFF_D + 32768, [3, D], BF16)
        dt_s = V(OFF_D, [1, TC], F32)
        ug_s = V(OFF_D + 4096, [1, TC], F32)
        onerow = V(OFF_D + 8192, [1, TC], F32)
        e16a = V(OFF_D + 12288, [16, L], F32)
        e16f = V(OFF_D + 13312, [16, L], F32)
        hst = V(OFF_D + 14336, [16, 512], F32)
        zsil = V(OFF_D + 16384, [128, 2, 512], BF16)
        wst = V(OFF_B, [128, 2, 4096], F32)
        wxf = V(OFF_B + 32768, [128, NDI, 80], F32)
        dtb = V(OFF_B + 43008, [128, NDI], F32)
        xq_f = V(OFF_A, [128, ND, TC], BF16)
        attn_f = V(OFF_C, [128, ND, TC], F32)
        wbO = V(OFF_D, [128, 2, NDI, 128], BF16)
        stage2 = V(OFF_D + 16384, [128, D], F32)
        x1t = V(OFF_D + 24576, [128, D], F32)
        xqt = V(OFF_D + 32768, [128, D], BF16)
        g2 = V(OFF_B, [128, NFF, TC], BF16)
        wbG = V(OFF_D, [128, 2, 16, 128], BF16)
        wbU = V(OFF_D + 8192, [128, 2, 16, 128], BF16)
        wfst = V(OFF_D + 16384, [128, 8, 128], F32)
        qtmp = V(OFF_D + 20480, [128, TC], F32)
        sil2 = V(OFF_D + 24576, [128, 2, 512], BF16)
        mxp = V(OFF_D + 26624, [128, TC], BF16)
        sg128 = V(OFF_D + 28672, [128, TC], F32)
        r128 = V(OFF_D + 28672, [128, TC], F32)
        su128 = V(OFF_D + 32768, [128, TC], F32)
        dnf = V(OFF_A, [128, ND, TC], BF16)
        wbD = V(OFF_D, [128, 2, 32, 128], BF16)
        wfst2 = V(OFF_D + 16384, [128, 8, 128], F32)
        qtmp2 = V(OFF_D + 20480, [128, TC], F32)
        s2b = V(OFF_D + 24576, [128, TC], F32)
        stage3 = V(OFF_B, [128, D], F32)
        x1rr = V(OFF_B + 8192, [128, D], F32)

        for dst, src in [(idn_s[:, :], idn[:, :]), (idnf_s[:, :], idnf[:, :]),
                         (gm_s[:, :, :], gmask[:, :, :]), (chm_s[:, :], chm[:, :]),
                         (arow_s[:, :], a_row[:, :]),
                         (cw_s[:, :, :], conv_w[:, :, :]),
                         (cb_s[:, :], conv_b[:, :]),
                         (dv_s[:, :], d_vec[:, :]),
                         (dtb[:, :], dt_bias[:, :]),
                         (wxf[:, :, :], w_xT[:, :, :])]:
            sch.emit("sync", lambda e, dst=dst, src=src: e.dma_start(out=dst, in_=src),
                     inc=dX, inc_n=16)
        pre_dX = dX.v

        sch.emit("vector", lambda e: e.memset(ones_s[:, :], 1.0), inc=vC, selfwait=True)
        sch.emit("vector", lambda e: e.memset(cst[:, 0:1], EPS), inc=vC, selfwait=True)
        sch.emit("vector", lambda e: e.memset(cst[:, 1:2], 1.0), inc=vC, selfwait=True)
        sch.emit("vector", lambda e: e.memset(cst[:, 2:3], MAGIC), inc=vC, selfwait=True)
        sch.emit("vector", lambda e: e.memset(zcol[:, :], 0.0), inc=vC, selfwait=True)
        sch.emit("vector", lambda e: e.tensor_scalar(wx_s[:, :, :], wxf[:, :, :], 0.0, None, AL.add),
                 waits=[(dX, pre_dX)], inc=vC)
        sch.emit("vector", lambda e: e.tensor_reduce(wsum_s[:, 3:4], dtb[:, :], AX, AL.add),
                 waits=[(dX, pre_dX)], inc=vC, selfwait=True)
        v0 = vC.v
        sch.emit("tensor", lambda e: e.matmul(pb[6][0:1, 0:1], ones_s[:, 0:1],
                                              wsum_s[:, 3:4], start=True, stop=True),
                 waits=[(vC, v0)], inc=tC)
        t0 = tC.v
        sch.emit("vector", lambda e: e.tensor_scalar(mb_s[0:1, 0:1], pb[6][0:1, 0:1],
                                                     1.0 / DI, None, AL.mult),
                 waits=[(tC, t0)], inc=vC)
        sch.emit("scalar", lambda e: e.activation(arow_s[0:1, :], arow_s[0:1, :], AF.Exp),
                 waits=[(dX, pre_dX)], inc=sC)
        s0 = sC.v
        sch.emit("tensor", lambda e: e.transpose(pb[7][0:16, 0:1], arow_s[0:1, 0:16],
                                                 idnf_s[0:1, 0:1]),
                 waits=[(sC, s0)], inc=tC)
        t1 = tC.v
        sch.emit("vector", lambda e: e.tensor_scalar(a16[0:16, 0:1], pb[7][0:16, 0:1],
                                                     -1.0, None, AL.mult),
                 waits=[(tC, t1)], inc=vC)
        sch.emit("vector", lambda e: e.tensor_scalar(a16[0:16, 1:2], pb[7][0:16, 0:1], 0.0, None, AL.add), inc=vC)
        v1 = vC.v
        sch.emit("sync", lambda e: e.dma_start(out=col_d[:, 0:1], in_=a16[0:16, 0:1], allow_slow_non_contiguous=True),
                 waits=[(vC, v1)], inc=dX, inc_n=16)
        d0 = dX.v
        for g in range(8):
            sch.emit("sync", lambda e, g=g: e.dma_start(out=a128[16 * g:16 * (g + 1), 0:1],
                                                        in_=col_d[:, 0:1], allow_slow_non_contiguous=True),
                     waits=[(dX, d0)], inc=dX, inc_n=16)
        sch.emit("vector", lambda e: e.memset(wsum_s[:, 0:3], 0.0), inc=vC, selfwait=True)
        for w3 in range(3):
            for ch in range(4):
                p2 = ch % 2
                vprev = vC.v
                sch.emit("sync", lambda e, w3=w3, ch=ch, p2=p2: e.dma_start(
                    out=wst[:, p2, :], in_=wsl[:, w3, 4096 * ch:4096 * (ch + 1)]),
                    waits=[(vC, max(0, vprev - 1))], inc=dX, inc_n=16)
                dm = dX.v
                sch.emit("vector", lambda e, p2=p2: e.tensor_reduce(
                    wsum_s[:, 3:4], wst[:, p2, :], AX, AL.add, apply_absolute_value=True),
                    waits=[(dX, dm)], inc=vC, selfwait=True)
                sch.emit("vector", lambda e, w3=w3: e.tensor_tensor(
                    wsum_s[:, w3:w3 + 1], wsum_s[:, w3:w3 + 1], wsum_s[:, 3:4], AL.add),
                    inc=vC)
        v2 = vC.v
        sch.emit("sync", lambda e: e.dma_start(out=ws_in_d[:, 0:4], in_=wsum_s[:, 0:4]),
                 waits=[(vC, v2)], inc=dX, inc_n=16)
        d1 = dX.v
        sch.emit("gpsimd", lambda e: e.collective_compute(
            "AllReduce", AL.add, replica_groups=[list(range(NCORE))],
            ins=[ws_in_d[:, :]], outs=[ws_out_d[:, :]]),
            waits=[(dX, d1)], inc=cc)
        cc0 = cc.v
        sch.emit("sync", lambda e: e.dma_start(out=wsum_s[:, 0:4], in_=ws_out_d[:, :]),
                 waits=[(cc, cc0)], inc=dX, inc_n=16)
        d2 = dX.v
        sch.emit("tensor", lambda e: e.matmul(pb[6][0:1, 0:3], ones_s[:, 0:1],
                                              wsum_s[:, 0:3], start=True, stop=True),
                 waits=[(dX, d2)], inc=tC)
        t2 = tC.v
        sch.emit("vector", lambda e: e.tensor_scalar(wsum_s[0:1, 0:3], pb[6][0:1, 0:3],
                                                     1.0 / 16777216.0, EPS, AL.mult, AL.add),
                 waits=[(tC, t2)], inc=vC)
        v3 = vC.v
        sch.emit("sync", lambda e: e.dma_start(out=ws_out_d[0:1, 0:3], in_=wsum_s[0:1, 0:3]),
                 waits=[(vC, v3)], inc=dX, inc_n=16)
        d3 = dX.v
        sch.emit("sync", lambda e: e.dma_start(out=rws_s[:, 0:3],
                                               in_=ws_out_d[0:1, 0:3].broadcast_to([128, 3])),
                 waits=[(dX, d3)], inc=dX, inc_n=16)
        d4 = dX.v
        sch.emit("vector", lambda e: e.tensor_scalar(wsb_s[:, 0:3], rws_s[:, 0:3], 0.0, None, AL.add),
                 waits=[(dX, d4)], inc=vC)
        sch.emit("vector", lambda e: e.reciprocal(rws_s[:, 0:3], rws_s[:, 0:3]), inc=vC)
        barrier()

        def tap(name, src, waits):
            if name in taps:
                sch.emit("sync", lambda e, name=name, src=src: e.dma_start(
                    out=taps[name][...], in_=src), waits=waits, inc=dX, inc_n=16)
                barrier()

        if "t_rws" in taps:
            sch.emit("sync", lambda e: e.dma_start(out=taps["t_rws"][...],
                                                   in_=rws_s[:, 0:3]),
                     waits=[(vC, vC.v)], inc=dX, inc_n=16)

        grp_t = {}
        for r in range(2):
            sch.emit("gpsimd", lambda e, r=r: e.dma_start(
                out=xtokb[:, :, :], in_=xm[r].rearrange("(n p) d -> p n d", p=128)),
                inc=dW, inc_n=16)
            sch.emit("gpsimd", lambda e, r=r: e.dma_start(out=xhs[:, :], in_=xh[r][:, :]),
                     inc=dW, inc_n=16)
            d_x0 = dW.v
            nblk = 0
            for dh in range(ND):
                for tt in range(9):
                    bank = 6 + (nblk % 2)
                    vprev = vC.v
                    if tt < 8:
                        sch.emit("tensor", lambda e, dh=dh, tt=tt, bank=bank: e.transpose(
                            pbb(bank, 128), xtokb[:, tt, dh * 128:(dh + 1) * 128],
                            idn_s[:, :]),
                            waits=[(dW, d_x0), (vC, max(0, vprev - 1))], inc=tC)
                        tm = tC.v
                        sch.emit("vector", lambda e, dh=dh, tt=tt, bank=bank: e.tensor_scalar(
                            xf[:, dh, 3 + tt * 128: 3 + (tt + 1) * 128], pbb(bank, 128), 0.0, None, AL.add),
                            waits=[(tC, tm)], inc=vC)
                    else:
                        sch.emit("tensor", lambda e, dh=dh, bank=bank: e.transpose(
                            pbb(bank, 3), xhs[0:3, dh * 128:(dh + 1) * 128],
                            idn_s[0:3, 0:3]),
                            waits=[(dW, d_x0), (vC, max(0, vprev - 1))], inc=tC)
                        tm = tC.v
                        sch.emit("vector", lambda e, dh=dh, bank=bank: e.tensor_scalar(
                            xf[:, dh, 0:3], pbb(bank, 3), 0.0, None, AL.add), waits=[(tC, tm)], inc=vC)
                    nblk += 1
            v_xf = vC.v
            if r == 0:
                tap("t_xf", xf[:, :, :], [(vC, v_xf)])

            evm = {}
            smark = {}
            for g in range(16):
                sch.emit("gpsimd", lambda e, g=g: e.dma_start(
                    out=wbA[:, g % 2, :, :],
                    in_=w_inT[:, 256 * g:256 * (g + 1)].rearrange("(kh p) n -> p kh n", p=128)),
                    waits=[(tC, grp_t.get(("a", r, g - 2), 0))], inc=dW, inc_n=16)
                d_g = dW.v
                for nl in range(2):
                    n_hat = g * 2 + nl
                    pA, pB, pC = (0, 1, 4) if n_hat % 2 == 0 else (2, 3, 5)
                    for kh in range(16):
                        w0 = [(dW, d_g),
                              (vC, max(evm.get(n_hat - 2, 0), v_xf))] if kh == 0 else []
                        sch.emit("tensor", lambda e, g=g, nl=nl, kh=kh, pA=pA: e.matmul(
                            pbf(pA), wbA[:, g % 2, kh, nl * 128:(nl + 1) * 128],
                            xf[:, kh, 0:512], start=(kh == 0), stop=(kh == 15),
                            skip_group_check=True), waits=w0)
                        sch.emit("tensor", lambda e, g=g, nl=nl, kh=kh, pB=pB: e.matmul(
                            pbf(pB), wbA[:, g % 2, kh, nl * 128:(nl + 1) * 128],
                            xf[:, kh, 512:1024], start=(kh == 0), stop=(kh == 15),
                            skip_group_check=True))
                        sch.emit("tensor", lambda e, g=g, nl=nl, kh=kh, pC=pC: e.matmul(
                            pbf(pC, 3), wbA[:, g % 2, kh, nl * 128:(nl + 1) * 128],
                            xf[:, kh, 1024:1027], start=(kh == 0), stop=(kh == 15),
                            skip_group_check=True),
                            inc=(tC if kh == 15 else None))
                    grp_t[("a", r, g)] = tC.v
                    t_mm = tC.v
                    sp = n_hat % 2
                    sch.emit("vector", lambda e, sp=sp, pA=pA: e.tensor_scalar(
                        xss[:, sp, 0:512], pbf(pA), 0.0, None, AL.add), waits=[(tC, t_mm)], inc=vC)
                    sch.emit("vector", lambda e, sp=sp, pB=pB: e.tensor_scalar(
                        xss[:, sp, 512:1024], pbf(pB), 0.0, None, AL.add), inc=vC)
                    sch.emit("vector", lambda e, sp=sp, pC=pC: e.tensor_scalar(
                        xss[:, sp, 1024:1027], pbf(pC, 3), 0.0, None, AL.add), inc=vC)
                    evm[n_hat] = vC.v
                    sch.emit("vector", lambda e, sp=sp, n_hat=n_hat: e.tensor_scalar(
                        cvt[:, sp, :], xss[:, sp, 0:TC], cw_s[:, n_hat, 0:1],
                        cb_s[:, n_hat:n_hat + 1], AL.mult, AL.add),
                        waits=[(sC, smark.get(n_hat - 2, 0))], inc=vC)
                    for k in range(1, 4):
                        sch.emit("vector", lambda e, sp=sp, n_hat=n_hat, k=k:
                                 e.scalar_tensor_tensor(
                                     cvt[:, sp, :], xss[:, sp, k:TC + k],
                                     cw_s[:, n_hat, k:k + 1], cvt[:, sp, :],
                                     AL.mult, AL.add), inc=vC)
                    v_cv = vC.v
                    sch.emit("scalar", lambda e, sp=sp, n_hat=n_hat: e.activation(
                        xcv[:, n_hat, :], cvt[:, sp, :], AF.Silu),
                        waits=[(vC, v_cv)], inc=sC)
                    smark[n_hat] = sC.v
            s_xcv = sC.v
            if r == 0:
                tap("t_xcv", xcv[:, :, :], [(sC, s_xcv)])
            barrier()

            sch.emit("vector", lambda e: e.memset(onerow[:, :], 1.0), inc=vC, selfwait=True)
            for nh in range(NDI):
                sch.emit("tensor", lambda e, nh=nh: e.matmul(
                    pb[6][0:80, :], wx_s[:, nh, :], xcv[:, nh, 0:512],
                    start=(nh == 0), stop=(nh == NDI - 1), skip_group_check=True))
                sch.emit("tensor", lambda e, nh=nh: e.matmul(
                    pb[7][0:80, :], wx_s[:, nh, :], xcv[:, nh, 512:1024],
                    start=(nh == 0), stop=(nh == NDI - 1), skip_group_check=True),
                    inc=(tC if nh == NDI - 1 else None))
            t_p = tC.v
            sch.emit("vector", lambda e: e.tensor_scalar(pB_s[:, 0:512], pb[6][32:48, :], 0.0, None, AL.add),
                     waits=[(tC, t_p)], inc=vC)
            sch.emit("vector", lambda e: e.tensor_scalar(pB_s[:, 512:1024], pb[7][32:48, :], 0.0, None, AL.add),
                     inc=vC)
            sch.emit("vector", lambda e: e.tensor_scalar(pC_s[:, 0:512], pb[6][64:80, :], 0.0, None, AL.add),
                     inc=vC)
            sch.emit("vector", lambda e: e.tensor_scalar(pC_s[:, 512:1024], pb[7][64:80, :], 0.0, None, AL.add),
                     inc=vC)
            v_p = vC.v
            sch.emit("scalar", lambda e: e.activation(dt_s[0:1, 0:512], pb[6][0:1, :],
                                                      AF.Exp, bias=mb_s[0:1, 0:1]),
                     waits=[(vC, v_p)], inc=sC)
            sch.emit("scalar", lambda e: e.activation(dt_s[0:1, 512:1024], pb[7][0:1, :],
                                                      AF.Exp, bias=mb_s[0:1, 0:1]),
                     inc=sC)
            sch.emit("scalar", lambda e: e.activation(dt_s[0:1, :], dt_s[0:1, :], AF.Ln,
                                                      bias=cst[0:1, 1:2]), inc=sC)
            s_dt = sC.v
            sch.emit("vector", lambda e: e.tensor_tensor_scan(
                ug_s[0:1, :], onerow[0:1, :], dt_s[0:1, :], 0.0, AL.mult, AL.add),
                waits=[(sC, s_dt)], inc=vC)
            v_ug = vC.v
            sch.emit("sync", lambda e: e.dma_start(out=row_d[0:1, :], in_=ug_s[0:1, :]),
                     waits=[(vC, v_ug)], inc=dX, inc_n=16)
            sch.emit("sync", lambda e: e.dma_start(out=row_d[1:2, :], in_=dt_s[0:1, :]),
                     inc=dX, inc_n=16)
            d_row = dX.v
            sch.emit("sync", lambda e: e.dma_start(
                out=ug16[:, :], in_=row_d[0:1, :].broadcast_to([16, TC])),
                waits=[(dX, d_row)], inc=dX, inc_n=16)
            sch.emit("sync", lambda e: e.dma_start(
                out=dt16[:, :], in_=row_d[1:2, :].broadcast_to([16, TC])),
                inc=dX, inc_n=16)
            d_b16 = dX.v

            for c in range(NCH):
                lo, hi = c * L, (c + 1) * L
                if c == 0:
                    sch.emit("vector", lambda e: e.memset(sc16[0:16, 0:1], 0.0),
                             waits=[(dX, d_b16)], inc=vC, selfwait=True)
                else:
                    sch.emit("vector", lambda e, lo=lo: e.tensor_scalar(
                        sc16[0:16, 0:1], ug16[:, lo - 1:lo], 0.0, None, AL.add),
                        waits=[(dX, d_b16)], inc=vC)
                sch.emit("vector", lambda e, hi=hi: e.tensor_scalar(
                    sc16[0:16, 1:2], ug16[:, hi - 1:hi], 0.0, None, AL.add), inc=vC)
                sch.emit("vector", lambda e: e.tensor_tensor(
                    sc16[0:16, 2:3], sc16[0:16, 0:1], sc16[0:16, 1:2], AL.add), inc=vC)
                sch.emit("vector", lambda e: e.tensor_scalar(
                    sc16[0:16, 2:3], sc16[0:16, 2:3], 0.5, None, AL.mult), inc=vC)
                sch.emit("vector", lambda e, c=c, lo=lo, hi=hi: e.tensor_tensor(
                    bdt[:, c, :], pB_s[:, lo:hi], dt16[:, lo:hi], AL.mult), inc=vC)
                sch.emit("vector", lambda e, lo=lo, hi=hi: e.tensor_scalar(
                    e16a[:, :], ug16[:, lo:hi], sc16[0:16, 2:3], 5.3,
                    AL.subtract, AL.min), inc=vC)
                sch.emit("vector", lambda e: e.tensor_scalar(
                    e16a[:, :], e16a[:, :], -5.3, None, AL.max), inc=vC)
                v_td = vC.v
                sch.emit("scalar", lambda e: e.activation(e16f[:, :], e16a[:, :], AF.Exp,
                                                          scale=a16[0:16, 0:1]),
                         waits=[(vC, v_td)], inc=sC)
                s_F = sC.v
                sch.emit("vector", lambda e, c=c, lo=lo, hi=hi: e.tensor_tensor(
                    cf_s[:, c, :], pC_s[:, lo:hi], e16f[:, :], AL.mult),
                    waits=[(sC, s_F)], inc=vC)
                v_cf = vC.v
                sch.emit("scalar", lambda e: e.activation(e16f[:, :], e16a[:, :], AF.Exp,
                                                          scale=a16[0:16, 1:2]),
                         waits=[(vC, v_cf)], inc=sC)
                s_R = sC.v
                sch.emit("vector", lambda e, c=c: e.tensor_tensor(
                    bdtr[:, c, :], bdt[:, c, :], e16f[:, :], AL.mult),
                    waits=[(sC, s_R)], inc=vC)
                sch.emit("vector", lambda e, lo=lo, hi=hi: e.tensor_scalar(
                    e16a[:, :], ug16[:, lo:hi], sc16[0:16, 0:1], None, AL.subtract),
                    inc=vC)
                v_eb = vC.v
                sch.emit("scalar", lambda e: e.activation(e16f[:, :], e16a[:, :], AF.Exp,
                                                          scale=a16[0:16, 0:1]),
                         waits=[(vC, v_eb)], inc=sC)
                s_Eb = sC.v
                sch.emit("vector", lambda e, c=c, lo=lo, hi=hi: e.tensor_tensor(
                    cbar[:, c, :], pC_s[:, lo:hi], e16f[:, :], AL.mult),
                    waits=[(sC, s_Eb)], inc=vC)
                sch.emit("vector", lambda e, lo=lo, hi=hi: e.tensor_scalar(
                    e16a[:, :], ug16[:, lo:hi], sc16[0:16, 1:2], None, AL.subtract),
                    inc=vC)
                v_e2 = vC.v
                sch.emit("scalar", lambda e: e.activation(e16f[:, :], e16a[:, :], AF.Exp,
                                                          scale=a16[0:16, 1:2]),
                         waits=[(vC, v_e2)], inc=sC)
                s_E2 = sC.v
                sch.emit("vector", lambda e, c=c: e.tensor_tensor(
                    e16b[:, :], bdt[:, c, :], e16f[:, :], AL.mult),
                    waits=[(sC, s_E2)], inc=vC)
                v_bt = vC.v
                for th in range(2):
                    sch.emit("tensor", lambda e, th=th: e.transpose(
                        pbb(6 + th, 16), e16b[0:16, th * 128:(th + 1) * 128],
                        idn_s[0:16, 0:16]),
                        waits=[(vC, v_bt)], inc=tC)
                    t_bt = tC.v
                    sch.emit("vector", lambda e, c=c, th=th: e.tensor_scalar(
                        btT[:, c, th, :], pbb(6 + th, 16), 0.0, None, AL.add), waits=[(tC, t_bt)], inc=vC)
                sch.emit("vector", lambda e: e.tensor_tensor(
                    sc16[0:16, 3:4], sc16[0:16, 1:2], sc16[0:16, 0:1], AL.subtract),
                    inc=vC)
                v_ec = vC.v
                sch.emit("scalar", lambda e, c=c: e.activation(
                    ec16[0:16, c:c + 1], sc16[0:16, 3:4], AF.Exp, scale=a16[0:16, 0:1]),
                    waits=[(vC, v_ec)], inc=sC)
            s_prep = sC.v
            sch.emit("sync", lambda e: e.dma_start(out=col_d[:, 0:NCH], in_=ec16[0:16, :], allow_slow_non_contiguous=True),
                     waits=[(sC, s_prep)], inc=dX, inc_n=16)
            d_ec = dX.v
            for g in range(8):
                sch.emit("sync", lambda e, g=g: e.dma_start(
                    out=ec128[16 * g:16 * (g + 1), :], in_=col_d[:, 0:NCH],
                    allow_slow_non_contiguous=True),
                    waits=[(dX, d_ec)], inc=dX, inc_n=16)
            d_ec128 = dX.v

            if r == 0:
                tap("t_pB", pB_s[:, :], [(vC, vC.v)])
                tap("t_pC", pC_s[:, :], [(vC, vC.v)])
                tap("t_dt", dt_s[:, :], [(sC, sC.v)])
                tap("t_ug", ug_s[:, :], [(vC, vC.v)])
                tap("t_cf", cf_s[:, :, :], [(vC, vC.v)])
                tap("t_bdtr", bdtr[:, :, :], [(vC, vC.v)])
                tap("t_cbar", cbar[:, :, :], [(vC, vC.v)])

            def emit_xcT(c):
                lo = c * L
                for th in range(2):
                    for dh in range(NDI):
                        bank = 6 + ((th * NDI + dh) % 2)
                        vprev = vC.v
                        sch.emit("tensor", lambda e, th=th, dh=dh, bank=bank, lo=lo:
                                 e.transpose(
                                     pbb(bank, 128),
                                     xcv[:, dh, lo + th * 128: lo + (th + 1) * 128],
                                     idn_s[:, :]),
                                 waits=[(vC, max(0, vprev - 1))], inc=tC)
                        tm = tC.v
                        sch.emit("vector", lambda e, th=th, dh=dh, bank=bank: e.tensor_scalar(
                            xcT[:, th, dh * 128:(dh + 1) * 128], pbb(bank, 128), 0.0, None, AL.add),
                            waits=[(tC, tm)], inc=vC)
                return vC.v

            sch.emit("vector", lambda e: e.memset(hloc[:, :], 0.0), inc=vC, selfwait=True)
            hv = {}
            for c in range(NCH):
                v_xcT = emit_xcT(c)
                for dgr in range(8):
                    hb = 4 + dgr % 2
                    sch.emit("tensor", lambda e, c=c, dgr=dgr, hb=hb: e.matmul(
                        pb[hb][0:16, :], btT[:, c, 0, :],
                        xcT[:, 0, dgr * 512:(dgr + 1) * 512], start=True, stop=False,
                        skip_group_check=True),
                        waits=[(vC, max(v_xcT, hv.get((c, dgr - 2), 0)))])
                    sch.emit("tensor", lambda e, c=c, dgr=dgr, hb=hb: e.matmul(
                        pb[hb][0:16, :], btT[:, c, 1, :],
                        xcT[:, 1, dgr * 512:(dgr + 1) * 512], start=False, stop=True,
                        skip_group_check=True), inc=tC)
                    t_h = tC.v
                    dhp = dH.v
                    sch.emit("vector", lambda e, hb=hb: e.tensor_scalar(
                        hst[0:16, :], pb[hb][0:16, :], 0.0, None, AL.add),
                        waits=[(tC, t_h), (dH, dhp)], inc=vC)
                    hv[(c, dgr)] = vC.v
                    v_h = vC.v
                    sch.emit("sync", lambda e, c=c, dgr=dgr: e.dma_start(
                        out=hpack[16 * dgr:16 * (dgr + 1), c, :], in_=hst[0:16, :]),
                        waits=[(vC, v_h)], inc=dH, inc_n=16)
                d_hc = dH.v
                sch.emit("vector", lambda e, c=c: e.scalar_tensor_tensor(
                    hloc[:, :], hloc[:, :], ec128[:, c:c + 1], hpack[:, c, :],
                    AL.mult, AL.add), waits=[(dH, d_hc), (dX, d_ec128)], inc=vC)
            v_hloc = vC.v

            if r == 0:
                tap("t_hpack", hpack[:, :, :], [(vC, vC.v)])
                tap("t_hloc", hloc[:, :], [(vC, vC.v)])

            sch.emit("sync", lambda e: e.dma_start(out=gin_d[:, 0:512], in_=hloc[:, :]),
                     waits=[(vC, v_hloc)], inc=dX, inc_n=16)
            sch.emit("sync", lambda e: e.dma_start(out=gin_d[0:1, 512:513],
                                                   in_=ug_s[0:1, TC - 1:TC]),
                     inc=dX, inc_n=16)
            sch.emit("sync", lambda e: e.dma_start(out=gin_d[1:128, 512:513],
                                                   in_=zcol[0:127, :],
                                                   allow_slow_non_contiguous=True),
                     inc=dX, inc_n=16)
            d_gin = dX.v
            sch.emit("gpsimd", lambda e: e.collective_compute(
                "AllGather", AL.bypass, replica_groups=[list(range(NCORE))],
                ins=[gin_d[:, :]], outs=[gout_d[:, :]]),
                waits=[(dX, d_gin)], inc=cc)
            cc_g = cc.v
            sch.emit("vector", lambda e: e.memset(hrun[:, :], 0.0), inc=vC, selfwait=True)
            for j in range(NCORE - 1):
                vpr = vC.v
                sch.emit("sync", lambda e, j=j: e.dma_start(
                    out=gsl[:, :], in_=gout_d[128 * j:128 * (j + 1), 0:512]),
                    waits=[(cc, cc_g), (vC, vpr)], inc=dX, inc_n=16)
                sch.emit("sync", lambda e, j=j: e.dma_start(
                    out=ujb[:, :],
                    in_=gout_d[128 * j:128 * j + 1, 512:513].broadcast_to([128, 1])),
                    inc=dX, inc_n=16)
                d_j = dX.v
                sch.emit("vector", lambda e, j=j: e.tensor_tensor(
                    ujb[:, :], ujb[:, :], chm_s[:, j:j + 1], AL.mult),
                    waits=[(dX, d_j)], inc=vC)
                v_um = vC.v
                sch.emit("scalar", lambda e: e.activation(ej_s[:, :], ujb[:, :], AF.Exp,
                                                          scale=a128[:, 0:1]),
                         waits=[(vC, v_um)], inc=sC)
                s_ej = sC.v
                sch.emit("vector", lambda e: e.tensor_scalar(
                    htmp[:, :], hrun[:, :], ej_s[:, 0:1], None, AL.mult),
                    waits=[(sC, s_ej)], inc=vC)
                sch.emit("vector", lambda e, j=j: e.scalar_tensor_tensor(
                    hrun[:, :], gsl[:, :], chm_s[:, j:j + 1], htmp[:, :],
                    AL.mult, AL.add), inc=vC)

            if r == 0:
                tap("t_hrun", hrun[:, :], [(vC, vC.v)])

            yev = {}
            gv = {}
            for c in range(NCH):
                lo = c * L
                v_hr = vC.v
                t_pre = tC.v
                for g in range(8):
                    sch.emit("gpsimd", lambda e, g=g: e.dma_start(
                        out=h0b[0:16, 512 * g:512 * (g + 1)],
                        in_=hrun[16 * g:16 * (g + 1), :]),
                        waits=([(vC, v_hr), (tC, t_pre)] if g == 0 else []),
                        inc=dW, inc_n=16)
                d_h0w = dW.v
                v_xcT = emit_xcT(c)
                for th in range(2):
                    gb = 4 + th
                    sch.emit("tensor", lambda e, c=c, th=th, gb=gb: e.matmul(
                        pbf(gb, L), bdtr[0:16, c, th * 128:(th + 1) * 128],
                        cf_s[0:16, c, :], start=True, stop=True),
                        waits=[(vC, max(v_xcT, gv.get((c - 1, th), 0)))], inc=tC)
                    t_g = tC.v
                    sch.emit("vector", lambda e, th=th, gb=gb: e.tensor_tensor(
                        gsb[:, th, :], pbf(gb, L), gm_s[:, th, :], AL.mult),
                        waits=[(tC, t_g)], inc=vC)
                    gv[(c, th)] = vC.v
                v_g = vC.v
                for dh in range(NDI):
                    pban = dh % 2
                    sch.emit("tensor", lambda e, c=c, dh=dh, pban=pban: e.matmul(
                        pbf(pban, L), h0b[0:16, dh * 128:(dh + 1) * 128],
                        cbar[0:16, c, :], start=True, stop=False, skip_group_check=True),
                        waits=[(vC, v_g), (dW, d_h0w),
                               (vC, yev.get(dh - 2, 0))])
                    for th in range(2):
                        sch.emit("tensor", lambda e, c=c, dh=dh, th=th, pban=pban: e.matmul(
                            pbf(pban, L), xcT[:, th, dh * 128:(dh + 1) * 128],
                            gsb[:, th, :], start=False, stop=(th == 1),
                            skip_group_check=True),
                            inc=(tC if th == 1 else None))
                    t_y = tC.v
                    sch.emit("vector", lambda e, c=c, dh=dh, pban=pban, lo=lo:
                             e.scalar_tensor_tensor(
                                 xcv[:, dh, lo:lo + L], xcv[:, dh, lo:lo + L],
                                 dv_s[:, dh:dh + 1], pbf(pban, L), AL.mult, AL.add),
                             waits=[(tC, t_y)], inc=vC)
                    yev[dh] = vC.v
                sch.emit("vector", lambda e, c=c: e.scalar_tensor_tensor(
                    hrun[:, :], hrun[:, :], ec128[:, c:c + 1], hpack[:, c, :],
                    AL.mult, AL.add), waits=[(dW, d_h0w)], inc=vC)
            if r == 0:
                tap("t_ymix", xcv[:, :, :], [(vC, vC.v)])
            barrier()

            evz = {}
            for g in range(16):
                sch.emit("gpsimd", lambda e, g=g: e.dma_start(
                    out=wbA[:, g % 2, :, :],
                    in_=w_inT[:, DI + 256 * g:DI + 256 * (g + 1)].rearrange(
                        "(kh p) n -> p kh n", p=128)),
                    waits=[(tC, grp_t.get(("z", r, g - 2), 0))], inc=dW, inc_n=16)
                d_g = dW.v
                for nl in range(2):
                    n_hat = g * 2 + nl
                    pA, pB = (0, 1) if n_hat % 2 == 0 else (2, 3)
                    for kh in range(16):
                        w0 = [(dW, d_g), (sC, evz.get(n_hat - 2, 0))] if kh == 0 else []
                        sch.emit("tensor", lambda e, g=g, nl=nl, kh=kh, pA=pA: e.matmul(
                            pbf(pA), wbA[:, g % 2, kh, nl * 128:(nl + 1) * 128],
                            xf[:, kh, 3:515], start=(kh == 0), stop=(kh == 15),
                            skip_group_check=True), waits=w0)
                        sch.emit("tensor", lambda e, g=g, nl=nl, kh=kh, pB=pB: e.matmul(
                            pbf(pB), wbA[:, g % 2, kh, nl * 128:(nl + 1) * 128],
                            xf[:, kh, 515:1027], start=(kh == 0), stop=(kh == 15),
                            skip_group_check=True),
                            inc=(tC if kh == 15 else None))
                    grp_t[("z", r, g)] = tC.v
                    t_mm = tC.v
                    v_pr = vC.v
                    sch.emit("scalar", lambda e, pA=pA: e.activation(
                        zsil[:, 0, :], pbf(pA), AF.Silu),
                        waits=[(tC, t_mm), (vC, v_pr)], inc=sC)
                    sch.emit("scalar", lambda e, pB=pB: e.activation(
                        zsil[:, 1, :], pbf(pB), AF.Silu), inc=sC)
                    evz[n_hat] = sC.v
                    s_z = sC.v
                    sch.emit("vector", lambda e, n_hat=n_hat: e.tensor_tensor(
                        xcv[:, n_hat, 0:512], xcv[:, n_hat, 0:512], zsil[:, 0, :],
                        AL.mult), waits=[(sC, s_z)], inc=vC)
                    sch.emit("vector", lambda e, n_hat=n_hat: e.tensor_tensor(
                        xcv[:, n_hat, 512:1024], xcv[:, n_hat, 512:1024], zsil[:, 1, :],
                        AL.mult), inc=vC)
            if r == 0:
                tap("t_g", xcv[:, :, :], [(vC, vC.v)])
            barrier()

            evo = {}
            for oh in range(ND):
                sch.emit("gpsimd", lambda e, oh=oh: e.dma_start(
                    out=wbO[:, oh % 2, :, :],
                    in_=out_wT[:, 128 * oh:128 * (oh + 1)].rearrange(
                        "(kh p) o -> p kh o", p=128)),
                    waits=[(tC, grp_t.get(("o", r, oh - 2), 0))], inc=dW, inc_n=16)
                d_g = dW.v
                pA, pB = (0, 1) if oh % 2 == 0 else (2, 3)
                for kh in range(NDI):
                    w0 = [(dW, d_g), (vC, evo.get(oh - 2, 0))] if kh == 0 else []
                    sch.emit("tensor", lambda e, oh=oh, kh=kh, pA=pA: e.matmul(
                        pbf(pA), wbO[:, oh % 2, kh, :], xcv[:, kh, 0:512],
                        start=(kh == 0), stop=(kh == NDI - 1), skip_group_check=True),
                        waits=w0)
                    sch.emit("tensor", lambda e, oh=oh, kh=kh, pB=pB: e.matmul(
                        pbf(pB), wbO[:, oh % 2, kh, :], xcv[:, kh, 512:1024],
                        start=(kh == 0), stop=(kh == NDI - 1), skip_group_check=True),
                        inc=(tC if kh == NDI - 1 else None))
                grp_t[("o", r, oh)] = tC.v
                t_mm = tC.v
                sch.emit("vector", lambda e, oh=oh, pA=pA: e.tensor_scalar(
                    attn_f[:, oh, 0:512], pbf(pA), 0.0, None, AL.add), waits=[(tC, t_mm)], inc=vC)
                sch.emit("vector", lambda e, oh=oh, pB=pB: e.tensor_scalar(
                    attn_f[:, oh, 512:1024], pbf(pB), 0.0, None, AL.add), inc=vC)
                evo[oh] = vC.v
            if r == 0:
                tap("t_attn", attn_f[:, :, :], [(vC, vC.v)])
            barrier()

            for tt in range(8):
                vpr = vC.v
                sch.emit("sync", lambda e, tt=tt, r=r: e.dma_start(
                    out=stage2[:, :],
                    in_=xm[r].rearrange("(n p) d -> p n d", p=128)[:, tt, :]),
                    waits=[(vC, vpr)], inc=dX, inc_n=16)
                d_x = dX.v
                for oh in range(ND):
                    bank = 6 + (oh % 2)
                    vprev = vC.v
                    sch.emit("tensor", lambda e, tt=tt, oh=oh, bank=bank: e.transpose(
                        pb[bank][:, 0:128], attn_f[:, oh, tt * 128:(tt + 1) * 128],
                        idnf_s[:, :]),
                        waits=[(vC, max(0, vprev - 1)), (dX, d_x)], inc=tC)
                    tm = tC.v
                    sch.emit("vector", lambda e, oh=oh, bank=bank: e.scalar_tensor_tensor(
                        stage2[:, oh * 128:(oh + 1) * 128], pb[bank][:, 0:128], 1.0,
                        stage2[:, oh * 128:(oh + 1) * 128], AL.mult, AL.add,
                        accum_out=sumc[:, oh:oh + 1]),
                        waits=[(tC, tm)], inc=vC)
                v_s2 = vC.v
                sch.emit("vector", lambda e: e.tensor_reduce(
                    sumc[:, 16:17], sumc[:, 0:16], AX, AL.add), waits=[(vC, v_s2)],
                    inc=vC, selfwait=True)
                sch.emit("scalar", lambda e: e.activation(
                    x1t[:, :], stage2[:, :], AF.Square, accum_out=sumc[:, 17:18]),
                    waits=[(vC, v_s2)], inc=sC)
                s_sq = sC.v
                sch.emit("vector", lambda e: e.tensor_scalar(
                    sumc[:, 18:19], sumc[:, 16:17], 1.0 / D, None, AL.mult), inc=vC)
                sch.emit("vector", lambda e: e.tensor_scalar(
                    sumc[:, 19:20], sumc[:, 17:18], 1.0 / D, None, AL.mult),
                    waits=[(sC, s_sq)], inc=vC)
                sch.emit("vector", lambda e: e.tensor_tensor(
                    sumc[:, 20:21], sumc[:, 18:19], sumc[:, 18:19], AL.mult), inc=vC)
                sch.emit("vector", lambda e: e.tensor_tensor(
                    sumc[:, 19:20], sumc[:, 19:20], sumc[:, 20:21], AL.subtract), inc=vC)
                v_var = vC.v
                sch.emit("scalar", lambda e: e.activation(
                    sumc[:, 20:21], sumc[:, 19:20], AF.Sqrt, bias=cst[:, 0:1]),
                    waits=[(vC, v_var)], inc=sC)
                s_std = sC.v
                sch.emit("vector", lambda e: e.reciprocal(sumc[:, 21:22], sumc[:, 20:21]),
                         waits=[(sC, s_std)], inc=vC)
                sch.emit("vector", lambda e: e.tensor_tensor(
                    sumc[:, 22:23], sumc[:, 18:19], sumc[:, 21:22], AL.mult), inc=vC)
                sch.emit("vector", lambda e: e.tensor_scalar(
                    sumc[:, 22:23], sumc[:, 22:23], -1.0, None, AL.mult), inc=vC)
                v_inv = vC.v
                sch.emit("scalar", lambda e: e.activation(
                    x1t[:, :], stage2[:, :], AF.Identity, scale=sumc[:, 21:22],
                    bias=sumc[:, 22:23]), waits=[(vC, v_inv)], inc=sC)
                s_x1 = sC.v
                sch.emit("sync", lambda e, tt=tt: e.dma_start(
                    out=x1_d[tt, :, :], in_=x1t[:, :]), waits=[(sC, s_x1)],
                    inc=dX, inc_n=16)
                sch.emit("vector", lambda e: e.tensor_reduce(
                    sumc[:, 16:17], x1t[:, :], AX, AL.max, apply_absolute_value=True),
                    waits=[(sC, s_x1)], inc=vC, selfwait=True)
                sch.emit("vector", lambda e: e.tensor_scalar(
                    sumc[:, 17:18], sumc[:, 16:17], 1.0 / 127.0, EPS, AL.mult, AL.add),
                    inc=vC)
                sch.emit("vector", lambda e, tt=tt: e.tensor_scalar(
                    scol8[:, tt:tt + 1], sumc[:, 17:18], 0.0, None, AL.add), inc=vC)
                sch.emit("vector", lambda e: e.reciprocal(sumc[:, 18:19], sumc[:, 17:18]),
                         inc=vC)
                sch.emit("vector", lambda e: e.tensor_scalar(
                    stage2[:, :], x1t[:, :], sumc[:, 18:19], MAGIC, AL.mult, AL.add),
                    inc=vC)
                sch.emit("vector", lambda e: e.tensor_scalar(
                    xqt[:, :], stage2[:, :], MAGIC, None, AL.subtract), inc=vC)
                v_xq = vC.v
                for oh in range(ND):
                    bank = 6 + (oh % 2)
                    vprev = vC.v
                    sch.emit("tensor", lambda e, oh=oh, bank=bank: e.transpose(
                        pbb(bank, 128), xqt[:, oh * 128:(oh + 1) * 128], idn_s[:, :]),
                        waits=[(vC, v_xq), (vC, max(0, vprev - 1))], inc=tC)
                    tm = tC.v
                    sch.emit("vector", lambda e, tt=tt, oh=oh, bank=bank: e.tensor_scalar(
                        xq_f[:, oh, tt * 128:(tt + 1) * 128], pbb(bank, 128), 0.0, None, AL.add),
                        waits=[(tC, tm)], inc=vC)
            if r == 0:
                tap("t_xq", xq_f[:, :, :], [(vC, vC.v)])
            barrier()
            v_sc = vC.v
            sch.emit("tensor", lambda e: e.transpose(pb[6][0:8, 0:128], scol8[:, :],
                                                     idnf_s[:, :]),
                     waits=[(vC, v_sc)], inc=tC)
            t_sc = tC.v
            sch.emit("vector", lambda e: e.tensor_scalar(rst[:, :], pb[6][0:8, 0:128], 0.0, None, AL.add),
                     waits=[(tC, t_sc)], inc=vC)
            v_sc2 = vC.v
            sch.emit("sync", lambda e: e.dma_start(
                out=row_d[3:4, 0:TC].rearrange("o (b t) -> (o b) t", b=8),
                in_=rst[:, :]), waits=[(vC, v_sc2)], inc=dX, inc_n=16)
            d_sc = dX.v
            sch.emit("sync", lambda e: e.dma_start(
                out=sg128[:, :], in_=row_d[3:4, 0:TC].broadcast_to([128, TC])),
                waits=[(dX, d_sc)], inc=dX, inc_n=16)
            d_sg = dX.v
            sch.emit("vector", lambda e: e.tensor_scalar(
                su128[:, :], sg128[:, :], wsb_s[:, 1:2], None, AL.mult),
                waits=[(dX, d_sg)], inc=vC)
            sch.emit("vector", lambda e: e.tensor_scalar(
                sg128[:, :], sg128[:, :], wsb_s[:, 0:1], None, AL.mult), inc=vC)
            v_sgsu = vC.v

            def quant_w(dst_ap, src_ap, n_el, wcol, dwait):
                sch.emit("scalar", lambda e, dst_ap=dst_ap, src_ap=src_ap, wcol=wcol,
                         n_el=n_el: e.activation(
                             qtmp[:, 0:n_el], src_ap, AF.Identity,
                             scale=rws_s[:, wcol:wcol + 1], bias=cst[:, 2:3]),
                         waits=[(dX, dwait), (dW, dW.v)], inc=sC)
                s_q = sC.v
                sch.emit("vector", lambda e, n_el=n_el: e.tensor_scalar(
                    qtmp[:, 0:n_el], qtmp[:, 0:n_el], MAGIC, 1.0, AL.subtract, AL.min),
                    waits=[(sC, s_q)], inc=vC)
                sch.emit("vector", lambda e, dst_ap=dst_ap, n_el=n_el: e.tensor_scalar(
                    dst_ap, qtmp[:, 0:n_el], -1.0, None, AL.max), inc=vC)

            sch.emit("vector", lambda e: e.memset(mxp[:, :], 0.0), inc=vC, selfwait=True)
            evf = {}
            wfst_mark = [0]
            for nh in range(NFF):
                par = nh % 2
                if r == 0:
                    for wname, wsrc, wdst, scr, wcol in [
                            ("g", w_gT, wbG, scr_g, 0), ("u", w_uT, wbU, scr_u, 1)]:
                        for half in range(2):
                            sch.emit("sync", lambda e, nh=nh, wsrc=wsrc, half=half:
                                     e.dma_start(
                                         out=wfst[:, :, :],
                                         in_=wsrc[1024 * half:1024 * (half + 1),
                                                  128 * nh:128 * (nh + 1)].rearrange(
                                             "(kh p) n -> p kh n", p=128)),
                                     waits=[(tC, grp_t.get(("f", r, nh - 2), 0)),
                                            (vC, wfst_mark[0])],
                                     inc=dX, inc_n=16)
                            d_w = dX.v
                            quant_w(wdst[:, par, 8 * half:8 * (half + 1), :].rearrange(
                                "p a b -> p (a b)"),
                                wfst[:, :, :].rearrange("p a b -> p (a b)"),
                                1024, wcol, d_w)
                            wfst_mark[0] = vC.v
                        v_q = vC.v
                        sch.emit("gpsimd", lambda e, nh=nh, scr=scr, par=par, wdst=wdst:
                                 e.dma_start(
                                     out=scr[:, 128 * nh:128 * (nh + 1)].rearrange(
                                         "(kh p) n -> p kh n", p=128),
                                     in_=wdst[:, par, :, :]),
                                 waits=[(vC, v_q)], inc=dW, inc_n=16)
                    w_wait = [(vC, vC.v), (dW, dW.v)]
                else:
                    sch.emit("sync", lambda e, nh=nh, par=par: e.dma_start(
                        out=wbG[:, par, :, :],
                        in_=scr_g[:, 128 * nh:128 * (nh + 1)].rearrange(
                            "(kh p) n -> p kh n", p=128)),
                        waits=[(tC, grp_t.get(("f", r, nh - 2), 0))], inc=dX, inc_n=16)
                    sch.emit("sync", lambda e, nh=nh, par=par: e.dma_start(
                        out=wbU[:, par, :, :],
                        in_=scr_u[:, 128 * nh:128 * (nh + 1)].rearrange(
                            "(kh p) n -> p kh n", p=128)),
                        inc=dX, inc_n=16)
                    w_wait = [(dX, dX.v)]
                pg = (0, 1) if par == 0 else (4, 5)
                pu = (2, 3) if par == 0 else (6, 7)
                for kh in range(ND):
                    w0 = (w_wait + [(vC, evf.get(nh - 2, (0, 0))[0]),
                                    (sC, evf.get(nh - 2, (0, 0))[1])]) if kh == 0 else []
                    sch.emit("tensor", lambda e, kh=kh, par=par, pg=pg: e.matmul(
                        pbf(pg[0]), wbG[:, par, kh, :], xq_f[:, kh, 0:512],
                        start=(kh == 0), stop=(kh == ND - 1), skip_group_check=True),
                        waits=w0)
                    sch.emit("tensor", lambda e, kh=kh, par=par, pg=pg: e.matmul(
                        pbf(pg[1]), wbG[:, par, kh, :], xq_f[:, kh, 512:1024],
                        start=(kh == 0), stop=(kh == ND - 1), skip_group_check=True))
                    sch.emit("tensor", lambda e, kh=kh, par=par, pu=pu: e.matmul(
                        pbf(pu[0]), wbU[:, par, kh, :], xq_f[:, kh, 0:512],
                        start=(kh == 0), stop=(kh == ND - 1), skip_group_check=True))
                    sch.emit("tensor", lambda e, kh=kh, par=par, pu=pu: e.matmul(
                        pbf(pu[1]), wbU[:, par, kh, :], xq_f[:, kh, 512:1024],
                        start=(kh == 0), stop=(kh == ND - 1), skip_group_check=True),
                        inc=(tC if kh == ND - 1 else None))
                grp_t[("f", r, nh)] = tC.v
                t_mm = tC.v
                sch.emit("vector", lambda e, pg=pg: e.tensor_tensor(
                    pbf(pg[0]), pbf(pg[0]), sg128[:, 0:512], AL.mult),
                    waits=[(tC, t_mm), (vC, v_sgsu)], inc=vC)
                sch.emit("vector", lambda e, pg=pg: e.tensor_tensor(
                    pbf(pg[1]), pbf(pg[1]), sg128[:, 512:1024], AL.mult), inc=vC)
                sch.emit("vector", lambda e, pu=pu: e.tensor_tensor(
                    pbf(pu[0]), pbf(pu[0]), su128[:, 0:512], AL.mult), inc=vC)
                sch.emit("vector", lambda e, pu=pu: e.tensor_tensor(
                    pbf(pu[1]), pbf(pu[1]), su128[:, 512:1024], AL.mult), inc=vC)
                v_dq = vC.v
                sch.emit("scalar", lambda e, pg=pg: e.activation(
                    sil2[:, 0, :], pbf(pg[0]), AF.Silu),
                    waits=[(vC, v_dq)], inc=sC)
                sch.emit("scalar", lambda e, pg=pg: e.activation(
                    sil2[:, 1, :], pbf(pg[1]), AF.Silu), inc=sC)
                s_sil = sC.v
                sch.emit("vector", lambda e, nh=nh, pu=pu: e.tensor_tensor(
                    g2[:, nh, 0:512], sil2[:, 0, :], pbf(pu[0]), AL.mult),
                    waits=[(sC, s_sil)], inc=vC)
                sch.emit("vector", lambda e, nh=nh, pu=pu: e.tensor_tensor(
                    g2[:, nh, 512:1024], sil2[:, 1, :], pbf(pu[1]), AL.mult), inc=vC)
                evf[nh] = (vC.v, sC.v)
                sch.emit("vector", lambda e, nh=nh: e.tensor_tensor(
                    mxp[:, :], mxp[:, :], g2[:, nh, :], AL.max), inc=vC)
                sch.emit("vector", lambda e, nh=nh: e.scalar_tensor_tensor(
                    mxp[:, :], g2[:, nh, :], -1.0, mxp[:, :], AL.mult, AL.max), inc=vC)
            v_gup = vC.v
            v_mx = vC.v
            for tb in range(8):
                bank = 6 + (tb % 2)
                vprev = vC.v
                sch.emit("tensor", lambda e, tb=tb, bank=bank: e.transpose(
                    pbb(bank, 128), mxp[:, tb * 128:(tb + 1) * 128], idn_s[:, :]),
                    waits=[(vC, v_mx), (vC, max(0, vprev - 1))], inc=tC)
                tm = tC.v
                sch.emit("vector", lambda e, tb=tb, bank=bank: e.tensor_reduce(
                    amx[:, tb:tb + 1], pbb(bank, 128), AX, AL.max),
                    waits=[(tC, tm)], inc=vC, selfwait=True)
            sch.emit("vector", lambda e: e.tensor_scalar(
                amx[:, 8:16], amx[:, 0:8], 1.0 / 127.0, EPS, AL.mult, AL.add), inc=vC)
            v_s2 = vC.v
            sch.emit("tensor", lambda e: e.transpose(pb[6][0:8, 0:128], amx[:, 8:16],
                                                     idnf_s[:, :]),
                     waits=[(vC, v_s2)], inc=tC)
            t_s2 = tC.v
            sch.emit("vector", lambda e: e.tensor_scalar(rst[:, :], pb[6][0:8, 0:128], 0.0, None, AL.add),
                     waits=[(tC, t_s2)], inc=vC)
            v_s2b = vC.v
            sch.emit("sync", lambda e: e.dma_start(
                out=row_d[2:3, 0:TC].rearrange("o (b t) -> (o b) t", b=8),
                in_=rst[:, :]), waits=[(vC, v_s2b)], inc=dX, inc_n=16)
            d_s2 = dX.v
            sch.emit("sync", lambda e: e.dma_start(
                out=s2b[:, :], in_=row_d[2:3, 0:TC].broadcast_to([128, TC])),
                waits=[(dX, d_s2)], inc=dX, inc_n=16)
            sch.emit("vector", lambda e: e.reciprocal(amx[:, 8:16], amx[:, 8:16]),
                     waits=[(vC, v_s2b)], inc=vC)
            v_r = vC.v
            sch.emit("tensor", lambda e: e.transpose(pb[6][0:8, 0:128], amx[:, 8:16],
                                                     idnf_s[:, :]),
                     waits=[(vC, v_r)], inc=tC)
            t_r = tC.v
            sch.emit("vector", lambda e: e.tensor_scalar(rst[:, :], pb[6][0:8, 0:128], 0.0, None, AL.add),
                     waits=[(tC, t_r), (dX, d_s2)], inc=vC)
            v_r2 = vC.v
            sch.emit("sync", lambda e: e.dma_start(
                out=row_d[2:3, 0:TC].rearrange("o (b t) -> (o b) t", b=8),
                in_=rst[:, :]),
                waits=[(vC, v_r2)], inc=dX, inc_n=16)
            d_r = dX.v
            sch.emit("sync", lambda e: e.dma_start(
                out=r128[:, :], in_=row_d[2:3, 0:TC].broadcast_to([128, TC])),
                waits=[(dX, d_r)], inc=dX, inc_n=16)
            d_r128 = dX.v
            if r == 0:
                tap("t_g2r", g2[:, :, :], [(vC, vC.v)])
                tap("t_r128", r128[:, :], [(dX, d_r128)])
                tap("t_wq", wbG[:, :, :, :], [(vC, vC.v)])
            for nh in range(NFF):
                w0 = [(dX, d_r128)] if nh == 0 else []
                sch.emit("vector", lambda e, nh=nh: e.tensor_tensor(
                    qtmp[:, :], g2[:, nh, :], r128[:, :], AL.mult), waits=w0, inc=vC)
                sch.emit("vector", lambda e: e.tensor_scalar(
                    qtmp[:, :], qtmp[:, :], MAGIC, None, AL.add), inc=vC)
                sch.emit("vector", lambda e, nh=nh: e.tensor_scalar(
                    g2[:, nh, :], qtmp[:, :], MAGIC, None, AL.subtract), inc=vC)
            if r == 0:
                tap("t_g2", g2[:, :, :], [(vC, vC.v)])
            barrier()

            def quant_d(dst_ap, src_ap, n_el, dwait):
                sch.emit("scalar", lambda e, dst_ap=dst_ap, src_ap=src_ap, n_el=n_el:
                         e.activation(qtmp2[:, 0:n_el], src_ap, AF.Identity,
                                      scale=rws_s[:, 2:3], bias=cst[:, 2:3]),
                         waits=[(dX, dwait), (dW, dW.v)], inc=sC)
                s_q = sC.v
                sch.emit("vector", lambda e, n_el=n_el: e.tensor_scalar(
                    qtmp2[:, 0:n_el], qtmp2[:, 0:n_el], MAGIC, 1.0, AL.subtract, AL.min),
                    waits=[(sC, s_q)], inc=vC)
                sch.emit("vector", lambda e, dst_ap=dst_ap, n_el=n_el: e.tensor_scalar(
                    dst_ap, qtmp2[:, 0:n_el], -1.0, None, AL.max), inc=vC)

            evd = {}
            for oh in range(ND):
                par = oh % 2
                pA, pB = (0, 1) if par == 0 else (2, 3)
                if r == 0:
                    for half in range(2):
                        for q4 in range(4):
                            k0 = half * 4096 + q4 * 1024
                            sch.emit("sync", lambda e, oh=oh, k0=k0: e.dma_start(
                                out=wfst2[:, :, :],
                                in_=w_dT[k0:k0 + 1024,
                                         128 * oh:128 * (oh + 1)].rearrange(
                                    "(kh p) n -> p kh n", p=128)),
                                waits=[(tC, tC.v), (vC, wfst_mark[0])],
                                inc=dX, inc_n=16)
                            d_w = dX.v
                            quant_d(wbD[:, par, (half * 32 + q4 * 8) % 32:
                                        (half * 32 + q4 * 8) % 32 + 8, :].rearrange(
                                "p a b -> p (a b)"),
                                wfst2[:, :, :].rearrange("p a b -> p (a b)"),
                                1024, d_w)
                            wfst_mark[0] = vC.v
                        v_q = vC.v
                        sch.emit("gpsimd", lambda e, oh=oh, par=par, half=half: e.dma_start(
                            out=scr_d[4096 * half:4096 * (half + 1),
                                      128 * oh:128 * (oh + 1)].rearrange(
                                "(kh p) n -> p kh n", p=128),
                            in_=wbD[:, par, :, :]), waits=[(vC, v_q)], inc=dW, inc_n=16)
                        for kh in range(32):
                            khg = half * 32 + kh
                            w0 = [(vC, v_q), (vC, evd.get(oh - 2, 0))] if kh == 0 else []
                            sch.emit("tensor", lambda e, kh=kh, khg=khg, par=par, pA=pA:
                                     e.matmul(
                                         pbf(pA), wbD[:, par, kh, :], g2[:, khg, 0:512],
                                         start=(khg == 0), stop=(khg == NFF - 1),
                                         skip_group_check=True), waits=w0)
                            sch.emit("tensor", lambda e, kh=kh, khg=khg, par=par, pB=pB:
                                     e.matmul(
                                         pbf(pB), wbD[:, par, kh, :], g2[:, khg, 512:1024],
                                         start=(khg == 0), stop=(khg == NFF - 1),
                                         skip_group_check=True),
                                     inc=(tC if kh == 31 else None))
                else:
                    for half in range(2):
                        sch.emit("sync", lambda e, oh=oh, par=par, half=half: e.dma_start(
                            out=wbD[:, par, :, :],
                            in_=scr_d[4096 * half:4096 * (half + 1),
                                      128 * oh:128 * (oh + 1)].rearrange(
                                "(kh p) n -> p kh n", p=128)),
                            waits=[(tC, tC.v), (vC, evd.get(oh - 2, 0))],
                            inc=dX, inc_n=16)
                        d_w = dX.v
                        for kh in range(32):
                            khg = half * 32 + kh
                            w0 = [(dX, d_w)] if kh == 0 else []
                            sch.emit("tensor", lambda e, kh=kh, khg=khg, par=par, pA=pA:
                                     e.matmul(
                                         pbf(pA), wbD[:, par, kh, :], g2[:, khg, 0:512],
                                         start=(khg == 0), stop=(khg == NFF - 1),
                                         skip_group_check=True), waits=w0)
                            sch.emit("tensor", lambda e, kh=kh, khg=khg, par=par, pB=pB:
                                     e.matmul(
                                         pbf(pB), wbD[:, par, kh, :], g2[:, khg, 512:1024],
                                         start=(khg == 0), stop=(khg == NFF - 1),
                                         skip_group_check=True),
                                     inc=(tC if kh == 31 else None))
                grp_t[("d", r, oh)] = tC.v
                t_mm = tC.v
                sch.emit("vector", lambda e, oh=oh, pA=pA: e.scalar_tensor_tensor(
                    dnf[:, oh, 0:512], pbf(pA), wsb_s[:, 2:3], s2b[:, 0:512],
                    AL.mult, AL.mult), waits=[(tC, t_mm)], inc=vC)
                sch.emit("vector", lambda e, oh=oh, pB=pB: e.scalar_tensor_tensor(
                    dnf[:, oh, 512:1024], pbf(pB), wsb_s[:, 2:3], s2b[:, 512:1024],
                    AL.mult, AL.mult), inc=vC)
                evd[oh] = vC.v
            barrier()

            for tt in range(8):
                vpr = vC.v
                sch.emit("sync", lambda e, tt=tt: e.dma_start(
                    out=x1rr[:, :], in_=x1_d[tt, :, :]), waits=[(vC, vpr)],
                    inc=dX, inc_n=16)
                d_x1 = dX.v
                for oh in range(ND):
                    bank = 6 + (oh % 2)
                    vprev = vC.v
                    sch.emit("tensor", lambda e, tt=tt, oh=oh, bank=bank: e.transpose(
                        pbb(bank, 128), dnf[:, oh, tt * 128:(tt + 1) * 128], idn_s[:, :]),
                        waits=[(vC, max(0, vprev - 1))], inc=tC)
                    tm = tC.v
                    sch.emit("vector", lambda e, oh=oh, bank=bank: e.scalar_tensor_tensor(
                        stage3[:, oh * 128:(oh + 1) * 128], pbb(bank, 128), 1.0,
                        x1rr[:, oh * 128:(oh + 1) * 128], AL.mult, AL.add,
                        accum_out=sumc[:, oh:oh + 1]),
                        waits=[(tC, tm), (dX, d_x1)], inc=vC)
                v_s3 = vC.v
                sch.emit("vector", lambda e: e.tensor_reduce(
                    sumc[:, 16:17], sumc[:, 0:16], AX, AL.add), waits=[(vC, v_s3)],
                    inc=vC, selfwait=True)
                sch.emit("scalar", lambda e: e.activation(
                    x1rr[:, :], stage3[:, :], AF.Square, accum_out=sumc[:, 17:18]),
                    waits=[(vC, v_s3)], inc=sC)
                s_sq = sC.v
                sch.emit("vector", lambda e: e.tensor_scalar(
                    sumc[:, 18:19], sumc[:, 16:17], 1.0 / D, None, AL.mult), inc=vC)
                sch.emit("vector", lambda e: e.tensor_scalar(
                    sumc[:, 19:20], sumc[:, 17:18], 1.0 / D, None, AL.mult),
                    waits=[(sC, s_sq)], inc=vC)
                sch.emit("vector", lambda e: e.tensor_tensor(
                    sumc[:, 20:21], sumc[:, 18:19], sumc[:, 18:19], AL.mult), inc=vC)
                sch.emit("vector", lambda e: e.tensor_tensor(
                    sumc[:, 19:20], sumc[:, 19:20], sumc[:, 20:21], AL.subtract), inc=vC)
                v_var = vC.v
                sch.emit("scalar", lambda e: e.activation(
                    sumc[:, 20:21], sumc[:, 19:20], AF.Sqrt, bias=cst[:, 0:1]),
                    waits=[(vC, v_var)], inc=sC)
                s_std = sC.v
                sch.emit("vector", lambda e: e.reciprocal(sumc[:, 21:22], sumc[:, 20:21]),
                         waits=[(sC, s_std)], inc=vC)
                sch.emit("vector", lambda e: e.tensor_tensor(
                    sumc[:, 22:23], sumc[:, 18:19], sumc[:, 21:22], AL.mult), inc=vC)
                sch.emit("vector", lambda e: e.tensor_scalar(
                    sumc[:, 22:23], sumc[:, 22:23], -1.0, None, AL.mult), inc=vC)
                v_inv = vC.v
                sch.emit("scalar", lambda e: e.activation(
                    stage3[:, :], stage3[:, :], AF.Identity, scale=sumc[:, 21:22],
                    bias=sumc[:, 22:23]), waits=[(vC, v_inv)], inc=sC)
                s_out = sC.v
                sch.emit("sync", lambda e, tt=tt, r=r: e.dma_start(
                    out=yout[r, :, :].rearrange("(n p) d -> p n d", p=128)[:, tt, :],
                    in_=stage3[:, :]), waits=[(sC, s_out)], inc=dX, inc_n=16)
            barrier()

        dxw, dww, dhw = dX.v, dW.v, dH.v
        for eng in ENGS:
            def thunk(e, dxw=dxw, dww=dww, dhw=dhw):
                e.wait_ge(dX.sem, dxw)
                e.wait_ge(dW.sem, dww)
                e.wait_ge(dH.sem, dhw)
            sch.q[eng].append(thunk)

        with nc.Block() as block:
            @block.sync
            def _(e):
                for fn in sch.q["sync"]:
                    fn(e)

            @block.gpsimd
            def _(e):
                for fn in sch.q["gpsimd"]:
                    fn(e)

            @block.vector
            def _(e):
                for fn in sch.q["vector"]:
                    fn(e)

            @block.scalar
            def _(e):
                for fn in sch.q["scalar"]:
                    fn(e)

            @block.tensor
            def _(e):
                for fn in sch.q["tensor"]:
                    fn(e)

    return nc


_NC_CACHE = None


def kernel(L0_attn_x, L0_attn_W_in, L0_attn_conv_w, L0_attn_conv_b, L0_attn_W_x,
           L0_attn_A_log, L0_attn_dt_bias, L0_attn_out_W, L0_attn_D,
           L0_ffn_w_gate, L0_ffn_w_up, L0_ffn_w_down):
    global _NC_CACHE
    x = np.ascontiguousarray(np.asarray(L0_attn_x, dtype=np.float32))
    w_inT = np.ascontiguousarray(np.asarray(L0_attn_W_in, np.float32).T)
    conv_w = np.ascontiguousarray(
        np.asarray(L0_attn_conv_w, np.float32).reshape(NDI, 128, 4).transpose(1, 0, 2))
    conv_b = np.ascontiguousarray(
        np.asarray(L0_attn_conv_b, np.float32).reshape(NDI, 128).T)
    _wx = np.asarray(L0_attn_W_x, np.float32)
    _wxt = np.zeros((128, NDI, 80), np.float32)
    _wxd = _wx.T.reshape(NDI, 128, 33).transpose(1, 0, 2)
    _wxt[:, :, 0] = _wxd[:, :, 32]
    _wxt[:, :, 32:48] = _wxd[:, :, 0:16]
    _wxt[:, :, 64:80] = _wxd[:, :, 16:32]
    w_xT = np.ascontiguousarray(_wxt)
    a_log = np.ascontiguousarray(np.asarray(L0_attn_A_log, np.float32)[0:1, :])
    dt_b = np.ascontiguousarray(np.asarray(L0_attn_dt_bias, np.float32).reshape(NDI, 128).T)
    out_wT = np.ascontiguousarray(np.asarray(L0_attn_out_W, np.float32).T)
    d_vec = np.ascontiguousarray(np.asarray(L0_attn_D, np.float32).reshape(NDI, 128).T)
    w_gT = np.ascontiguousarray(np.asarray(L0_ffn_w_gate, np.float32).T)
    w_uT = np.ascontiguousarray(np.asarray(L0_ffn_w_up, np.float32).T)
    w_dT = np.ascontiguousarray(np.asarray(L0_ffn_w_down, np.float32).T)

    idn = np.eye(128, dtype=BF)
    idnf = np.eye(128, dtype=np.float32)
    tau = (np.arange(2)[None, :, None] * 128 + np.arange(128)[:, None, None])
    tcol = np.arange(L)[None, None, :]
    gmask = np.ascontiguousarray(
        np.broadcast_to((tcol >= tau), (128, 2, L)).astype(np.float32))

    if _NC_CACHE is None:
        _NC_CACHE = build_program()
    nc = _NC_CACHE

    in_maps = []
    for c in range(NCORE):
        chm = np.zeros((128, NCORE), np.float32)
        chm[:, :c] = 1.0
        im = {
            "w_inT": w_inT, "w_xT": w_xT, "out_wT": out_wT,
            "w_gT": w_gT, "w_uT": w_uT, "w_dT": w_dT,
            "conv_w": conv_w, "conv_b": conv_b, "d_vec": d_vec,
            "dt_bias": dt_b, "a_row": a_log,
            "idn": idn, "idnf": idnf, "gmask": gmask, "chm": chm,
            "wsl": np.ascontiguousarray(np.stack([
                w_gT.reshape(8, -1)[c],
                w_uT.reshape(8, -1)[c],
                w_dT.reshape(8, -1)[c],
            ]).reshape(3, 128, 16384).transpose(1, 0, 2)),
        }
        for r in range(2):
            t0 = c * TC
            im[f"xm{r}"] = np.ascontiguousarray(x[r, t0:t0 + TC, :])
            if c == 0:
                im[f"xh{r}"] = np.zeros((3, D), np.float32)
            else:
                im[f"xh{r}"] = np.ascontiguousarray(x[r, t0 - 3:t0, :])
        in_maps.append(im)

    res = run_bass_kernel_spmd(nc, in_maps, core_ids=list(range(NCORE)))
    out = np.empty((2, 8192, D), np.float32)
    for c in range(NCORE):
        yc = np.asarray(res.results[c]["y"]).reshape(2, TC, D)
        for r in range(2):
            out[r, c * TC:(c + 1) * TC, :] = yc[r]
    return out


# version 04
# speedup vs baseline: 1.1159x; 1.0676x over previous
import contextlib
import numpy as np
import ml_dtypes
import concourse.bass as bass
import concourse.mybir as mybir
from concourse.bass_utils import run_bass_kernel_spmd

F32 = mybir.dt.float32
BF16 = mybir.dt.bfloat16
U8 = mybir.dt.uint8
BF = ml_dtypes.bfloat16
AL = mybir.AluOpType
AF = mybir.ActivationFunctionType
AX = mybir.AxisListType.X

NCORE = 8
TC = 1024
L = 256
NCH = TC // L
D, DI, DFF, S = 2048, 4096, 8192, 16
ND, NDI, NFF = D // 128, DI // 128, DFF // 128
MAGIC = float(1.5 * 2 ** 23)
EPS = 1e-5

OFF_A = 0
OFF_B = 33024
OFF_C = 98560
OFF_D = 164096
ARENA = 200960

ENGS = ("sync", "gpsimd", "vector", "scalar", "tensor")
DEBUG_TAPS = False


class Ctr:
    def __init__(self, sem):
        self.sem = sem
        self.v = 0


class Sched:
    def __init__(self):
        self.q = {e: [] for e in ENGS}

    def emit(self, eng, fn, waits=(), inc=None, inc_n=1, selfwait=False):
        if inc is not None and eng in ("vector", "scalar", "gpsimd"):
            selfwait = True
        if inc is not None and inc_n == 16:
            selfwait = True
        wl = [(c.sem, v) for (c, v) in waits]
        if selfwait and inc is not None and inc_n == 1:
            wl.append((inc.sem, inc.v))
        if inc is not None:
            inc.v += inc_n
        tgt = (inc.sem, inc_n, inc.v) if inc is not None else None

        def thunk(e, fn=fn, wl=wl, tgt=tgt, selfwait=selfwait):
            for sem_, v in wl:
                e.wait_ge(sem_, v)
            inst = fn(e)
            if tgt is not None:
                inst.then_inc(tgt[0], tgt[1])
                if selfwait:
                    e.wait_ge(tgt[0], tgt[2])

        self.q[eng].append(thunk)


def build_program():
    nc = bass.Bass("TRN2", target_bir_lowering=False)

    def din(name, shape, dtype=F32):
        return nc.dram_tensor(name, shape, dtype, kind="ExternalInput").ap()

    xm = [din("xm0", [TC, D]), din("xm1", [TC, D])]
    xh = [din("xh0", [3, D]), din("xh1", [3, D])]
    w_inT = din("w_inT", [D, DFF])
    w_xT = din("w_xT", [128, NDI, 80])
    out_wT = din("out_wT", [DI, D])
    w_gT = din("w_gT", [D, DFF])
    w_uT = din("w_uT", [D, DFF])
    w_dT = din("w_dT", [DFF, D])
    conv_w = din("conv_w", [128, NDI, 4])
    conv_b = din("conv_b", [128, NDI])
    d_vec = din("d_vec", [128, NDI])
    dt_bias = din("dt_bias", [128, NDI])
    a_row = din("a_row", [1, S])
    wsl = din("wsl", [128, 3, 16384])
    idn = din("idn", [128, 128], BF16)
    idnf = din("idnf", [128, 128], F32)
    gmask = din("gmask", [128, 2, L])
    chm = din("chm", [128, NCORE])
    yout = nc.dram_tensor("y", [2, TC, D], F32, kind="ExternalOutput").ap()

    gin_d = nc.dram_tensor("gin_d", [128, 513], F32).ap()
    gout_d = nc.dram_tensor("gout_d", [128 * NCORE, 513], F32).ap()
    ws_in_d = nc.dram_tensor("ws_in_d", [128, 4], F32).ap()
    ws_out_d = nc.dram_tensor("ws_out_d", [128, 4], F32).ap()
    row_d = nc.dram_tensor("row_d", [4, TC], F32).ap()
    col_d = nc.dram_tensor("col_d", [16, 8], F32).ap()
    x1_d = nc.dram_tensor("x1_d", [8, 128, D], F32).ap()
    scr_g = nc.dram_tensor("scr_g", [D, DFF], BF16).ap()
    scr_u = nc.dram_tensor("scr_u", [D, DFF], BF16).ap()
    scr_d = nc.dram_tensor("scr_d", [DFF, D], BF16).ap()

    taps = {}
    if DEBUG_TAPS:
        taps = {
            "t_xf": nc.dram_tensor("t_xf", [128, ND, 3 + TC], BF16, kind="ExternalOutput").ap(),
            "t_xcv": nc.dram_tensor("t_xcv", [128, NDI, TC], BF16, kind="ExternalOutput").ap(),
            "t_pB": nc.dram_tensor("t_pB", [16, TC], F32, kind="ExternalOutput").ap(),
            "t_pC": nc.dram_tensor("t_pC", [16, TC], F32, kind="ExternalOutput").ap(),
            "t_dt": nc.dram_tensor("t_dt", [1, TC], F32, kind="ExternalOutput").ap(),
            "t_ug": nc.dram_tensor("t_ug", [1, TC], F32, kind="ExternalOutput").ap(),
            "t_cf": nc.dram_tensor("t_cf", [16, NCH, L], BF16, kind="ExternalOutput").ap(),
            "t_bdtr": nc.dram_tensor("t_bdtr", [16, NCH, L], BF16, kind="ExternalOutput").ap(),
            "t_cbar": nc.dram_tensor("t_cbar", [16, NCH, L], BF16, kind="ExternalOutput").ap(),
            "t_hpack": nc.dram_tensor("t_hpack", [128, NCH, 512], F32, kind="ExternalOutput").ap(),
            "t_hloc": nc.dram_tensor("t_hloc", [128, 512], F32, kind="ExternalOutput").ap(),
            "t_hrun": nc.dram_tensor("t_hrun", [128, 512], F32, kind="ExternalOutput").ap(),
            "t_ymix": nc.dram_tensor("t_ymix", [128, NDI, TC], BF16, kind="ExternalOutput").ap(),
            "t_g": nc.dram_tensor("t_g", [128, NDI, TC], BF16, kind="ExternalOutput").ap(),
            "t_attn": nc.dram_tensor("t_attn", [128, ND, TC], F32, kind="ExternalOutput").ap(),
            "t_xq": nc.dram_tensor("t_xq", [128, ND, TC], BF16, kind="ExternalOutput").ap(),
            "t_g2": nc.dram_tensor("t_g2", [128, NFF, TC], BF16, kind="ExternalOutput").ap(),
            "t_g2r": nc.dram_tensor("t_g2r", [128, NFF, TC], BF16, kind="ExternalOutput").ap(),
            "t_rws": nc.dram_tensor("t_rws", [128, 3], F32, kind="ExternalOutput").ap(),
            "t_r128": nc.dram_tensor("t_r128", [128, TC], F32, kind="ExternalOutput").ap(),
            "t_wq": nc.dram_tensor("t_wq", [128, 2, 16, 128], BF16, kind="ExternalOutput").ap(),
        }

    sch = Sched()

    with contextlib.ExitStack() as top:
        sem = {k: Ctr(top.enter_context(nc.semaphore(k)))
               for k in ["dX", "dW", "dH", "tC", "vC", "sC", "cc", "bar"]}
        dX, dW, dH, tC, vC, sC, cc, bar = (sem[k] for k in
                                           ["dX", "dW", "dH", "tC", "vC", "sC", "cc", "bar"])

        pb = [top.enter_context(nc.psum_tensor(f"pb{i}", [128, 512], F32)) for i in range(8)]

        def pbf(i, n=512):
            return pb[i][:, 0:n]

        def pbb(i, n=1024):
            return pb[i][:, :].bitcast(BF16)[:, 0:n]

        def barrier():
            bar.v += 5
            tgt = bar.v
            dxw, dww, dhw, ccw = dX.v, dW.v, dH.v, cc.v
            for eng in ENGS:
                def thunk(e, dxw=dxw, dww=dww, dhw=dhw, ccw=ccw, tgt=tgt):
                    e.wait_ge(dX.sem, dxw)
                    e.wait_ge(dW.sem, dww)
                    e.wait_ge(dH.sem, dhw)
                    e.wait_ge(cc.sem, ccw)
                    e.nop().then_inc(bar.sem, 1)
                    e.wait_ge(bar.sem, tgt)
                sch.q[eng].append(thunk)

        arena = top.enter_context(nc.sbuf_tensor("arena", [128, ARENA], U8))

        def V(off, shape, dtype):
            nb = int(np.prod(shape[1:])) * mybir.dt.size(dtype)
            ap = arena[0:shape[0], off:off + nb].bitcast(dtype)
            if len(shape) == 2:
                return ap
            if len(shape) == 3:
                return ap.rearrange("p (a b) -> p a b", a=shape[1], b=shape[2])
            if len(shape) == 4:
                return ap.rearrange("p (a b c) -> p a b c", a=shape[1], b=shape[2], c=shape[3])
            raise ValueError(shape)

        def sb(name, shape, dtype):
            return top.enter_context(nc.sbuf_tensor(name, shape, dtype))

        idn_s = sb("idn_s", [128, 128], BF16)
        idnf_s = sb("idnf_s", [128, 128], F32)
        gm_s = sb("gm_s", [128, 2, L], F32)
        chm_s = sb("chm_s", [128, NCORE], F32)
        cw_s = sb("cw_s", [128, NDI, 4], F32)
        cb_s = sb("cb_s", [128, NDI], F32)
        dv_s = sb("dv_s", [128, NDI], F32)
        wx_s = sb("wx_s", [128, NDI, 80], BF16)
        a16 = sb("a16", [16, 4], F32)
        a128 = sb("a128", [128, 1], F32)
        mb_s = sb("mb_s", [1, 8], F32)
        ones_s = sb("ones_s", [128, 8], F32)
        rws_s = sb("rws_s", [128, 4], F32)
        wsb_s = sb("wsb_s", [128, 4], F32)
        wsum_s = sb("wsum_s", [128, 4], F32)
        sumc = sb("sumc", [128, 24], F32)
        rst = sb("rst", [8, 128], F32)
        zcol = sb("zcol", [128, 1], F32)
        e16b = sb("e16b", [16, L], BF16)
        btT = sb("btT", [128, NCH, 2, 16], BF16)
        gsb = sb("gsb", [128, 2, L], BF16)
        ec16 = sb("ec16", [16, NCH], F32)
        ec128 = sb("ec128", [128, NCH], F32)
        sc16 = sb("sc16", [16, 8], F32)
        ujb = sb("ujb", [128, 1], F32)
        ej_s = sb("ej_s", [128, 1], F32)
        amx = sb("amx", [128, 16], F32)
        scol8 = sb("scol8", [128, 8], F32)
        arow_s = sb("arow_s", [1, S], F32)
        cst = sb("cst", [128, 4], F32)

        xf = V(OFF_A, [128, ND, 3 + TC], BF16)
        xcv = V(OFF_B, [128, NDI, TC], BF16)
        wbA = V(OFF_C, [128, 2, 16, 256], BF16)
        xss = V(OFF_C + 16384, [128, 2, 1056], F32)
        cvt = V(OFF_C + 24832, [128, 2, TC], F32)
        xcT = V(OFF_C, [128, 2, DI], BF16)
        hpack = V(OFF_C + 16384, [128, NCH, 512], F32)
        ug16 = V(OFF_C + 24576, [16, TC], F32)
        dt16 = V(OFF_C + 28672, [16, TC], F32)
        bdt = V(OFF_C + 32768, [16, NCH, L], F32)
        pB_s = V(OFF_D + 18432, [16, TC], F32)
        pC_s = V(OFF_D + 22528, [16, TC], F32)
        cf_s = V(OFF_C + 40960, [16, NCH, L], BF16)
        bdtr = V(OFF_C + 43008, [16, NCH, L], BF16)
        cbar = V(OFF_C + 45056, [16, NCH, L], BF16)
        h0b = V(OFF_C + 47104, [16, DI], BF16)
        hloc = V(OFF_C + 55296, [128, 512], F32)
        hrun = V(OFF_C + 57344, [128, 512], F32)
        htmp = V(OFF_C + 59392, [128, 512], F32)
        gsl = V(OFF_C + 61440, [128, 512], F32)
        xtokb = V(OFF_D, [128, 8, D], BF16)
        xhs = V(OFF_D + 32768, [3, D], BF16)
        dt_s = V(OFF_D, [1, TC], F32)
        ug_s = V(OFF_D + 4096, [1, TC], F32)
        onerow = V(OFF_D + 8192, [1, TC], F32)
        e16a = V(OFF_D + 12288, [16, L], F32)
        e16f = V(OFF_D + 13312, [16, L], F32)
        hst = V(OFF_D + 14336, [16, 512], F32)
        zsil = V(OFF_D + 16384, [128, 2, 512], BF16)
        wst = V(OFF_B, [128, 2, 4096], F32)
        wxf = V(OFF_B + 32768, [128, NDI, 80], F32)
        dtb = V(OFF_B + 43008, [128, NDI], F32)
        xq_f = V(OFF_A, [128, ND, TC], BF16)
        attn_f = V(OFF_C, [128, ND, TC], F32)
        wbO = V(OFF_D, [128, 2, NDI, 128], BF16)
        stage2 = V(OFF_D + 16384, [128, D], F32)
        x1t = V(OFF_D + 24576, [128, D], F32)
        xqt = V(OFF_D + 32768, [128, D], BF16)
        g2 = V(OFF_B, [128, NFF, TC], BF16)
        wbG = V(OFF_D, [128, 2, 16, 128], BF16)
        wbU = V(OFF_D + 8192, [128, 2, 16, 128], BF16)
        wfst = V(OFF_D + 16384, [128, 16, 128], F32)
        qtmp = V(OFF_D + 16384, [128, TC], F32)
        sil2 = V(OFF_D + 24576, [128, 2, 512], BF16)
        mxp = V(OFF_D + 26624, [128, TC], BF16)
        sg128 = V(OFF_D + 28672, [128, TC], F32)
        r128 = V(OFF_D + 28672, [128, TC], F32)
        su128 = V(OFF_D + 32768, [128, TC], F32)
        dnf = V(OFF_A, [128, ND, TC], BF16)
        wbD = V(OFF_D, [128, 2, 32, 128], BF16)
        wfst2 = V(OFF_D + 16384, [128, 16, 128], F32)
        s2b = V(OFF_D + 24576, [128, TC], F32)
        stage3 = V(OFF_B, [128, D], F32)
        x1rr = V(OFF_B + 8192, [128, D], F32)

        for dst, src in [(idn_s[:, :], idn[:, :]), (idnf_s[:, :], idnf[:, :]),
                         (gm_s[:, :, :], gmask[:, :, :]), (chm_s[:, :], chm[:, :]),
                         (arow_s[:, :], a_row[:, :]),
                         (cw_s[:, :, :], conv_w[:, :, :]),
                         (cb_s[:, :], conv_b[:, :]),
                         (dv_s[:, :], d_vec[:, :]),
                         (dtb[:, :], dt_bias[:, :]),
                         (wxf[:, :, :], w_xT[:, :, :])]:
            sch.emit("sync", lambda e, dst=dst, src=src: e.dma_start(out=dst, in_=src),
                     inc=dX, inc_n=16)
        pre_dX = dX.v

        sch.emit("vector", lambda e: e.memset(ones_s[:, :], 1.0), inc=vC, selfwait=True)
        sch.emit("vector", lambda e: e.memset(cst[:, 0:1], EPS), inc=vC, selfwait=True)
        sch.emit("vector", lambda e: e.memset(cst[:, 1:2], 1.0), inc=vC, selfwait=True)
        sch.emit("vector", lambda e: e.memset(cst[:, 2:3], MAGIC), inc=vC, selfwait=True)
        sch.emit("vector", lambda e: e.memset(zcol[:, :], 0.0), inc=vC, selfwait=True)
        sch.emit("vector", lambda e: e.tensor_scalar(wx_s[:, :, :], wxf[:, :, :], 0.0, None, AL.add),
                 waits=[(dX, pre_dX)], inc=vC)
        sch.emit("vector", lambda e: e.tensor_reduce(wsum_s[:, 3:4], dtb[:, :], AX, AL.add),
                 waits=[(dX, pre_dX)], inc=vC, selfwait=True)
        v0 = vC.v
        sch.emit("tensor", lambda e: e.matmul(pb[6][0:1, 0:1], ones_s[:, 0:1],
                                              wsum_s[:, 3:4], start=True, stop=True),
                 waits=[(vC, v0)], inc=tC)
        t0 = tC.v
        sch.emit("vector", lambda e: e.tensor_scalar(mb_s[0:1, 0:1], pb[6][0:1, 0:1],
                                                     1.0 / DI, None, AL.mult),
                 waits=[(tC, t0)], inc=vC)
        sch.emit("scalar", lambda e: e.activation(arow_s[0:1, :], arow_s[0:1, :], AF.Exp),
                 waits=[(dX, pre_dX)], inc=sC)
        s0 = sC.v
        sch.emit("tensor", lambda e: e.transpose(pb[7][0:16, 0:1], arow_s[0:1, 0:16],
                                                 idnf_s[0:1, 0:1]),
                 waits=[(sC, s0)], inc=tC)
        t1 = tC.v
        sch.emit("vector", lambda e: e.tensor_scalar(a16[0:16, 0:1], pb[7][0:16, 0:1],
                                                     -1.0, None, AL.mult),
                 waits=[(tC, t1)], inc=vC)
        sch.emit("vector", lambda e: e.tensor_scalar(a16[0:16, 1:2], pb[7][0:16, 0:1], 0.0, None, AL.add), inc=vC)
        v1 = vC.v
        sch.emit("sync", lambda e: e.dma_start(out=col_d[:, 0:1], in_=a16[0:16, 0:1], allow_slow_non_contiguous=True),
                 waits=[(vC, v1)], inc=dX, inc_n=16)
        d0 = dX.v
        for g in range(8):
            sch.emit("sync", lambda e, g=g: e.dma_start(out=a128[16 * g:16 * (g + 1), 0:1],
                                                        in_=col_d[:, 0:1], allow_slow_non_contiguous=True),
                     waits=[(dX, d0)], inc=dX, inc_n=16)
        sch.emit("vector", lambda e: e.memset(wsum_s[:, 0:3], 0.0), inc=vC, selfwait=True)
        for w3 in range(3):
            for ch in range(4):
                p2 = ch % 2
                vprev = vC.v
                sch.emit("sync", lambda e, w3=w3, ch=ch, p2=p2: e.dma_start(
                    out=wst[:, p2, :], in_=wsl[:, w3, 4096 * ch:4096 * (ch + 1)]),
                    waits=[(vC, max(0, vprev - 1))], inc=dX, inc_n=16)
                dm = dX.v
                sch.emit("vector", lambda e, p2=p2: e.tensor_reduce(
                    wsum_s[:, 3:4], wst[:, p2, :], AX, AL.add, apply_absolute_value=True),
                    waits=[(dX, dm)], inc=vC, selfwait=True)
                sch.emit("vector", lambda e, w3=w3: e.tensor_tensor(
                    wsum_s[:, w3:w3 + 1], wsum_s[:, w3:w3 + 1], wsum_s[:, 3:4], AL.add),
                    inc=vC)
        v2 = vC.v
        sch.emit("sync", lambda e: e.dma_start(out=ws_in_d[:, 0:4], in_=wsum_s[:, 0:4]),
                 waits=[(vC, v2)], inc=dX, inc_n=16)
        d1 = dX.v
        sch.emit("gpsimd", lambda e: e.collective_compute(
            "AllReduce", AL.add, replica_groups=[list(range(NCORE))],
            ins=[ws_in_d[:, :]], outs=[ws_out_d[:, :]]),
            waits=[(dX, d1)], inc=cc)
        cc0 = cc.v
        sch.emit("sync", lambda e: e.dma_start(out=wsum_s[:, 0:4], in_=ws_out_d[:, :]),
                 waits=[(cc, cc0)], inc=dX, inc_n=16)
        d2 = dX.v
        sch.emit("tensor", lambda e: e.matmul(pb[6][0:1, 0:3], ones_s[:, 0:1],
                                              wsum_s[:, 0:3], start=True, stop=True),
                 waits=[(dX, d2)], inc=tC)
        t2 = tC.v
        sch.emit("vector", lambda e: e.tensor_scalar(wsum_s[0:1, 0:3], pb[6][0:1, 0:3],
                                                     1.0 / 16777216.0, EPS, AL.mult, AL.add),
                 waits=[(tC, t2)], inc=vC)
        v3 = vC.v
        sch.emit("sync", lambda e: e.dma_start(out=ws_out_d[0:1, 0:3], in_=wsum_s[0:1, 0:3]),
                 waits=[(vC, v3)], inc=dX, inc_n=16)
        d3 = dX.v
        sch.emit("sync", lambda e: e.dma_start(out=rws_s[:, 0:3],
                                               in_=ws_out_d[0:1, 0:3].broadcast_to([128, 3])),
                 waits=[(dX, d3)], inc=dX, inc_n=16)
        d4 = dX.v
        sch.emit("vector", lambda e: e.tensor_scalar(wsb_s[:, 0:3], rws_s[:, 0:3], 0.0, None, AL.add),
                 waits=[(dX, d4)], inc=vC)
        sch.emit("vector", lambda e: e.reciprocal(rws_s[:, 0:3], rws_s[:, 0:3]), inc=vC)
        barrier()

        def tap(name, src, waits):
            if name in taps:
                sch.emit("sync", lambda e, name=name, src=src: e.dma_start(
                    out=taps[name][...], in_=src), waits=waits, inc=dX, inc_n=16)
                barrier()

        if "t_rws" in taps:
            sch.emit("sync", lambda e: e.dma_start(out=taps["t_rws"][...],
                                                   in_=rws_s[:, 0:3]),
                     waits=[(vC, vC.v)], inc=dX, inc_n=16)

        grp_t = {}
        for r in range(2):
            sch.emit("gpsimd", lambda e, r=r: e.dma_start(
                out=xtokb[:, :, :], in_=xm[r].rearrange("(n p) d -> p n d", p=128)),
                inc=dW, inc_n=16)
            sch.emit("gpsimd", lambda e, r=r: e.dma_start(out=xhs[:, :], in_=xh[r][:, :]),
                     inc=dW, inc_n=16)
            d_x0 = dW.v
            nblk = 0
            for dh in range(ND):
                for tt in range(9):
                    bank = 6 + (nblk % 2)
                    vprev = vC.v
                    if tt < 8:
                        sch.emit("tensor", lambda e, dh=dh, tt=tt, bank=bank: e.transpose(
                            pbb(bank, 128), xtokb[:, tt, dh * 128:(dh + 1) * 128],
                            idn_s[:, :]),
                            waits=[(dW, d_x0), (vC, max(0, vprev - 1))], inc=tC)
                        tm = tC.v
                        sch.emit("vector", lambda e, dh=dh, tt=tt, bank=bank: e.tensor_scalar(
                            xf[:, dh, 3 + tt * 128: 3 + (tt + 1) * 128], pbb(bank, 128), 0.0, None, AL.add),
                            waits=[(tC, tm)], inc=vC)
                    else:
                        sch.emit("tensor", lambda e, dh=dh, bank=bank: e.transpose(
                            pbb(bank, 3), xhs[0:3, dh * 128:(dh + 1) * 128],
                            idn_s[0:3, 0:3]),
                            waits=[(dW, d_x0), (vC, max(0, vprev - 1))], inc=tC)
                        tm = tC.v
                        sch.emit("vector", lambda e, dh=dh, bank=bank: e.tensor_scalar(
                            xf[:, dh, 0:3], pbb(bank, 3), 0.0, None, AL.add), waits=[(tC, tm)], inc=vC)
                    nblk += 1
            v_xf = vC.v
            if r == 0:
                tap("t_xf", xf[:, :, :], [(vC, v_xf)])

            evm = {}
            smark = {}
            for g in range(16):
                sch.emit("gpsimd", lambda e, g=g: e.dma_start(
                    out=wbA[:, g % 2, :, :],
                    in_=w_inT[:, 256 * g:256 * (g + 1)].rearrange("(kh p) n -> p kh n", p=128)),
                    waits=[(tC, grp_t.get(("a", r, g - 2), 0))], inc=dW, inc_n=16)
                d_g = dW.v
                for nl in range(2):
                    n_hat = g * 2 + nl
                    pA, pB, pC = (0, 1, 4) if n_hat % 2 == 0 else (2, 3, 5)
                    for kh in range(16):
                        w0 = [(dW, d_g),
                              (vC, max(evm.get(n_hat - 2, 0), v_xf))] if kh == 0 else []
                        sch.emit("tensor", lambda e, g=g, nl=nl, kh=kh, pA=pA: e.matmul(
                            pbf(pA), wbA[:, g % 2, kh, nl * 128:(nl + 1) * 128],
                            xf[:, kh, 0:512], start=(kh == 0), stop=(kh == 15),
                            skip_group_check=True), waits=w0)
                        sch.emit("tensor", lambda e, g=g, nl=nl, kh=kh, pB=pB: e.matmul(
                            pbf(pB), wbA[:, g % 2, kh, nl * 128:(nl + 1) * 128],
                            xf[:, kh, 512:1024], start=(kh == 0), stop=(kh == 15),
                            skip_group_check=True))
                        sch.emit("tensor", lambda e, g=g, nl=nl, kh=kh, pC=pC: e.matmul(
                            pbf(pC, 3), wbA[:, g % 2, kh, nl * 128:(nl + 1) * 128],
                            xf[:, kh, 1024:1027], start=(kh == 0), stop=(kh == 15),
                            skip_group_check=True),
                            inc=(tC if kh == 15 else None))
                    grp_t[("a", r, g)] = tC.v
                    t_mm = tC.v
                    sp = n_hat % 2
                    sch.emit("vector", lambda e, sp=sp, pA=pA: e.tensor_scalar(
                        xss[:, sp, 0:512], pbf(pA), 0.0, None, AL.add), waits=[(tC, t_mm)], inc=vC)
                    sch.emit("vector", lambda e, sp=sp, pB=pB: e.tensor_scalar(
                        xss[:, sp, 512:1024], pbf(pB), 0.0, None, AL.add), inc=vC)
                    sch.emit("vector", lambda e, sp=sp, pC=pC: e.tensor_scalar(
                        xss[:, sp, 1024:1027], pbf(pC, 3), 0.0, None, AL.add), inc=vC)
                    evm[n_hat] = vC.v
                    sch.emit("vector", lambda e, sp=sp, n_hat=n_hat: e.tensor_scalar(
                        cvt[:, sp, :], xss[:, sp, 0:TC], cw_s[:, n_hat, 0:1],
                        cb_s[:, n_hat:n_hat + 1], AL.mult, AL.add),
                        waits=[(sC, smark.get(n_hat - 2, 0))], inc=vC)
                    for k in range(1, 4):
                        sch.emit("vector", lambda e, sp=sp, n_hat=n_hat, k=k:
                                 e.scalar_tensor_tensor(
                                     cvt[:, sp, :], xss[:, sp, k:TC + k],
                                     cw_s[:, n_hat, k:k + 1], cvt[:, sp, :],
                                     AL.mult, AL.add), inc=vC)
                    v_cv = vC.v
                    sch.emit("scalar", lambda e, sp=sp, n_hat=n_hat: e.activation(
                        xcv[:, n_hat, :], cvt[:, sp, :], AF.Silu),
                        waits=[(vC, v_cv)], inc=sC)
                    smark[n_hat] = sC.v
            s_xcv = sC.v
            if r == 0:
                tap("t_xcv", xcv[:, :, :], [(sC, s_xcv)])
            barrier()

            sch.emit("vector", lambda e: e.memset(onerow[:, :], 1.0), inc=vC, selfwait=True)
            for nh in range(NDI):
                sch.emit("tensor", lambda e, nh=nh: e.matmul(
                    pb[6][0:80, :], wx_s[:, nh, :], xcv[:, nh, 0:512],
                    start=(nh == 0), stop=(nh == NDI - 1), skip_group_check=True))
                sch.emit("tensor", lambda e, nh=nh: e.matmul(
                    pb[7][0:80, :], wx_s[:, nh, :], xcv[:, nh, 512:1024],
                    start=(nh == 0), stop=(nh == NDI - 1), skip_group_check=True),
                    inc=(tC if nh == NDI - 1 else None))
            t_p = tC.v
            sch.emit("vector", lambda e: e.tensor_scalar(pB_s[:, 0:512], pb[6][32:48, :], 0.0, None, AL.add),
                     waits=[(tC, t_p)], inc=vC)
            sch.emit("vector", lambda e: e.tensor_scalar(pB_s[:, 512:1024], pb[7][32:48, :], 0.0, None, AL.add),
                     inc=vC)
            sch.emit("vector", lambda e: e.tensor_scalar(pC_s[:, 0:512], pb[6][64:80, :], 0.0, None, AL.add),
                     inc=vC)
            sch.emit("vector", lambda e: e.tensor_scalar(pC_s[:, 512:1024], pb[7][64:80, :], 0.0, None, AL.add),
                     inc=vC)
            v_p = vC.v
            sch.emit("scalar", lambda e: e.activation(dt_s[0:1, 0:512], pb[6][0:1, :],
                                                      AF.Exp, bias=mb_s[0:1, 0:1]),
                     waits=[(vC, v_p)], inc=sC)
            sch.emit("scalar", lambda e: e.activation(dt_s[0:1, 512:1024], pb[7][0:1, :],
                                                      AF.Exp, bias=mb_s[0:1, 0:1]),
                     inc=sC)
            sch.emit("scalar", lambda e: e.activation(dt_s[0:1, :], dt_s[0:1, :], AF.Ln,
                                                      bias=cst[0:1, 1:2]), inc=sC)
            s_dt = sC.v
            sch.emit("vector", lambda e: e.tensor_tensor_scan(
                ug_s[0:1, :], onerow[0:1, :], dt_s[0:1, :], 0.0, AL.mult, AL.add),
                waits=[(sC, s_dt)], inc=vC)
            v_ug = vC.v
            sch.emit("sync", lambda e: e.dma_start(out=row_d[0:1, :], in_=ug_s[0:1, :]),
                     waits=[(vC, v_ug)], inc=dX, inc_n=16)
            sch.emit("sync", lambda e: e.dma_start(out=row_d[1:2, :], in_=dt_s[0:1, :]),
                     inc=dX, inc_n=16)
            d_row = dX.v
            sch.emit("sync", lambda e: e.dma_start(
                out=ug16[:, :], in_=row_d[0:1, :].broadcast_to([16, TC])),
                waits=[(dX, d_row)], inc=dX, inc_n=16)
            sch.emit("sync", lambda e: e.dma_start(
                out=dt16[:, :], in_=row_d[1:2, :].broadcast_to([16, TC])),
                inc=dX, inc_n=16)
            d_b16 = dX.v

            for c in range(NCH):
                lo, hi = c * L, (c + 1) * L
                if c == 0:
                    sch.emit("vector", lambda e: e.memset(sc16[0:16, 0:1], 0.0),
                             waits=[(dX, d_b16)], inc=vC, selfwait=True)
                else:
                    sch.emit("vector", lambda e, lo=lo: e.tensor_scalar(
                        sc16[0:16, 0:1], ug16[:, lo - 1:lo], 0.0, None, AL.add),
                        waits=[(dX, d_b16)], inc=vC)
                sch.emit("vector", lambda e, hi=hi: e.tensor_scalar(
                    sc16[0:16, 1:2], ug16[:, hi - 1:hi], 0.0, None, AL.add), inc=vC)
                sch.emit("vector", lambda e: e.tensor_tensor(
                    sc16[0:16, 2:3], sc16[0:16, 0:1], sc16[0:16, 1:2], AL.add), inc=vC)
                sch.emit("vector", lambda e: e.tensor_scalar(
                    sc16[0:16, 2:3], sc16[0:16, 2:3], 0.5, None, AL.mult), inc=vC)
                sch.emit("vector", lambda e, c=c, lo=lo, hi=hi: e.tensor_tensor(
                    bdt[:, c, :], pB_s[:, lo:hi], dt16[:, lo:hi], AL.mult), inc=vC)
                sch.emit("vector", lambda e, lo=lo, hi=hi: e.tensor_scalar(
                    e16a[:, :], ug16[:, lo:hi], sc16[0:16, 2:3], 5.3,
                    AL.subtract, AL.min), inc=vC)
                sch.emit("vector", lambda e: e.tensor_scalar(
                    e16a[:, :], e16a[:, :], -5.3, None, AL.max), inc=vC)
                v_td = vC.v
                sch.emit("scalar", lambda e: e.activation(e16f[:, :], e16a[:, :], AF.Exp,
                                                          scale=a16[0:16, 0:1]),
                         waits=[(vC, v_td)], inc=sC)
                s_F = sC.v
                sch.emit("vector", lambda e, c=c, lo=lo, hi=hi: e.tensor_tensor(
                    cf_s[:, c, :], pC_s[:, lo:hi], e16f[:, :], AL.mult),
                    waits=[(sC, s_F)], inc=vC)
                v_cf = vC.v
                sch.emit("scalar", lambda e: e.activation(e16f[:, :], e16a[:, :], AF.Exp,
                                                          scale=a16[0:16, 1:2]),
                         waits=[(vC, v_cf)], inc=sC)
                s_R = sC.v
                sch.emit("vector", lambda e, c=c: e.tensor_tensor(
                    bdtr[:, c, :], bdt[:, c, :], e16f[:, :], AL.mult),
                    waits=[(sC, s_R)], inc=vC)
                sch.emit("vector", lambda e, lo=lo, hi=hi: e.tensor_scalar(
                    e16a[:, :], ug16[:, lo:hi], sc16[0:16, 0:1], None, AL.subtract),
                    inc=vC)
                v_eb = vC.v
                sch.emit("scalar", lambda e: e.activation(e16f[:, :], e16a[:, :], AF.Exp,
                                                          scale=a16[0:16, 0:1]),
                         waits=[(vC, v_eb)], inc=sC)
                s_Eb = sC.v
                sch.emit("vector", lambda e, c=c, lo=lo, hi=hi: e.tensor_tensor(
                    cbar[:, c, :], pC_s[:, lo:hi], e16f[:, :], AL.mult),
                    waits=[(sC, s_Eb)], inc=vC)
                sch.emit("vector", lambda e, lo=lo, hi=hi: e.tensor_scalar(
                    e16a[:, :], ug16[:, lo:hi], sc16[0:16, 1:2], None, AL.subtract),
                    inc=vC)
                v_e2 = vC.v
                sch.emit("scalar", lambda e: e.activation(e16f[:, :], e16a[:, :], AF.Exp,
                                                          scale=a16[0:16, 1:2]),
                         waits=[(vC, v_e2)], inc=sC)
                s_E2 = sC.v
                sch.emit("vector", lambda e, c=c: e.tensor_tensor(
                    e16b[:, :], bdt[:, c, :], e16f[:, :], AL.mult),
                    waits=[(sC, s_E2)], inc=vC)
                v_bt = vC.v
                for th in range(2):
                    sch.emit("tensor", lambda e, th=th: e.transpose(
                        pbb(6 + th, 16), e16b[0:16, th * 128:(th + 1) * 128],
                        idn_s[0:16, 0:16]),
                        waits=[(vC, v_bt)], inc=tC)
                    t_bt = tC.v
                    sch.emit("vector", lambda e, c=c, th=th: e.tensor_scalar(
                        btT[:, c, th, :], pbb(6 + th, 16), 0.0, None, AL.add), waits=[(tC, t_bt)], inc=vC)
                sch.emit("vector", lambda e: e.tensor_tensor(
                    sc16[0:16, 3:4], sc16[0:16, 1:2], sc16[0:16, 0:1], AL.subtract),
                    inc=vC)
                v_ec = vC.v
                sch.emit("scalar", lambda e, c=c: e.activation(
                    ec16[0:16, c:c + 1], sc16[0:16, 3:4], AF.Exp, scale=a16[0:16, 0:1]),
                    waits=[(vC, v_ec)], inc=sC)
            s_prep = sC.v
            sch.emit("sync", lambda e: e.dma_start(out=col_d[:, 0:NCH], in_=ec16[0:16, :], allow_slow_non_contiguous=True),
                     waits=[(sC, s_prep)], inc=dX, inc_n=16)
            d_ec = dX.v
            for g in range(8):
                sch.emit("sync", lambda e, g=g: e.dma_start(
                    out=ec128[16 * g:16 * (g + 1), :], in_=col_d[:, 0:NCH],
                    allow_slow_non_contiguous=True),
                    waits=[(dX, d_ec)], inc=dX, inc_n=16)
            d_ec128 = dX.v

            if r == 0:
                tap("t_pB", pB_s[:, :], [(vC, vC.v)])
                tap("t_pC", pC_s[:, :], [(vC, vC.v)])
                tap("t_dt", dt_s[:, :], [(sC, sC.v)])
                tap("t_ug", ug_s[:, :], [(vC, vC.v)])
                tap("t_cf", cf_s[:, :, :], [(vC, vC.v)])
                tap("t_bdtr", bdtr[:, :, :], [(vC, vC.v)])
                tap("t_cbar", cbar[:, :, :], [(vC, vC.v)])

            def emit_xcT(c):
                lo = c * L
                for th in range(2):
                    for dh in range(NDI):
                        bank = 6 + ((th * NDI + dh) % 2)
                        vprev = vC.v
                        sch.emit("tensor", lambda e, th=th, dh=dh, bank=bank, lo=lo:
                                 e.transpose(
                                     pbb(bank, 128),
                                     xcv[:, dh, lo + th * 128: lo + (th + 1) * 128],
                                     idn_s[:, :]),
                                 waits=[(vC, max(0, vprev - 1))], inc=tC)
                        tm = tC.v
                        sch.emit("vector", lambda e, th=th, dh=dh, bank=bank: e.tensor_scalar(
                            xcT[:, th, dh * 128:(dh + 1) * 128], pbb(bank, 128), 0.0, None, AL.add),
                            waits=[(tC, tm)], inc=vC)
                return vC.v

            sch.emit("vector", lambda e: e.memset(hloc[:, :], 0.0), inc=vC, selfwait=True)
            hv = {}
            for c in range(NCH):
                v_xcT = emit_xcT(c)
                for dgr in range(8):
                    hb = 4 + dgr % 2
                    sch.emit("tensor", lambda e, c=c, dgr=dgr, hb=hb: e.matmul(
                        pb[hb][0:16, :], btT[:, c, 0, :],
                        xcT[:, 0, dgr * 512:(dgr + 1) * 512], start=True, stop=False,
                        skip_group_check=True),
                        waits=[(vC, max(v_xcT, hv.get((c, dgr - 2), 0)))])
                    sch.emit("tensor", lambda e, c=c, dgr=dgr, hb=hb: e.matmul(
                        pb[hb][0:16, :], btT[:, c, 1, :],
                        xcT[:, 1, dgr * 512:(dgr + 1) * 512], start=False, stop=True,
                        skip_group_check=True), inc=tC)
                    t_h = tC.v
                    dhp = dH.v
                    sch.emit("vector", lambda e, hb=hb: e.tensor_scalar(
                        hst[0:16, :], pb[hb][0:16, :], 0.0, None, AL.add),
                        waits=[(tC, t_h), (dH, dhp)], inc=vC)
                    hv[(c, dgr)] = vC.v
                    v_h = vC.v
                    sch.emit("sync", lambda e, c=c, dgr=dgr: e.dma_start(
                        out=hpack[16 * dgr:16 * (dgr + 1), c, :], in_=hst[0:16, :]),
                        waits=[(vC, v_h)], inc=dH, inc_n=16)
                d_hc = dH.v
                sch.emit("vector", lambda e, c=c: e.scalar_tensor_tensor(
                    hloc[:, :], hloc[:, :], ec128[:, c:c + 1], hpack[:, c, :],
                    AL.mult, AL.add), waits=[(dH, d_hc), (dX, d_ec128)], inc=vC)
            v_hloc = vC.v

            if r == 0:
                tap("t_hpack", hpack[:, :, :], [(vC, vC.v)])
                tap("t_hloc", hloc[:, :], [(vC, vC.v)])

            sch.emit("sync", lambda e: e.dma_start(out=gin_d[:, 0:512], in_=hloc[:, :]),
                     waits=[(vC, v_hloc)], inc=dX, inc_n=16)
            sch.emit("sync", lambda e: e.dma_start(out=gin_d[0:1, 512:513],
                                                   in_=ug_s[0:1, TC - 1:TC]),
                     inc=dX, inc_n=16)
            sch.emit("sync", lambda e: e.dma_start(out=gin_d[1:128, 512:513],
                                                   in_=zcol[0:127, :],
                                                   allow_slow_non_contiguous=True),
                     inc=dX, inc_n=16)
            d_gin = dX.v
            sch.emit("gpsimd", lambda e: e.collective_compute(
                "AllGather", AL.bypass, replica_groups=[list(range(NCORE))],
                ins=[gin_d[:, :]], outs=[gout_d[:, :]]),
                waits=[(dX, d_gin)], inc=cc)
            cc_g = cc.v
            sch.emit("vector", lambda e: e.memset(hrun[:, :], 0.0), inc=vC, selfwait=True)
            for j in range(NCORE - 1):
                vpr = vC.v
                sch.emit("sync", lambda e, j=j: e.dma_start(
                    out=gsl[:, :], in_=gout_d[128 * j:128 * (j + 1), 0:512]),
                    waits=[(cc, cc_g), (vC, vpr)], inc=dX, inc_n=16)
                sch.emit("sync", lambda e, j=j: e.dma_start(
                    out=ujb[:, :],
                    in_=gout_d[128 * j:128 * j + 1, 512:513].broadcast_to([128, 1])),
                    inc=dX, inc_n=16)
                d_j = dX.v
                sch.emit("vector", lambda e, j=j: e.tensor_tensor(
                    ujb[:, :], ujb[:, :], chm_s[:, j:j + 1], AL.mult),
                    waits=[(dX, d_j)], inc=vC)
                v_um = vC.v
                sch.emit("scalar", lambda e: e.activation(ej_s[:, :], ujb[:, :], AF.Exp,
                                                          scale=a128[:, 0:1]),
                         waits=[(vC, v_um)], inc=sC)
                s_ej = sC.v
                sch.emit("vector", lambda e: e.tensor_scalar(
                    htmp[:, :], hrun[:, :], ej_s[:, 0:1], None, AL.mult),
                    waits=[(sC, s_ej)], inc=vC)
                sch.emit("vector", lambda e, j=j: e.scalar_tensor_tensor(
                    hrun[:, :], gsl[:, :], chm_s[:, j:j + 1], htmp[:, :],
                    AL.mult, AL.add), inc=vC)

            if r == 0:
                tap("t_hrun", hrun[:, :], [(vC, vC.v)])

            yev = {}
            gv = {}
            for c in range(NCH):
                lo = c * L
                v_hr = vC.v
                t_pre = tC.v
                for g in range(8):
                    sch.emit("gpsimd", lambda e, g=g: e.dma_start(
                        out=h0b[0:16, 512 * g:512 * (g + 1)],
                        in_=hrun[16 * g:16 * (g + 1), :]),
                        waits=([(vC, v_hr), (tC, t_pre)] if g == 0 else []),
                        inc=dW, inc_n=16)
                d_h0w = dW.v
                v_xcT = emit_xcT(c)
                for th in range(2):
                    gb = 4 + th
                    sch.emit("tensor", lambda e, c=c, th=th, gb=gb: e.matmul(
                        pbf(gb, L), bdtr[0:16, c, th * 128:(th + 1) * 128],
                        cf_s[0:16, c, :], start=True, stop=True),
                        waits=[(vC, max(v_xcT, gv.get((c - 1, th), 0)))], inc=tC)
                    t_g = tC.v
                    sch.emit("vector", lambda e, th=th, gb=gb: e.tensor_tensor(
                        gsb[:, th, :], pbf(gb, L), gm_s[:, th, :], AL.mult),
                        waits=[(tC, t_g)], inc=vC)
                    gv[(c, th)] = vC.v
                v_g = vC.v
                for dh in range(NDI):
                    pban = dh % 2
                    sch.emit("tensor", lambda e, c=c, dh=dh, pban=pban: e.matmul(
                        pbf(pban, L), h0b[0:16, dh * 128:(dh + 1) * 128],
                        cbar[0:16, c, :], start=True, stop=False, skip_group_check=True),
                        waits=[(vC, v_g), (dW, d_h0w),
                               (vC, yev.get(dh - 2, 0))])
                    for th in range(2):
                        sch.emit("tensor", lambda e, c=c, dh=dh, th=th, pban=pban: e.matmul(
                            pbf(pban, L), xcT[:, th, dh * 128:(dh + 1) * 128],
                            gsb[:, th, :], start=False, stop=(th == 1),
                            skip_group_check=True),
                            inc=(tC if th == 1 else None))
                    t_y = tC.v
                    sch.emit("vector", lambda e, c=c, dh=dh, pban=pban, lo=lo:
                             e.scalar_tensor_tensor(
                                 xcv[:, dh, lo:lo + L], xcv[:, dh, lo:lo + L],
                                 dv_s[:, dh:dh + 1], pbf(pban, L), AL.mult, AL.add),
                             waits=[(tC, t_y)], inc=vC)
                    yev[dh] = vC.v
                sch.emit("vector", lambda e, c=c: e.scalar_tensor_tensor(
                    hrun[:, :], hrun[:, :], ec128[:, c:c + 1], hpack[:, c, :],
                    AL.mult, AL.add), waits=[(dW, d_h0w)], inc=vC)
            if r == 0:
                tap("t_ymix", xcv[:, :, :], [(vC, vC.v)])
            barrier()

            evz = {}
            for g in range(16):
                sch.emit("gpsimd", lambda e, g=g: e.dma_start(
                    out=wbA[:, g % 2, :, :],
                    in_=w_inT[:, DI + 256 * g:DI + 256 * (g + 1)].rearrange(
                        "(kh p) n -> p kh n", p=128)),
                    waits=[(tC, grp_t.get(("z", r, g - 2), 0))], inc=dW, inc_n=16)
                d_g = dW.v
                for nl in range(2):
                    n_hat = g * 2 + nl
                    pA, pB = (0, 1) if n_hat % 2 == 0 else (2, 3)
                    for kh in range(16):
                        w0 = [(dW, d_g), (sC, evz.get(n_hat - 2, 0))] if kh == 0 else []
                        sch.emit("tensor", lambda e, g=g, nl=nl, kh=kh, pA=pA: e.matmul(
                            pbf(pA), wbA[:, g % 2, kh, nl * 128:(nl + 1) * 128],
                            xf[:, kh, 3:515], start=(kh == 0), stop=(kh == 15),
                            skip_group_check=True), waits=w0)
                        sch.emit("tensor", lambda e, g=g, nl=nl, kh=kh, pB=pB: e.matmul(
                            pbf(pB), wbA[:, g % 2, kh, nl * 128:(nl + 1) * 128],
                            xf[:, kh, 515:1027], start=(kh == 0), stop=(kh == 15),
                            skip_group_check=True),
                            inc=(tC if kh == 15 else None))
                    grp_t[("z", r, g)] = tC.v
                    t_mm = tC.v
                    v_pr = vC.v
                    sch.emit("scalar", lambda e, pA=pA: e.activation(
                        zsil[:, 0, :], pbf(pA), AF.Silu),
                        waits=[(tC, t_mm), (vC, v_pr)], inc=sC)
                    sch.emit("scalar", lambda e, pB=pB: e.activation(
                        zsil[:, 1, :], pbf(pB), AF.Silu), inc=sC)
                    evz[n_hat] = sC.v
                    s_z = sC.v
                    sch.emit("vector", lambda e, n_hat=n_hat: e.tensor_tensor(
                        xcv[:, n_hat, 0:512], xcv[:, n_hat, 0:512], zsil[:, 0, :],
                        AL.mult), waits=[(sC, s_z)], inc=vC)
                    sch.emit("vector", lambda e, n_hat=n_hat: e.tensor_tensor(
                        xcv[:, n_hat, 512:1024], xcv[:, n_hat, 512:1024], zsil[:, 1, :],
                        AL.mult), inc=vC)
            if r == 0:
                tap("t_g", xcv[:, :, :], [(vC, vC.v)])
            barrier()

            evo = {}
            for oh in range(ND):
                sch.emit("gpsimd", lambda e, oh=oh: e.dma_start(
                    out=wbO[:, oh % 2, :, :],
                    in_=out_wT[:, 128 * oh:128 * (oh + 1)].rearrange(
                        "(kh p) o -> p kh o", p=128)),
                    waits=[(tC, grp_t.get(("o", r, oh - 2), 0))], inc=dW, inc_n=16)
                d_g = dW.v
                pA, pB = (0, 1) if oh % 2 == 0 else (2, 3)
                for kh in range(NDI):
                    w0 = [(dW, d_g), (vC, evo.get(oh - 2, 0))] if kh == 0 else []
                    sch.emit("tensor", lambda e, oh=oh, kh=kh, pA=pA: e.matmul(
                        pbf(pA), wbO[:, oh % 2, kh, :], xcv[:, kh, 0:512],
                        start=(kh == 0), stop=(kh == NDI - 1), skip_group_check=True),
                        waits=w0)
                    sch.emit("tensor", lambda e, oh=oh, kh=kh, pB=pB: e.matmul(
                        pbf(pB), wbO[:, oh % 2, kh, :], xcv[:, kh, 512:1024],
                        start=(kh == 0), stop=(kh == NDI - 1), skip_group_check=True),
                        inc=(tC if kh == NDI - 1 else None))
                grp_t[("o", r, oh)] = tC.v
                t_mm = tC.v
                sch.emit("vector", lambda e, oh=oh, pA=pA: e.tensor_scalar(
                    attn_f[:, oh, 0:512], pbf(pA), 0.0, None, AL.add), waits=[(tC, t_mm)], inc=vC)
                sch.emit("vector", lambda e, oh=oh, pB=pB: e.tensor_scalar(
                    attn_f[:, oh, 512:1024], pbf(pB), 0.0, None, AL.add), inc=vC)
                evo[oh] = vC.v
            if r == 0:
                tap("t_attn", attn_f[:, :, :], [(vC, vC.v)])
            barrier()

            for tt in range(8):
                vpr = vC.v
                sch.emit("sync", lambda e, tt=tt, r=r: e.dma_start(
                    out=stage2[:, :],
                    in_=xm[r].rearrange("(n p) d -> p n d", p=128)[:, tt, :]),
                    waits=[(vC, vpr)], inc=dX, inc_n=16)
                d_x = dX.v
                for oh in range(ND):
                    bank = 6 + (oh % 2)
                    vprev = vC.v
                    sch.emit("tensor", lambda e, tt=tt, oh=oh, bank=bank: e.transpose(
                        pb[bank][:, 0:128], attn_f[:, oh, tt * 128:(tt + 1) * 128],
                        idnf_s[:, :]),
                        waits=[(vC, max(0, vprev - 1)), (dX, d_x)], inc=tC)
                    tm = tC.v
                    sch.emit("vector", lambda e, oh=oh, bank=bank: e.scalar_tensor_tensor(
                        stage2[:, oh * 128:(oh + 1) * 128], pb[bank][:, 0:128], 1.0,
                        stage2[:, oh * 128:(oh + 1) * 128], AL.mult, AL.add,
                        accum_out=sumc[:, oh:oh + 1]),
                        waits=[(tC, tm)], inc=vC)
                v_s2 = vC.v
                sch.emit("vector", lambda e: e.tensor_reduce(
                    sumc[:, 16:17], sumc[:, 0:16], AX, AL.add), waits=[(vC, v_s2)],
                    inc=vC, selfwait=True)
                sch.emit("scalar", lambda e: e.activation(
                    x1t[:, :], stage2[:, :], AF.Square, accum_out=sumc[:, 17:18]),
                    waits=[(vC, v_s2)], inc=sC)
                s_sq = sC.v
                sch.emit("vector", lambda e: e.tensor_scalar(
                    sumc[:, 18:19], sumc[:, 16:17], 1.0 / D, None, AL.mult), inc=vC)
                sch.emit("vector", lambda e: e.tensor_scalar(
                    sumc[:, 19:20], sumc[:, 17:18], 1.0 / D, None, AL.mult),
                    waits=[(sC, s_sq)], inc=vC)
                sch.emit("vector", lambda e: e.tensor_tensor(
                    sumc[:, 20:21], sumc[:, 18:19], sumc[:, 18:19], AL.mult), inc=vC)
                sch.emit("vector", lambda e: e.tensor_tensor(
                    sumc[:, 19:20], sumc[:, 19:20], sumc[:, 20:21], AL.subtract), inc=vC)
                v_var = vC.v
                sch.emit("scalar", lambda e: e.activation(
                    sumc[:, 20:21], sumc[:, 19:20], AF.Sqrt, bias=cst[:, 0:1]),
                    waits=[(vC, v_var)], inc=sC)
                s_std = sC.v
                sch.emit("vector", lambda e: e.reciprocal(sumc[:, 21:22], sumc[:, 20:21]),
                         waits=[(sC, s_std)], inc=vC)
                sch.emit("vector", lambda e: e.tensor_tensor(
                    sumc[:, 22:23], sumc[:, 18:19], sumc[:, 21:22], AL.mult), inc=vC)
                sch.emit("vector", lambda e: e.tensor_scalar(
                    sumc[:, 22:23], sumc[:, 22:23], -1.0, None, AL.mult), inc=vC)
                v_inv = vC.v
                sch.emit("scalar", lambda e: e.activation(
                    x1t[:, :], stage2[:, :], AF.Identity, scale=sumc[:, 21:22],
                    bias=sumc[:, 22:23]), waits=[(vC, v_inv)], inc=sC)
                s_x1 = sC.v
                sch.emit("sync", lambda e, tt=tt: e.dma_start(
                    out=x1_d[tt, :, :], in_=x1t[:, :]), waits=[(sC, s_x1)],
                    inc=dX, inc_n=16)
                sch.emit("vector", lambda e: e.tensor_reduce(
                    sumc[:, 16:17], x1t[:, :], AX, AL.max, apply_absolute_value=True),
                    waits=[(sC, s_x1)], inc=vC, selfwait=True)
                sch.emit("vector", lambda e: e.tensor_scalar(
                    sumc[:, 17:18], sumc[:, 16:17], 1.0 / 127.0, EPS, AL.mult, AL.add),
                    inc=vC)
                sch.emit("vector", lambda e, tt=tt: e.tensor_scalar(
                    scol8[:, tt:tt + 1], sumc[:, 17:18], 0.0, None, AL.add), inc=vC)
                sch.emit("vector", lambda e: e.reciprocal(sumc[:, 18:19], sumc[:, 17:18]),
                         inc=vC)
                sch.emit("vector", lambda e: e.tensor_scalar(
                    stage2[:, :], x1t[:, :], sumc[:, 18:19], MAGIC, AL.mult, AL.add),
                    inc=vC)
                sch.emit("vector", lambda e: e.tensor_scalar(
                    xqt[:, :], stage2[:, :], MAGIC, None, AL.subtract), inc=vC)
                v_xq = vC.v
                for oh in range(ND):
                    bank = 6 + (oh % 2)
                    vprev = vC.v
                    sch.emit("tensor", lambda e, oh=oh, bank=bank: e.transpose(
                        pbb(bank, 128), xqt[:, oh * 128:(oh + 1) * 128], idn_s[:, :]),
                        waits=[(vC, v_xq), (vC, max(0, vprev - 1))], inc=tC)
                    tm = tC.v
                    sch.emit("vector", lambda e, tt=tt, oh=oh, bank=bank: e.tensor_scalar(
                        xq_f[:, oh, tt * 128:(tt + 1) * 128], pbb(bank, 128), 0.0, None, AL.add),
                        waits=[(tC, tm)], inc=vC)
            if r == 0:
                tap("t_xq", xq_f[:, :, :], [(vC, vC.v)])
            barrier()
            v_sc = vC.v
            sch.emit("tensor", lambda e: e.transpose(pb[6][0:8, 0:128], scol8[:, :],
                                                     idnf_s[:, :]),
                     waits=[(vC, v_sc)], inc=tC)
            t_sc = tC.v
            sch.emit("vector", lambda e: e.tensor_scalar(rst[:, :], pb[6][0:8, 0:128], 0.0, None, AL.add),
                     waits=[(tC, t_sc)], inc=vC)
            v_sc2 = vC.v
            sch.emit("sync", lambda e: e.dma_start(
                out=row_d[3:4, 0:TC].rearrange("o (b t) -> (o b) t", b=8),
                in_=rst[:, :]), waits=[(vC, v_sc2)], inc=dX, inc_n=16)
            d_sc = dX.v
            sch.emit("sync", lambda e: e.dma_start(
                out=sg128[:, :], in_=row_d[3:4, 0:TC].broadcast_to([128, TC])),
                waits=[(dX, d_sc)], inc=dX, inc_n=16)
            d_sg = dX.v
            sch.emit("vector", lambda e: e.tensor_scalar(
                su128[:, :], sg128[:, :], wsb_s[:, 1:2], None, AL.mult),
                waits=[(dX, d_sg)], inc=vC)
            sch.emit("vector", lambda e: e.tensor_scalar(
                sg128[:, :], sg128[:, :], wsb_s[:, 0:1], None, AL.mult), inc=vC)
            v_sgsu = vC.v

            def quant_w(dst_ap, src_ap, wcol, dwait):
                sch.emit("vector", lambda e, src_ap=src_ap, wcol=wcol: e.tensor_scalar(
                    src_ap, src_ap, rws_s[:, wcol:wcol + 1], MAGIC, AL.mult, AL.add),
                    waits=[(dX, dwait), (dW, dW.v)], inc=vC)
                sch.emit("vector", lambda e, src_ap=src_ap: e.tensor_scalar(
                    src_ap, src_ap, MAGIC, 1.0, AL.subtract, AL.min), inc=vC)
                sch.emit("vector", lambda e, dst_ap=dst_ap, src_ap=src_ap: e.tensor_scalar(
                    dst_ap, src_ap, -1.0, None, AL.max), inc=vC)

            sch.emit("vector", lambda e: e.memset(mxp[:, :], 0.0), inc=vC, selfwait=True)
            evf = {}
            wfst_mark = [0]
            for nh in range(NFF):
                par = nh % 2
                if r == 0:
                    for wname, wsrc, wdst, scr, wcol in [
                            ("g", w_gT, wbG, scr_g, 0), ("u", w_uT, wbU, scr_u, 1)]:
                        sch.emit("sync", lambda e, nh=nh, wsrc=wsrc: e.dma_start(
                            out=wfst[:, :, :],
                            in_=wsrc[:, 128 * nh:128 * (nh + 1)].rearrange(
                                "(kh p) n -> p kh n", p=128)),
                            waits=[(tC, grp_t.get(("f", r, nh - 2), 0)),
                                   (vC, wfst_mark[0])],
                            inc=dX, inc_n=16)
                        d_w = dX.v
                        quant_w(wdst[:, par, :, :].rearrange("p a b -> p (a b)"),
                                wfst[:, :, :].rearrange("p a b -> p (a b)"),
                                wcol, d_w)
                        wfst_mark[0] = vC.v
                        v_q = vC.v
                        sch.emit("gpsimd", lambda e, nh=nh, scr=scr, par=par, wdst=wdst:
                                 e.dma_start(
                                     out=scr[:, 128 * nh:128 * (nh + 1)].rearrange(
                                         "(kh p) n -> p kh n", p=128),
                                     in_=wdst[:, par, :, :]),
                                 waits=[(vC, v_q)], inc=dW, inc_n=16)
                    w_wait = [(vC, vC.v), (dW, dW.v)]
                else:
                    sch.emit("sync", lambda e, nh=nh, par=par: e.dma_start(
                        out=wbG[:, par, :, :],
                        in_=scr_g[:, 128 * nh:128 * (nh + 1)].rearrange(
                            "(kh p) n -> p kh n", p=128)),
                        waits=[(tC, grp_t.get(("f", r, nh - 2), 0))], inc=dX, inc_n=16)
                    sch.emit("sync", lambda e, nh=nh, par=par: e.dma_start(
                        out=wbU[:, par, :, :],
                        in_=scr_u[:, 128 * nh:128 * (nh + 1)].rearrange(
                            "(kh p) n -> p kh n", p=128)),
                        inc=dX, inc_n=16)
                    w_wait = [(dX, dX.v)]
                pg = (0, 1) if par == 0 else (4, 5)
                pu = (2, 3) if par == 0 else (6, 7)
                for kh in range(ND):
                    w0 = (w_wait + [(vC, evf.get(nh - 2, (0, 0))[0]),
                                    (sC, evf.get(nh - 2, (0, 0))[1])]) if kh == 0 else []
                    sch.emit("tensor", lambda e, kh=kh, par=par, pg=pg: e.matmul(
                        pbf(pg[0]), wbG[:, par, kh, :], xq_f[:, kh, 0:512],
                        start=(kh == 0), stop=(kh == ND - 1), skip_group_check=True),
                        waits=w0)
                    sch.emit("tensor", lambda e, kh=kh, par=par, pg=pg: e.matmul(
                        pbf(pg[1]), wbG[:, par, kh, :], xq_f[:, kh, 512:1024],
                        start=(kh == 0), stop=(kh == ND - 1), skip_group_check=True))
                    sch.emit("tensor", lambda e, kh=kh, par=par, pu=pu: e.matmul(
                        pbf(pu[0]), wbU[:, par, kh, :], xq_f[:, kh, 0:512],
                        start=(kh == 0), stop=(kh == ND - 1), skip_group_check=True))
                    sch.emit("tensor", lambda e, kh=kh, par=par, pu=pu: e.matmul(
                        pbf(pu[1]), wbU[:, par, kh, :], xq_f[:, kh, 512:1024],
                        start=(kh == 0), stop=(kh == ND - 1), skip_group_check=True),
                        inc=(tC if kh == ND - 1 else None))
                grp_t[("f", r, nh)] = tC.v
                t_mm = tC.v
                sch.emit("vector", lambda e, pg=pg: e.tensor_tensor(
                    pbf(pg[0]), pbf(pg[0]), sg128[:, 0:512], AL.mult),
                    waits=[(tC, t_mm), (vC, v_sgsu)], inc=vC)
                sch.emit("vector", lambda e, pg=pg: e.tensor_tensor(
                    pbf(pg[1]), pbf(pg[1]), sg128[:, 512:1024], AL.mult), inc=vC)
                sch.emit("vector", lambda e, pu=pu: e.tensor_tensor(
                    pbf(pu[0]), pbf(pu[0]), su128[:, 0:512], AL.mult), inc=vC)
                sch.emit("vector", lambda e, pu=pu: e.tensor_tensor(
                    pbf(pu[1]), pbf(pu[1]), su128[:, 512:1024], AL.mult), inc=vC)
                v_dq = vC.v
                sch.emit("scalar", lambda e, pg=pg: e.activation(
                    sil2[:, 0, :], pbf(pg[0]), AF.Silu),
                    waits=[(vC, v_dq)], inc=sC)
                sch.emit("scalar", lambda e, pg=pg: e.activation(
                    sil2[:, 1, :], pbf(pg[1]), AF.Silu), inc=sC)
                s_sil = sC.v
                sch.emit("vector", lambda e, nh=nh, pu=pu: e.tensor_tensor(
                    g2[:, nh, 0:512], sil2[:, 0, :], pbf(pu[0]), AL.mult),
                    waits=[(sC, s_sil)], inc=vC)
                sch.emit("vector", lambda e, nh=nh, pu=pu: e.tensor_tensor(
                    g2[:, nh, 512:1024], sil2[:, 1, :], pbf(pu[1]), AL.mult), inc=vC)
                evf[nh] = (vC.v, sC.v)
                sch.emit("vector", lambda e, nh=nh: e.tensor_tensor(
                    mxp[:, :], mxp[:, :], g2[:, nh, :], AL.max), inc=vC)
                sch.emit("vector", lambda e, nh=nh: e.scalar_tensor_tensor(
                    mxp[:, :], g2[:, nh, :], -1.0, mxp[:, :], AL.mult, AL.max), inc=vC)
            v_gup = vC.v
            v_mx = vC.v
            for tb in range(8):
                bank = 6 + (tb % 2)
                vprev = vC.v
                sch.emit("tensor", lambda e, tb=tb, bank=bank: e.transpose(
                    pbb(bank, 128), mxp[:, tb * 128:(tb + 1) * 128], idn_s[:, :]),
                    waits=[(vC, v_mx), (vC, max(0, vprev - 1))], inc=tC)
                tm = tC.v
                sch.emit("vector", lambda e, tb=tb, bank=bank: e.tensor_reduce(
                    amx[:, tb:tb + 1], pbb(bank, 128), AX, AL.max),
                    waits=[(tC, tm)], inc=vC, selfwait=True)
            sch.emit("vector", lambda e: e.tensor_scalar(
                amx[:, 8:16], amx[:, 0:8], 1.0 / 127.0, EPS, AL.mult, AL.add), inc=vC)
            v_s2 = vC.v
            sch.emit("tensor", lambda e: e.transpose(pb[6][0:8, 0:128], amx[:, 8:16],
                                                     idnf_s[:, :]),
                     waits=[(vC, v_s2)], inc=tC)
            t_s2 = tC.v
            sch.emit("vector", lambda e: e.tensor_scalar(rst[:, :], pb[6][0:8, 0:128], 0.0, None, AL.add),
                     waits=[(tC, t_s2)], inc=vC)
            v_s2b = vC.v
            sch.emit("sync", lambda e: e.dma_start(
                out=row_d[2:3, 0:TC].rearrange("o (b t) -> (o b) t", b=8),
                in_=rst[:, :]), waits=[(vC, v_s2b)], inc=dX, inc_n=16)
            d_s2 = dX.v
            sch.emit("sync", lambda e: e.dma_start(
                out=s2b[:, :], in_=row_d[2:3, 0:TC].broadcast_to([128, TC])),
                waits=[(dX, d_s2)], inc=dX, inc_n=16)
            sch.emit("vector", lambda e: e.reciprocal(amx[:, 8:16], amx[:, 8:16]),
                     waits=[(vC, v_s2b)], inc=vC)
            v_r = vC.v
            sch.emit("tensor", lambda e: e.transpose(pb[6][0:8, 0:128], amx[:, 8:16],
                                                     idnf_s[:, :]),
                     waits=[(vC, v_r)], inc=tC)
            t_r = tC.v
            sch.emit("vector", lambda e: e.tensor_scalar(rst[:, :], pb[6][0:8, 0:128], 0.0, None, AL.add),
                     waits=[(tC, t_r), (dX, d_s2)], inc=vC)
            v_r2 = vC.v
            sch.emit("sync", lambda e: e.dma_start(
                out=row_d[2:3, 0:TC].rearrange("o (b t) -> (o b) t", b=8),
                in_=rst[:, :]),
                waits=[(vC, v_r2)], inc=dX, inc_n=16)
            d_r = dX.v
            sch.emit("sync", lambda e: e.dma_start(
                out=r128[:, :], in_=row_d[2:3, 0:TC].broadcast_to([128, TC])),
                waits=[(dX, d_r)], inc=dX, inc_n=16)
            d_r128 = dX.v
            if r == 0:
                tap("t_g2r", g2[:, :, :], [(vC, vC.v)])
                tap("t_r128", r128[:, :], [(dX, d_r128)])
                tap("t_wq", wbG[:, :, :, :], [(vC, vC.v)])
            for nh in range(NFF):
                w0 = [(dX, d_r128)] if nh == 0 else []
                sch.emit("vector", lambda e, nh=nh: e.tensor_tensor(
                    qtmp[:, :], g2[:, nh, :], r128[:, :], AL.mult), waits=w0, inc=vC)
                sch.emit("vector", lambda e: e.tensor_scalar(
                    qtmp[:, :], qtmp[:, :], MAGIC, None, AL.add), inc=vC)
                sch.emit("vector", lambda e, nh=nh: e.tensor_scalar(
                    g2[:, nh, :], qtmp[:, :], MAGIC, None, AL.subtract), inc=vC)
            if r == 0:
                tap("t_g2", g2[:, :, :], [(vC, vC.v)])
            barrier()

            def quant_d(dst_ap, src_ap, dwait):
                sch.emit("vector", lambda e, src_ap=src_ap: e.tensor_scalar(
                    src_ap, src_ap, rws_s[:, 2:3], MAGIC, AL.mult, AL.add),
                    waits=[(dX, dwait), (dW, dW.v)], inc=vC)
                sch.emit("vector", lambda e, src_ap=src_ap: e.tensor_scalar(
                    src_ap, src_ap, MAGIC, 1.0, AL.subtract, AL.min), inc=vC)
                sch.emit("vector", lambda e, dst_ap=dst_ap, src_ap=src_ap: e.tensor_scalar(
                    dst_ap, src_ap, -1.0, None, AL.max), inc=vC)

            evd = {}
            for oh in range(ND):
                par = oh % 2
                pA, pB = (0, 1) if par == 0 else (2, 3)
                if r == 0:
                    for half in range(2):
                        for q2 in range(2):
                            k0 = half * 4096 + q2 * 2048
                            sch.emit("sync", lambda e, oh=oh, k0=k0: e.dma_start(
                                out=wfst2[:, :, :],
                                in_=w_dT[k0:k0 + 2048,
                                         128 * oh:128 * (oh + 1)].rearrange(
                                    "(kh p) n -> p kh n", p=128)),
                                waits=[(tC, tC.v), (vC, wfst_mark[0])],
                                inc=dX, inc_n=16)
                            d_w = dX.v
                            quant_d(wbD[:, par, 16 * q2:16 * (q2 + 1), :].rearrange(
                                "p a b -> p (a b)"),
                                wfst2[:, :, :].rearrange("p a b -> p (a b)"),
                                d_w)
                            wfst_mark[0] = vC.v
                        v_q = vC.v
                        sch.emit("gpsimd", lambda e, oh=oh, par=par, half=half: e.dma_start(
                            out=scr_d[4096 * half:4096 * (half + 1),
                                      128 * oh:128 * (oh + 1)].rearrange(
                                "(kh p) n -> p kh n", p=128),
                            in_=wbD[:, par, :, :]), waits=[(vC, v_q)], inc=dW, inc_n=16)
                        for kh in range(32):
                            khg = half * 32 + kh
                            w0 = [(vC, v_q), (vC, evd.get(oh - 2, 0))] if kh == 0 else []
                            sch.emit("tensor", lambda e, kh=kh, khg=khg, par=par, pA=pA:
                                     e.matmul(
                                         pbf(pA), wbD[:, par, kh, :], g2[:, khg, 0:512],
                                         start=(khg == 0), stop=(khg == NFF - 1),
                                         skip_group_check=True), waits=w0)
                            sch.emit("tensor", lambda e, kh=kh, khg=khg, par=par, pB=pB:
                                     e.matmul(
                                         pbf(pB), wbD[:, par, kh, :], g2[:, khg, 512:1024],
                                         start=(khg == 0), stop=(khg == NFF - 1),
                                         skip_group_check=True),
                                     inc=(tC if kh == 31 else None))
                else:
                    for half in range(2):
                        sch.emit("sync", lambda e, oh=oh, par=par, half=half: e.dma_start(
                            out=wbD[:, par, :, :],
                            in_=scr_d[4096 * half:4096 * (half + 1),
                                      128 * oh:128 * (oh + 1)].rearrange(
                                "(kh p) n -> p kh n", p=128)),
                            waits=[(tC, tC.v), (vC, evd.get(oh - 2, 0))],
                            inc=dX, inc_n=16)
                        d_w = dX.v
                        for kh in range(32):
                            khg = half * 32 + kh
                            w0 = [(dX, d_w)] if kh == 0 else []
                            sch.emit("tensor", lambda e, kh=kh, khg=khg, par=par, pA=pA:
                                     e.matmul(
                                         pbf(pA), wbD[:, par, kh, :], g2[:, khg, 0:512],
                                         start=(khg == 0), stop=(khg == NFF - 1),
                                         skip_group_check=True), waits=w0)
                            sch.emit("tensor", lambda e, kh=kh, khg=khg, par=par, pB=pB:
                                     e.matmul(
                                         pbf(pB), wbD[:, par, kh, :], g2[:, khg, 512:1024],
                                         start=(khg == 0), stop=(khg == NFF - 1),
                                         skip_group_check=True),
                                     inc=(tC if kh == 31 else None))
                grp_t[("d", r, oh)] = tC.v
                t_mm = tC.v
                sch.emit("vector", lambda e, oh=oh, pA=pA: e.scalar_tensor_tensor(
                    dnf[:, oh, 0:512], pbf(pA), wsb_s[:, 2:3], s2b[:, 0:512],
                    AL.mult, AL.mult), waits=[(tC, t_mm)], inc=vC)
                sch.emit("vector", lambda e, oh=oh, pB=pB: e.scalar_tensor_tensor(
                    dnf[:, oh, 512:1024], pbf(pB), wsb_s[:, 2:3], s2b[:, 512:1024],
                    AL.mult, AL.mult), inc=vC)
                evd[oh] = vC.v
            barrier()

            for tt in range(8):
                vpr = vC.v
                sch.emit("sync", lambda e, tt=tt: e.dma_start(
                    out=x1rr[:, :], in_=x1_d[tt, :, :]), waits=[(vC, vpr)],
                    inc=dX, inc_n=16)
                d_x1 = dX.v
                for oh in range(ND):
                    bank = 6 + (oh % 2)
                    vprev = vC.v
                    sch.emit("tensor", lambda e, tt=tt, oh=oh, bank=bank: e.transpose(
                        pbb(bank, 128), dnf[:, oh, tt * 128:(tt + 1) * 128], idn_s[:, :]),
                        waits=[(vC, max(0, vprev - 1))], inc=tC)
                    tm = tC.v
                    sch.emit("vector", lambda e, oh=oh, bank=bank: e.scalar_tensor_tensor(
                        stage3[:, oh * 128:(oh + 1) * 128], pbb(bank, 128), 1.0,
                        x1rr[:, oh * 128:(oh + 1) * 128], AL.mult, AL.add,
                        accum_out=sumc[:, oh:oh + 1]),
                        waits=[(tC, tm), (dX, d_x1)], inc=vC)
                v_s3 = vC.v
                sch.emit("vector", lambda e: e.tensor_reduce(
                    sumc[:, 16:17], sumc[:, 0:16], AX, AL.add), waits=[(vC, v_s3)],
                    inc=vC, selfwait=True)
                sch.emit("scalar", lambda e: e.activation(
                    x1rr[:, :], stage3[:, :], AF.Square, accum_out=sumc[:, 17:18]),
                    waits=[(vC, v_s3)], inc=sC)
                s_sq = sC.v
                sch.emit("vector", lambda e: e.tensor_scalar(
                    sumc[:, 18:19], sumc[:, 16:17], 1.0 / D, None, AL.mult), inc=vC)
                sch.emit("vector", lambda e: e.tensor_scalar(
                    sumc[:, 19:20], sumc[:, 17:18], 1.0 / D, None, AL.mult),
                    waits=[(sC, s_sq)], inc=vC)
                sch.emit("vector", lambda e: e.tensor_tensor(
                    sumc[:, 20:21], sumc[:, 18:19], sumc[:, 18:19], AL.mult), inc=vC)
                sch.emit("vector", lambda e: e.tensor_tensor(
                    sumc[:, 19:20], sumc[:, 19:20], sumc[:, 20:21], AL.subtract), inc=vC)
                v_var = vC.v
                sch.emit("scalar", lambda e: e.activation(
                    sumc[:, 20:21], sumc[:, 19:20], AF.Sqrt, bias=cst[:, 0:1]),
                    waits=[(vC, v_var)], inc=sC)
                s_std = sC.v
                sch.emit("vector", lambda e: e.reciprocal(sumc[:, 21:22], sumc[:, 20:21]),
                         waits=[(sC, s_std)], inc=vC)
                sch.emit("vector", lambda e: e.tensor_tensor(
                    sumc[:, 22:23], sumc[:, 18:19], sumc[:, 21:22], AL.mult), inc=vC)
                sch.emit("vector", lambda e: e.tensor_scalar(
                    sumc[:, 22:23], sumc[:, 22:23], -1.0, None, AL.mult), inc=vC)
                v_inv = vC.v
                sch.emit("scalar", lambda e: e.activation(
                    stage3[:, :], stage3[:, :], AF.Identity, scale=sumc[:, 21:22],
                    bias=sumc[:, 22:23]), waits=[(vC, v_inv)], inc=sC)
                s_out = sC.v
                sch.emit("sync", lambda e, tt=tt, r=r: e.dma_start(
                    out=yout[r, :, :].rearrange("(n p) d -> p n d", p=128)[:, tt, :],
                    in_=stage3[:, :]), waits=[(sC, s_out)], inc=dX, inc_n=16)
            barrier()

        dxw, dww, dhw = dX.v, dW.v, dH.v
        for eng in ENGS:
            def thunk(e, dxw=dxw, dww=dww, dhw=dhw):
                e.wait_ge(dX.sem, dxw)
                e.wait_ge(dW.sem, dww)
                e.wait_ge(dH.sem, dhw)
            sch.q[eng].append(thunk)

        with nc.Block() as block:
            @block.sync
            def _(e):
                for fn in sch.q["sync"]:
                    fn(e)

            @block.gpsimd
            def _(e):
                for fn in sch.q["gpsimd"]:
                    fn(e)

            @block.vector
            def _(e):
                for fn in sch.q["vector"]:
                    fn(e)

            @block.scalar
            def _(e):
                for fn in sch.q["scalar"]:
                    fn(e)

            @block.tensor
            def _(e):
                for fn in sch.q["tensor"]:
                    fn(e)

    return nc


_NC_CACHE = None


def kernel(L0_attn_x, L0_attn_W_in, L0_attn_conv_w, L0_attn_conv_b, L0_attn_W_x,
           L0_attn_A_log, L0_attn_dt_bias, L0_attn_out_W, L0_attn_D,
           L0_ffn_w_gate, L0_ffn_w_up, L0_ffn_w_down):
    global _NC_CACHE
    x = np.ascontiguousarray(np.asarray(L0_attn_x, dtype=np.float32))
    w_inT = np.ascontiguousarray(np.asarray(L0_attn_W_in, np.float32).T)
    conv_w = np.ascontiguousarray(
        np.asarray(L0_attn_conv_w, np.float32).reshape(NDI, 128, 4).transpose(1, 0, 2))
    conv_b = np.ascontiguousarray(
        np.asarray(L0_attn_conv_b, np.float32).reshape(NDI, 128).T)
    _wx = np.asarray(L0_attn_W_x, np.float32)
    _wxt = np.zeros((128, NDI, 80), np.float32)
    _wxd = _wx.T.reshape(NDI, 128, 33).transpose(1, 0, 2)
    _wxt[:, :, 0] = _wxd[:, :, 32]
    _wxt[:, :, 32:48] = _wxd[:, :, 0:16]
    _wxt[:, :, 64:80] = _wxd[:, :, 16:32]
    w_xT = np.ascontiguousarray(_wxt)
    a_log = np.ascontiguousarray(np.asarray(L0_attn_A_log, np.float32)[0:1, :])
    dt_b = np.ascontiguousarray(np.asarray(L0_attn_dt_bias, np.float32).reshape(NDI, 128).T)
    out_wT = np.ascontiguousarray(np.asarray(L0_attn_out_W, np.float32).T)
    d_vec = np.ascontiguousarray(np.asarray(L0_attn_D, np.float32).reshape(NDI, 128).T)
    w_gT = np.ascontiguousarray(np.asarray(L0_ffn_w_gate, np.float32).T)
    w_uT = np.ascontiguousarray(np.asarray(L0_ffn_w_up, np.float32).T)
    w_dT = np.ascontiguousarray(np.asarray(L0_ffn_w_down, np.float32).T)

    idn = np.eye(128, dtype=BF)
    idnf = np.eye(128, dtype=np.float32)
    tau = (np.arange(2)[None, :, None] * 128 + np.arange(128)[:, None, None])
    tcol = np.arange(L)[None, None, :]
    gmask = np.ascontiguousarray(
        np.broadcast_to((tcol >= tau), (128, 2, L)).astype(np.float32))

    if _NC_CACHE is None:
        _NC_CACHE = build_program()
    nc = _NC_CACHE

    in_maps = []
    for c in range(NCORE):
        chm = np.zeros((128, NCORE), np.float32)
        chm[:, :c] = 1.0
        im = {
            "w_inT": w_inT, "w_xT": w_xT, "out_wT": out_wT,
            "w_gT": w_gT, "w_uT": w_uT, "w_dT": w_dT,
            "conv_w": conv_w, "conv_b": conv_b, "d_vec": d_vec,
            "dt_bias": dt_b, "a_row": a_log,
            "idn": idn, "idnf": idnf, "gmask": gmask, "chm": chm,
            "wsl": np.ascontiguousarray(np.stack([
                w_gT.reshape(8, -1)[c],
                w_uT.reshape(8, -1)[c],
                w_dT.reshape(8, -1)[c],
            ]).reshape(3, 128, 16384).transpose(1, 0, 2)),
        }
        for r in range(2):
            t0 = c * TC
            im[f"xm{r}"] = np.ascontiguousarray(x[r, t0:t0 + TC, :])
            if c == 0:
                im[f"xh{r}"] = np.zeros((3, D), np.float32)
            else:
                im[f"xh{r}"] = np.ascontiguousarray(x[r, t0 - 3:t0, :])
        in_maps.append(im)

    res = run_bass_kernel_spmd(nc, in_maps, core_ids=list(range(NCORE)))
    out = np.empty((2, 8192, D), np.float32)
    for c in range(NCORE):
        yc = np.asarray(res.results[c]["y"]).reshape(2, TC, D)
        for r in range(2):
            out[r, c * TC:(c + 1) * TC, :] = yc[r]
    return out
